# version 27
# baseline (speedup 1.0000x reference)
import math
from contextlib import ExitStack

import numpy as np
import ml_dtypes

import concourse.bass as bass
import concourse.mybir as mybir
from concourse.bass_utils import run_bass_kernel_spmd

F32 = mybir.dt.float32
BF16 = mybir.dt.bfloat16
FP16 = mybir.dt.float16
AF = mybir.ActivationFunctionType
ALU = mybir.AluOpType
AX = mybir.AxisListType

S = 2048
E = 1024
DC = 256
DR = 128
H = 16
HD = 64
I = 2816
NSEQ = 2
NCORE = 8
TS = 512
NSL = S // TS
TT = 128
NTT = S // TT
QT = 512
NQT = NSL
KT = 128
NKT = S // KT
NT = 2
ITS = [(k * 512, min(512, I - k * 512)) for k in range(6)]
RCP6 = float(np.float32(1.0) / np.float32(6.0))
C6H = float(np.float32(1.0 / 6.0))
C6L = float(np.float64(1.0) / 6.0 - np.float64(np.float32(1.0 / 6.0)))
ENG = {"pe": "tensor", "sv": "vector", "ss": "scalar", "sg": "gpsimd", "sy": "sync"}


class Prog:
    def __init__(self, nc, sems):
        self.nc = nc
        self.sems = sems
        self.which = None
        self.h = None
        self.cnt = None

    def start(self, which, handle):
        self.which = which
        self.h = handle
        self.cnt = {k: 0 for k in self.sems}

    def op(self, engine, method, *args, waits=(), sem=None, lp=False, **kw):
        sem = sem or {"tensor": "pe", "vector": "sv", "scalar": "ss",
                      "gpsimd": "sg", "sync": "sy"}[engine]
        self.cnt[sem] += 1
        val = self.cnt[sem]
        if self.which == engine:
            waits = [(s, v) for (s, v) in waits if v > 0]
            if len(waits) > 1:
                for s, v in waits[:-1]:
                    self.h.wait_ge(self.sems[s], v)
            if lp:
                with self.nc.allow_low_precision(reason="bf16 softmax denominators (error budgeted)"):
                    inst = getattr(self.h, method)(*args, **kw)
            else:
                inst = getattr(self.h, method)(*args, **kw)
            if waits:
                inst._wait_ge(self.sems[waits[-1][0]], waits[-1][1])
            inst.then_inc(self.sems[sem], 1)
        return (sem, val)

    def dma(self, queue, out, in_, sem, waits=(), transpose=False):
        self.cnt[sem] += 16
        val = self.cnt[sem]
        if self.which == queue:
            waits = [(s, v) for (s, v) in waits if v > 0]
            if len(waits) > 1:
                for s, v in waits[:-1]:
                    self.h.wait_ge(self.sems[s], v)
            inst = self.h.dma_start(out=out, in_=in_, transpose=transpose)
            if waits:
                inst._wait_ge(self.sems[waits[-1][0]], waits[-1][1])
            inst.then_inc(self.sems[sem], 16)
        return (sem, val)

    def total(self, sem):
        return (sem, self.cnt[sem])

    def wait(self, engine, waits):
        if self.which == engine:
            for s, v in waits:
                if v > 0:
                    self.h.wait_ge(self.sems[s], v)

    def barrier(self, dma_drain=()):
        drains = [(s, self.cnt[s]) for s in dma_drain if self.cnt[s] > 0]
        vals = {}
        for e in ["pe", "sv", "ss", "sg", "sy"]:
            self.cnt[e] += 1
            vals[e] = self.cnt[e]
        for e, eng in ENG.items():
            if self.which == eng:
                for s, v in drains:
                    self.h.wait_ge(self.sems[s], v)
                self.h.nop().then_inc(self.sems[e], 1)
                for o, ov in vals.items():
                    if o != e:
                        self.h.wait_ge(self.sems[o], ov)


def build_program(tap=None):
    nc = bass.Bass("TRN2", target_bir_lowering=False)
    dr = {}

    def din(nm, shp, dt=F32):
        dr[nm] = nc.dram_tensor("i_" + nm, shp, dt, kind="ExternalInput").ap()

    din("x", [NSEQ, S, E])
    din("wq", [E, DC]); din("wqc", [DC, E - DR]); din("wqr", [E, DR])
    din("wkv", [E, DC]); din("wkr", [E, DR]); din("wkc", [DC, E - DR])
    din("wv", [DC, E]); din("wo", [E, E])
    din("bqp", [128, 2]); din("bqcp", [128, 7]); din("bqrp", [128, 1])
    din("bkvp", [128, 2]); din("bkrp", [128, 1]); din("bkcp", [128, 7])
    din("bvr", [1, E]); din("bor", [1, E])
    din("wg", [I, E]); din("wu", [I, E]); din("wdn", [E, I])
    din("chh", [128, S]); din("shh", [128, S])
    din("masks", [128, 4, QT], BF16)
    dr["out"] = nc.dram_tensor("out", [NSEQ, S, E], F32, kind="ExternalOutput").ap()
    dr["ln1s"] = nc.dram_tensor("ln1s", [NSEQ, S, E], BF16, kind="Internal").ap()
    dr["wgs"] = nc.dram_tensor("wgs", [8, 128, I], BF16, kind="Internal").ap()
    dr["wus"] = nc.dram_tensor("wus", [8, 128, I], BF16, kind="Internal").ap()
    dr["wds"] = nc.dram_tensor("wds", [22, 128, E], BF16, kind="Internal").ap()
    if tap is not None:
        dr["tapout"] = nc.dram_tensor("tapout", tap["shape"], tap.get("dtype", F32),
                                      kind="ExternalOutput").ap()

    T = {}
    root = ExitStack()

    def alloc(stk, name, shape, dtype, psum=False):
        fn = nc.psum_tensor if psum else nc.sbuf_tensor
        T[name] = stk.enter_context(fn(name, shape, dtype))
        return T[name]

    alloc(root, "onesF", [128, 128], BF16)
    alloc(root, "epst", [128, 1], F32)
    alloc(root, "PS_A", [128, 2048], F32, psum=True)
    alloc(root, "PS_B", [128, 1024], F32, psum=True)
    alloc(root, "PS_C", [128, 512], F32, psum=True)
    alloc(root, "PS_D", [128, 512], F32, psum=True)

    att = ExitStack()
    for nm, shp, dt in [
        ("masks", [128, 4, QT], BF16),
        ("wqb", [128, 8, DC], BF16), ("wqcb", [128, 2, E - DR], BF16),
        ("wqrb", [128, 8, DR], BF16), ("wkvb", [128, 8, DC], BF16),
        ("wkrb", [128, 8, DR], BF16), ("wkcb", [128, 2, E - DR], BF16),
        ("wvb", [128, 2, E], BF16), ("wob", [128, 8, E], BF16),
        ("bqp", [128, 2], F32), ("bqcp", [128, 7], F32), ("bqrp", [128, 1], F32),
        ("bkvp", [128, 2], F32), ("bkrp", [128, 1], F32), ("bkcp", [128, 7], F32),
        ("bvrb", [1, E], BF16), ("borb", [1, E], BF16),
        ("kT", [128, 8, S], BF16), ("V", [128, NKT, H, HD + 1], BF16),
        ("qTs", [128, 8, TS], BF16), ("attnT", [128, 8, TS], BF16),
        ("xst", [128, 8, 2, 128], BF16), ("xT", [128, 8, TS], BF16),
        ("cqT", [128, 2, TS], BF16), ("ckvT", [128, 2, TS], BF16),
        ("qr", [128, TS], F32), ("qrw", [128, TS], F32),
        ("chh", [128, TS], F32), ("shh", [128, TS], F32),
        ("PT0", [128, 2 * QT], BF16), ("PT1", [128, 2 * QT], BF16),
        ("rcpA", [128, 2 * QT], BF16),
        ("rbs", [64, 2 * QT], F32), ("rbs2", [64, 2 * QT], F32),
        ("xr", [128, E], F32), ("rres", [128, E], F32),
        ("s_sum", [128, 1], F32), ("s_sq", [128, 1], F32),
        ("s_sq2", [128, 1], F32),
        ("s_mean", [128, 1], F32), ("s_var", [128, 1], F32),
        ("s_rstd", [128, 1], F32),
        ("ln1t0", [128, E], BF16),
        ("wpin0", [128, E], F32), ("wpin1", [128, E], F32),
        ("wpabs", [128, E], F32), ("wppa", [128, E], F32), ("wppb", [128, E], F32),
        ("wpsgn", [128, E], F32),
        ("wpbm", [128, E // 16], F32), ("wpsc", [128, E // 16], F32),
        ("wprc", [128, E // 16], F32),
        ("wstg", [128, 8, 2, 128], BF16), ("wstgT", [128, 2, 128], BF16),
        ("wstgD", [128, 8, 128], BF16),
    ]:
        alloc(att, nm, shp, dt)
    att.close()

    ffn = ExitStack()
    for nm, shp, dt in [
        ("wgT", [128, 8, I], BF16), ("wuT", [128, 8, I], BF16),
        ("wdT", [128, 22, E], BF16),
        ("ln1L0", [128, E], BF16), ("ln1L1", [128, E], BF16),
        ("qabs", [128, E], F32), ("qpa", [128, E], F32), ("qpb", [128, E], F32),
        ("qsgn", [128, E], F32),
        ("qbm", [128, E // 16], F32), ("qsc", [128, E // 16], F32),
        ("qrc", [128, E // 16], F32),
        ("xq2", [128, 8, NT, 128], BF16), ("xqT", [128, 8, NT * 128], BF16),
        ("hsg", [128, 512], F32), ("hsgB", [128, 512], F32),
        ("h1ab", [128, 1024], F32),
        ("hq2", [128, 22, NT, 128], BF16), ("hqT", [128, 22, NT * 128], BF16),
        ("r2", [128, E], F32), ("r2b", [128, E], F32),
        ("f_sum", [128, 1], F32), ("f_sq", [128, 1], F32),
        ("f_sq2", [128, 1], F32),
        ("f_mean", [128, 1], F32), ("f_var", [128, 1], F32),
        ("f_rstd", [128, 1], F32),
    ]:
        alloc(ffn, nm, shp, dt)
    ffn.close()

    sems = {}
    for nm in ["pe", "sv", "ss", "sg", "sy",
               "dxs", "dxt", "dwl", "drt", "dsw", "dre",
               "dl10", "dl11", "dwp0", "dwp1", "dwpt", "dwps",
               "dwft", "dlf0", "dlf1", "dxq", "dhq", "dhq1", "dhq2", "dhq3",
               "dhq4", "dhq5", "dou0", "dou1", "dswg"]:
        sems[nm] = root.enter_context(nc.semaphore(nm))

    p = Prog(nc, sems)
    t = T

    def emit_program():
        tap_done = [False]

        def do_tap(tag):
            if tap is not None and tap["tag"] == tag and not tap_done[0]:
                tap_done[0] = True
                tsrc = tap["tensor"]
                if "@" in tsrc:
                    base, kk = tsrc.split("@")
                    tsrc_ap = t[base][:, :, 0:int(kk)]
                elif "|" in tsrc:
                    base, kk = tsrc.split("|")
                    tsrc_ap = t[base][:, 0:int(kk)]
                else:
                    tsrc_ap = t[tsrc][:]
                p.dma("sync", dr["tapout"], tsrc_ap, "dou0",
                      waits=[p.total("sv"), p.total("ss"), p.total("sg"),
                             p.total("dswg"), p.total("dsw"), p.total("pe")])
                p.barrier(dma_drain=["dou0"])

        RopA = t["rres"][:, 0:TS]
        RopB = t["rres"][:, TS:2 * TS]
        S2 = t["PS_A"]
        pp = [t["PS_A"][:, i * 512:(i + 1) * 512] for i in range(4)]
        OA = t["PS_B"][0:65, 0:512]
        OB = t["PS_B"][0:65, 512:1024]
        RB = t["PS_C"][0:64, :]
        RB2 = t["PS_D"][0:64, :]
        wop = [t["PS_C"][:, :], t["PS_D"][:, :]]
        gps2 = [t["PS_C"][:, :], t["PS_B"][0:128, 0:512]]
        ups2 = [t["PS_D"][:, :], t["PS_B"][0:128, 512:1024]]
        dn = [t["PS_A"][:, 0:1024], t["PS_A"][:, 1024:2048]]

        for nm, src in [("wqb", "wq"), ("wqrb", "wqr"), ("wkvb", "wkv"),
                        ("wkrb", "wkr"), ("wob", "wo"),
                        ("wqcb", "wqc"), ("wkcb", "wkc"), ("wvb", "wv")]:
            p.dma("gpsimd", t[nm][:], dr[src].rearrange("(c p) m -> p c m", p=128), "dwl")
        for nm in ["bqp", "bqcp", "bqrp", "bkvp", "bkrp", "bkcp", "masks"]:
            p.dma("gpsimd", t[nm][:], dr[nm], "dwl")
        p.dma("gpsimd", t["bvrb"][:], dr["bvr"], "dwl")
        p.dma("gpsimd", t["borb"][:], dr["bor"], "dwl")
        tk_ones = p.op("gpsimd", "memset", t["onesF"][:], 1.0)
        tk_eps = p.op("gpsimd", "memset", t["epst"][:], 1e-5)
        wl_all = p.total("dwl")

        scr_free = {}

        def quant(x_ap, F, out_ap, pre, x_waits, out_sem, out_waits=(),
                  mid=None, pool_stair=False):
            nblk = F // 16
            ab = t[pre["abs"]][:, 0:F]
            pa = t[pre["pa"]][:, 0:F]
            pb = t[pre["pb"]][:, 0:F]
            sg_ = t[pre["sgn"]][:, 0:F]
            bm = t[pre["bm"]][:, 0:nblk]
            sc = t[pre["sc"]][:, 0:nblk]
            rc = t[pre["rc"]][:, 0:nblk]
            pv = scr_free.get(pre["abs"], {})
            x_waits = list(x_waits)
            if pool_stair:
                tneg = p.op("vector", "tensor_scalar", sg_, x_ap, -1.0, None,
                            ALU.mult, waits=x_waits + pv.get("sgn", []))
                tka = p.op("vector", "tensor_tensor", ab, sg_, x_ap, ALU.max,
                           waits=[tneg] + x_waits + pv.get("ab", []))
                tg_ = p.op("vector", "tensor_scalar", sg_, x_ap, 0.0, None,
                           ALU.is_gt, waits=[tka])
                tl_ = p.op("vector", "tensor_scalar", pa, x_ap, 0.0, None,
                           ALU.is_lt, waits=x_waits + pv.get("pa", []))
                tks = p.op("vector", "tensor_tensor", sg_, sg_, pa,
                           ALU.subtract, waits=[tg_, tl_])
            else:
                tka = p.op("scalar", "activation", ab, x_ap, AF.Abs,
                           waits=x_waits + pv.get("ab", []))
                tks = p.op("scalar", "activation", sg_, x_ap, AF.Sign,
                           waits=x_waits + pv.get("sgn", []))
            a3 = ab.rearrange("p (nb b) -> p nb b", b=16)
            x3 = x_ap.rearrange("p (nb b) -> p nb b", b=16)
            tkb = p.op("vector", "tensor_reduce", bm, x3, axis=AX.X, op=ALU.max,
                       apply_absolute_value=True,
                       waits=x_waits + pv.get("bm", []))
            tlo = p.op("vector", "tensor_scalar", rc, bm, C6L, None, ALU.mult,
                       waits=[tkb] + pv.get("rc", []))
            thi = p.op("vector", "scalar_tensor_tensor", sc, bm, C6H, rc,
                       ALU.mult, ALU.add, waits=[tlo] + pv.get("sc", []))
            tkc = p.op("vector", "tensor_scalar", sc, sc, 1e-12, None, ALU.max,
                       waits=[thi])
            tkr = p.op("vector", "reciprocal", rc, sc, waits=[tkc])
            tkc2 = tkc
            scb = sc
            rb_ = rc.unsqueeze(2).broadcast_to([128, nblk, 16])
            tkn = p.op("gpsimd", "tensor_tensor",
                       pa.rearrange("p (nb b) -> p nb b", b=16), a3, rb_, ALU.mult,
                       waits=[tkr, tks, tka] + pv.get("pa", []))
            if mid is not None:
                mid([tka, tks, tkb])
            axn = pa
            t1 = p.op("vector", "tensor_scalar", pb, axn, 0.25, None,
                      ALU.is_gt, waits=[tkn] + pv.get("pb", []))
            for thr in (0.75, 1.25, 1.75):
                t1 = p.op("vector", "scalar_tensor_tensor", pb, axn, thr,
                          pb, ALU.is_gt, ALU.add, waits=[t1])
            h1_ = p.op("vector", "tensor_scalar", ab, axn, 5.0, None,
                       ALU.is_gt, waits=[tkn, tkb])
            for thr in (5.0, 3.5, 2.5):
                h1_ = p.op("vector", "scalar_tensor_tensor", ab, axn, thr,
                           ab, ALU.is_gt, ALU.add, waits=[h1_])
            tL = p.op("vector", "scalar_tensor_tensor", pb, pb, 0.5, ab,
                      ALU.mult, ALU.add, waits=[t1, h1_])
            tLs = p.op("vector" if pool_stair else "gpsimd",
                       "tensor_tensor", pb, pb, sg_, ALU.mult,
                       waits=[tL, tks])
            if out_ap.ndim == 3:
                nch = out_ap.shape[1]
                o4 = out_ap.rearrange("p c (s b) -> p c s b", b=16)
                pb4 = pb.rearrange("p (c s b) -> p c s b", c=nch, b=16)
                sb_ = scb.rearrange("p (c s) -> p c s", s=8).unsqueeze(3) \
                    .broadcast_to([128, nch, 8, 16])
                tko = p.op("gpsimd", "tensor_tensor", o4, pb4, sb_, ALU.mult,
                           waits=[tLs, tkc2] + list(out_waits), sem=out_sem)
            else:
                sb_ = scb.unsqueeze(2).broadcast_to([128, nblk, 16])
                tko = p.op("gpsimd", "tensor_tensor",
                           out_ap.rearrange("p (nb b) -> p nb b", b=16),
                           pb.rearrange("p (nb b) -> p nb b", b=16), sb_, ALU.mult,
                           waits=[tLs, tkc2] + list(out_waits), sem=out_sem)
            scr_free[pre["abs"]] = {
                "ab": [tL, tkn], "sgn": [tLs], "bm": [thi],
                "rc": [thi, tkn], "sc": [tko], "pa": [tL], "pb": [tko]}
            return tko

        prep_items = []
        for i_ in range(22):
            prep_items.append(("g", i_))
        for i_ in range(22):
            prep_items.append(("u", i_))
        for i_ in range(8):
            prep_items.append(("d", i_))
        prep = {"i": 0, "par": 0, "cons": {0: ("sv", 0), 1: ("sv", 0)},
                "stg_free": ("dwps", 0), "stgD_free": ("dwps", 0)}
        wscr = {"abs": "wpabs", "pa": "wppa", "pb": "wppb", "sgn": "wpsgn",
                "bm": "wpbm", "sc": "wpsc", "rc": "wprc"}

        def emit_prep(n):
            for _ in range(n):
                if prep["i"] >= len(prep_items):
                    return
                kind, it = prep_items[prep["i"]]
                prep["i"] += 1
                if kind in ("g", "u"):
                    src = (dr["wg"] if kind == "g" else dr["wu"])[
                        it * 128:(it + 1) * 128, :]
                    par = prep["par"]; prep["par"] ^= 1
                    wpin = t["wpin0"] if par == 0 else t["wpin1"]
                    psem = "dwp0" if par == 0 else "dwp1"
                    ld = p.dma("gpsimd", wpin[:], src, psem,
                               waits=[prep["cons"][par]])
                    j = it % 2
                    out_ap = t["wstg"][:, :, j, :]
                    tq = quant(wpin[:], E, out_ap, wscr,
                               [(psem, p.cnt[psem])], "sg",
                               out_waits=[prep["stg_free"]], pool_stair=True)
                    prep["cons"][par] = tq
                    if j == 1:
                        gw = it // 2
                        dsth = dr["wgs"] if kind == "g" else dr["wus"]
                        for c in range(8):
                            trk = p.dma("sync", t["wstgT"][:], t["wstg"][:, c],
                                        "dwpt", waits=[tq, p.total("dwps")],
                                        transpose=True)
                            p.dma("sync", dsth[c, :, gw * 256:(gw + 1) * 256],
                                  t["wstgT"][:].rearrange("p j f -> p (j f)"),
                                  "dwps", waits=[trk])
                        prep["stg_free"] = p.total("dwps")
                else:
                    for (c0, cw) in [(0, 1024), (1024, 1024), (2048, 768)]:
                        par = prep["par"]; prep["par"] ^= 1
                        wpin = t["wpin0"] if par == 0 else t["wpin1"]
                        psem = "dwp0" if par == 0 else "dwp1"
                        ld = p.dma("gpsimd", wpin[:, 0:cw],
                                   dr["wdn"][it * 128:(it + 1) * 128, c0:c0 + cw],
                                   psem, waits=[prep["cons"][par]])
                        out_ap = t["wstgD"][:, 0:cw // 128, :]
                        tq = quant(wpin[:, 0:cw], cw, out_ap, wscr,
                                   [(psem, p.cnt[psem])], "sg",
                                   out_waits=[prep["stgD_free"]], pool_stair=True)
                        prep["cons"][par] = tq
                        for c in range(cw // 128):
                            trk = p.dma("sync", t["wstgT"][:, 0, :].unsqueeze(1),
                                        t["wstgD"][:, c].unsqueeze(1), "dwpt",
                                        waits=[tq, p.total("dwps")], transpose=True)
                            p.dma("sync", dr["wds"][c0 // 128 + c, :,
                                                    it * 128:(it + 1) * 128],
                                  t["wstgT"][:, 0, :], "dwps", waits=[trk])
                        prep["stgD_free"] = p.total("dwps")

        pp_free = {i: ("ss", 0) for i in range(4)}
        pp_i = [0]

        def next_pp():
            i_ = pp_i[0] % 4
            pp_i[0] += 1
            return i_

        s2_free = ("ss", 0)
        s2_free2 = {0: ("ss", 0), 1: ("ss", 0)}
        oa_free = ("sv", 0)
        ob_free = ("sv", 0)
        wo_free = {0: ("sv", 0), 1: ("sv", 0)}
        rbc_free = ("sv", 0)
        rbs_free = {0: ("sv", 0), 1: ("sv", 0)}
        pt_free = {0: ("sg", 0), 1: ("sg", 0)}
        xr_free = ("sv", 0)
        ln1_free = {0: ("dl10", 0), 1: ("dl11", 0)}
        ln_last = ("sv", 0)
        slice_tok = [("sv", 0)]
        last_av = ("pe", 0)

        def emit_xload(s_i2, qt2):
            xsl = dr["x"][s_i2].rearrange("(n h t p) (c f) -> n h p c t f",
                                          h=2, t=2, p=128, f=128)[qt2]
            for hf in range(2):
                ldx = p.dma("gpsimd", t["xst"][:], xsl[hf], "dxs",
                            waits=slice_tok[-1:] + [p.total("dxt")])
                for c in range(8):
                    p.dma("sync",
                          t["xT"][:, c, hf * 256:(hf + 1) * 256]
                          .rearrange("p (j f) -> p j f", f=128),
                          t["xst"][:, c], "dxt", waits=[ldx], transpose=True)
            return p.total("dxt")

        xT_rdy = emit_xload(0, 0)
        for s_i in range(NSEQ):
            xs = dr["x"][s_i]
            vset = p.op("gpsimd", "memset", t["V"][:, :, :, HD:HD + 1], 1.0,
                        waits=[last_av])

            for qt in range(NQT):
                sl = slice(qt * TS, (qt + 1) * TS)

                def proj(wname, bias, nout, kchunks, src_cT, rdy, evac):
                    toks = []
                    for m in range(nout):
                        ppi = next_pp()
                        mm = None
                        for ci in range(kchunks):
                            w0 = ([rdy, pp_free[ppi], wl_all, s2_free2[0],
                                   s2_free2[1]] if ci == 0 else [])
                            mm = p.op("tensor", "matmul", pp[ppi],
                                      t[wname][:, ci, m * 128:(m + 1) * 128],
                                      t[src_cT][:, ci],
                                      start=(ci == 0), stop=(ci == kchunks - 1),
                                      waits=w0)
                        dtk = evac(m, ppi, mm)
                        pp_free[ppi] = dtk
                        toks.append(dtk)
                    return toks

                cq_toks = proj("wqb", "bqp", 2, 8, "xT", xT_rdy,
                               lambda m, ppi, mm: p.op(
                                   "scalar", "activation", t["cqT"][:, m], pp[ppi],
                                   AF.Gelu, bias=t["bqp"][:, m:m + 1],
                                   waits=[mm] + slice_tok[-1:]))
                ckv_toks = proj("wkvb", "bkvp", 2, 8, "xT", xT_rdy,
                                lambda m, ppi, mm: p.op(
                                    "scalar", "activation", t["ckvT"][:, m], pp[ppi],
                                    AF.Gelu, bias=t["bkvp"][:, m:m + 1],
                                    waits=[mm] + slice_tok[-1:]))
                qc_toks = proj("wqcb", "bqcp", 7, 2, "cqT", cq_toks[-1],
                               lambda m, ppi, mm: p.op(
                                   "scalar", "activation", t["qTs"][:, m], pp[ppi],
                                   AF.Gelu, bias=t["bqcp"][:, m:m + 1], waits=[mm]))
                kc_toks = proj("wkcb", "bkcp", 7, 2, "ckvT", ckv_toks[-1],
                               lambda m, ppi, mm: p.op(
                                   "scalar", "activation", t["kT"][:, m, sl], pp[ppi],
                                   AF.Gelu, bias=t["bkcp"][:, m:m + 1], waits=[mm]))
                tbl = []
                for nm in ["chh", "shh"]:
                    tbl.append(p.dma("gpsimd", t[nm][:], dr[nm][:, sl], "drt",
                                     waits=slice_tok[-1:]))
                tbl_all = p.total("drt")

                rp_toks = []
                for which, wname, bname, dstT, dsl in [
                        ("q", "wqrb", "bqrp", "qTs", None),
                        ("k", "wkrb", "bkrp", "kT", sl)]:
                    qr_tok = proj(wname, bname, 1, 8, "xT", xT_rdy,
                                  lambda m, ppi, mm: p.op(
                                      "vector", "tensor_scalar", t["qr"][:], pp[ppi],
                                      t[bname][:, 0:1], None, ALU.add,
                                      waits=[mm] + rp_toks[-1:] + slice_tok[-1:]))[0]
                    w1 = p.dma("gpsimd", t["qrw"][0:64, :], t["qr"][64:128, :], "dsw",
                               waits=[qr_tok] + rp_toks[-1:])
                    w2 = p.dma("gpsimd", t["qrw"][64:128, :], t["qr"][0:64, :], "dsw",
                               waits=[qr_tok])
                    sw_all = p.total("dsw")
                    a = p.op("vector", "tensor_tensor", RopA, t["qr"][:],
                             t["chh"][:], ALU.mult, waits=[sw_all, tbl_all])
                    b = p.op("vector", "tensor_tensor", RopB, t["qrw"][:],
                             t["shh"][:], ALU.mult, waits=[a])
                    g = p.op("vector", "tensor_tensor", RopA, RopA,
                             RopB, ALU.add, waits=[b])
                    dst = t[dstT][:, 7] if dsl is None else t[dstT][:, 7, dsl]
                    rp_toks.append(p.op("scalar", "activation", dst, RopA,
                                        AF.Gelu, waits=[g]))

                v_toks = []
                for tt_ in range(4):
                    nt = qt * 4 + tt_
                    for dvh in range(2):
                        ppi = next_pp()
                        mm = None
                        for ci in range(2):
                            mm = p.op("tensor", "matmul", pp[ppi],
                                      t["ckvT"][:, ci, tt_ * 128:(tt_ + 1) * 128],
                                      t["wvb"][:, ci, dvh * 512:(dvh + 1) * 512],
                                      start=(ci == 0), stop=False,
                                      waits=([ckv_toks[-1], pp_free[ppi], tk_ones,
                                              s2_free2[0], s2_free2[1]] if ci == 0 else []))
                        mm = p.op("tensor", "matmul", pp[ppi],
                                  t["onesF"][0:1, 0:128],
                                  t["bvrb"][0:1, dvh * 512:(dvh + 1) * 512],
                                  start=False, stop=True)
                        ev = p.op("vector", "tensor_copy",
                                  t["V"][:, nt, dvh * 8:(dvh + 1) * 8, 0:HD],
                                  pp[ppi].rearrange("p (h d) -> p h d", d=64),
                                  waits=[mm, vset])
                        pp_free[ppi] = ev
                        v_toks.append(ev)
                slice_rdy = [qc_toks[-1], kc_toks[-1], rp_toks[0], rp_toks[1],
                             v_toks[-1], p.total("sv")]
                slice_tok = slice_rdy
                do_tap("P%d_%d" % (s_i, qt))

                if qt + 1 < NQT:
                    xT_rdy = emit_xload(s_i, qt + 1)
                elif s_i + 1 < NSEQ:
                    xT_rdy = emit_xload(s_i + 1, 0)
                emit_prep(2)
                kmax = 4 * (qt + 1)
                attn_cons = []

                def emit_qk(ph, kt):
                    buf = kt % 2
                    S2b = S2[:, buf * 1024:(buf + 1) * 1024]
                    ksl = slice(kt * KT, (kt + 1) * KT)
                    w0 = [s2_free2[buf]] + (slice_rdy
                                            if (ph == 0 and kt < 2) else [])
                    p.op("tensor", "matmul", S2b[:, 0:QT],
                         t["kT"][0:64, ph, ksl], t["qTs"][0:64, ph],
                         start=True, stop=True, tile_position=(0, 0),
                         waits=w0)
                    return p.op("tensor", "matmul", S2b[:, QT:2 * QT],
                                t["kT"][64:128, ph, ksl], t["qTs"][64:128, ph],
                                start=True, stop=True, tile_position=(64, 0))

                pend_qk = None
                for ph in range(8):
                    av_tok = None
                    qk_tok = {0: pend_qk} if pend_qk is not None \
                        else {0: emit_qk(ph, 0)}
                    pend_qk = None
                    for kt in range(kmax):
                        buf = kt % 2
                        ptb = kt % 2
                        PT = [t["PT0"], t["PT1"]][ptb]
                        exp_tok = p.op("scalar", "activation", PT[:],
                                       S2[:, buf * 1024:(buf + 1) * 1024],
                                       AF.Exp, scale=0.125,
                                       waits=[qk_tok[kt], pt_free[ptb]])
                        s2_free2[buf] = exp_tok
                        if kt + 1 < kmax:
                            qk_tok[kt + 1] = emit_qk(ph, kt + 1)
                        elif ph + 1 < 8:
                            pend_qk = emit_qk(ph + 1, 0)
                        mask_toks = []
                        r = kt - 4 * qt
                        if 0 <= r <= 3:
                            for hh in range(2):
                                mask_toks.append(p.op(
                                    "vector", "tensor_tensor",
                                    PT[:, hh * QT:(hh + 1) * QT],
                                    PT[:, hh * QT:(hh + 1) * QT],
                                    t["masks"][:, r, :], ALU.mult,
                                    waits=[exp_tok, wl_all]))
                        first, last = (kt == 0), (kt == kmax - 1)
                        wv_ = [exp_tok] + mask_toks
                        if first:
                            wv_ += [oa_free, ob_free]
                        p.op("tensor", "matmul", OA, t["V"][:, kt, 2 * ph, :],
                             PT[:, 0:QT],
                             start=first, stop=last, waits=wv_)
                        av_tok = p.op("tensor", "matmul", OB,
                                      t["V"][:, kt, 2 * ph + 1, :],
                                      PT[:, QT:2 * QT],
                                      start=first, stop=last)
                        pt_free[ptb] = av_tok
                        last_av = av_tok
                    rbsb = t["rbs"] if ph % 2 == 0 else t["rbs2"]
                    rAB = p.op("vector", "reciprocal", t["rcpA"][64:65, :],
                               t["PS_B"][64:65, 0:1024], waits=[av_tok], lp=True)
                    bA = p.op("tensor", "matmul", RB, t["onesF"][64:65, 0:64],
                              t["rcpA"][64:65, 0:QT], start=True, stop=True,
                              tile_position=(64, 0), waits=[rAB, rbc_free])
                    bB = p.op("tensor", "matmul", RB2, t["onesF"][64:65, 0:64],
                              t["rcpA"][64:65, QT:2 * QT], start=True, stop=True,
                              tile_position=(64, 0))
                    cpA = p.op("vector", "tensor_copy", rbsb[:, 0:QT], RB,
                               waits=[bA, rbs_free[ph % 2]], lp=True)
                    cpB = p.op("vector", "tensor_copy", rbsb[:, QT:2 * QT], RB2,
                               waits=[bB], lp=True)
                    rbc_free = cpB
                    cA = p.op("vector", "scalar_tensor_tensor",
                              t["attnT"][0:64, ph, :], OA[0:64, :], 1.0,
                              rbsb[:, 0:QT], ALU.mult, ALU.mult,
                              waits=[cpA])
                    oa_free = cA
                    wB = p.op("vector", "scalar_tensor_tensor",
                              t["attnT"][64:128, ph, :],
                              OB[0:64, :], 1.0, rbsb[:, QT:2 * QT],
                              ALU.mult, ALU.mult, waits=[cpB])
                    ob_free = wB
                    rbs_free[ph % 2] = wB
                    attn_cons = [cA, wB]
                    emit_prep(1)
                do_tap("A%d_%d" % (s_i, qt))

                for tt_ in range(4):
                    gt = qt * 4 + tt_
                    tsl = slice(gt * TT, (gt + 1) * TT)
                    lsl = slice(tt_ * 128, (tt_ + 1) * 128)
                    ldr = p.dma("gpsimd", t["xr"][:], xs[tsl, :], "dre",
                                waits=[xr_free])
                    ev_toks = []
                    for eh in range(2):
                        esl = slice(eh * 512, (eh + 1) * 512)
                        mm = None
                        for c in range(8):
                            w0 = ([wo_free[eh], rbc_free] + attn_cons
                                  if c == 0 else [])
                            mm = p.op("tensor", "matmul", wop[eh],
                                      t["attnT"][:, c, lsl], t["wob"][:, c, esl],
                                      start=(c == 0), stop=False, waits=w0)
                        mm = p.op("tensor", "matmul", wop[eh],
                                  t["onesF"][0:1, 0:128], t["borb"][0:1, esl],
                                  start=False, stop=True)
                        ev = p.op("vector", "tensor_tensor", t["rres"][:, esl],
                                  wop[eh], t["xr"][:, esl], ALU.add,
                                  waits=[mm, p.total("dre"), ln_last])
                        wo_free[eh] = ev
                        ev_toks.append(ev)
                    xr_free = ev_toks[-1]
                    rbc_free = ev_toks[0]
                    sm = p.op("vector", "tensor_reduce", t["s_sum"][:], t["rres"][:],
                              axis=AX.X, op=ALU.add, waits=ev_toks)
                    sqa = p.op("scalar", "activation", t["qrw"][:], t["rres"][:, 0:512],
                               AF.Square, accum_out=t["s_sq"][:], waits=ev_toks)
                    sq = p.op("scalar", "activation", t["qrw"][:], t["rres"][:, 512:1024],
                              AF.Square, accum_out=t["s_sq2"][:], waits=[sqa])
                    mn = p.op("vector", "tensor_scalar", t["s_mean"][:], t["s_sum"][:],
                              1.0 / E, None, ALU.mult, waits=[sm])
                    s12 = p.op("vector", "tensor_tensor", t["s_sq"][:], t["s_sq"][:],
                               t["s_sq2"][:], ALU.add, waits=[sq])
                    v1 = p.op("vector", "tensor_scalar", t["s_var"][:], t["s_sq"][:],
                              1.0 / E, None, ALU.mult, waits=[s12])
                    m2 = p.op("vector", "tensor_tensor", t["s_sum"][:], t["s_mean"][:],
                              t["s_mean"][:], ALU.mult, waits=[mn])
                    v2 = p.op("vector", "tensor_tensor", t["s_var"][:], t["s_var"][:],
                              t["s_sum"][:], ALU.subtract, waits=[v1, m2])
                    lg = p.op("scalar", "activation", t["s_rstd"][:], t["s_var"][:],
                              AF.Ln, bias=t["epst"][:, 0:1], waits=[v2, tk_eps])
                    ex = p.op("scalar", "activation", t["s_rstd"][:], t["s_rstd"][:],
                              AF.Exp, scale=-0.5, waits=[lg])
                    ln1t = t["ln1t0"]
                    no = p.op("vector", "tensor_scalar", ln1t[:], t["rres"][:],
                              t["s_mean"][:, 0:1], t["s_rstd"][:, 0:1],
                              ALU.subtract, ALU.mult,
                              waits=[ex, ln1_free[0]])
                    ln_last = no
                    st_ = p.dma("gpsimd", dr["ln1s"][s_i][tsl, :], ln1t[:],
                                "dl10", waits=[no])
                    ln1_free[0] = st_
                slice_tok = [wo_free[0], wo_free[1], ln_last]
                do_tap("W%d_%d" % (s_i, qt))

        emit_prep(len(prep_items))
        p.barrier(dma_drain=["dwp0", "dwp1", "dwpt", "dwps", "dl10", "dl11",
                             "dre", "dxs", "dxt", "drt", "dsw", "dswg"])

        for nm, src in [("wgT", "wgs"), ("wuT", "wus"), ("wdT", "wds")]:
            p.dma("gpsimd", t[nm][:], dr[src].rearrange("c p m -> p c m"), "dwft")
        wft_all = p.total("dwft")
        T_H1 = {tl: t["h1ab"].rearrange("p (c t f) -> p c t f", t=2, f=128)
                [:, :, tl, :] for tl in range(NT)}
        xq_scr = {"abs": "qabs", "pa": "qpa", "pb": "qpb", "sgn": "qsgn",
                  "bm": "qbm", "sc": "qsc", "rc": "qrc"}
        tk_eps2 = p.op("gpsimd", "memset", t["epst"][:], 1e-5)

        gu_free = {"g0": ("sv", 0), "u0": ("sv", 0), "g1": ("sv", 0), "u1": ("sv", 0)}
        self_hsg = {0: ("sv", 0), 1: ("sv", 0)}
        prev_q_last = [("sg", 0)]
        dn_free = {0: ("ss", 0), 1: ("ss", 0)}
        out_free = {0: ("dou0", 0), 1: ("dou1", 0)}
        no_prev = {0: ("sv", 0), 1: ("sv", 0)}
        hq2_free = ("dhq", 0)
        xqT_free = ("pe", 0)
        hqT_free = ("pe", 0)
        R2 = {0: t["r2"], 1: t["r2b"]}
        T_LN = {0: t["ln1L0"], 1: t["ln1L1"]}
        xq_state = {"free": {0: ("sv", 0), 1: ("sv", 0)},
                    "xq2_free": ("dxq", 0), "ld": {}, "q_toks": []}

        def emit_xq(s_i2, g2):
            for tl in range(NT):
                gt = g2 * NT + tl
                tsl = slice(gt * TT, (gt + 1) * TT)
                lsem = "dlf0" if tl == 0 else "dlf1"
                p.dma("gpsimd", T_LN[tl][:], dr["ln1s"][s_i2][tsl, :], lsem,
                      waits=[xq_state["free"][tl]])
                xq_state["ld"][tl] = (lsem, p.cnt[lsem])
            q_toks = []
            for tl in range(NT):
                q_toks.append(quant(T_LN[tl][:], E, t["xq2"][:, :, tl, :], xq_scr,
                                    [xq_state["ld"][tl]], "sg",
                                    out_waits=[xq_state["xq2_free"]]))
            xq_state["q_toks"] = q_toks

        def emit_xqT(xqT_free_):
            for c in range(8):
                p.dma("scalar", t["xqT"][:, c].rearrange("p (j f) -> p j f", f=128),
                      t["xq2"][:, c], "dxq",
                      waits=xq_state["q_toks"] + [xqT_free_], transpose=True)
            xq_state["xq2_free"] = p.total("dxq")
            return p.total("dxq")

        glist = [(si2, gg) for si2 in range(NSEQ) for gg in range(NTT // NT)]
        emit_xq(*glist[0])
        xqT_rdy = emit_xqT(xqT_free)
        for gidx, (s_i, g_) in enumerate(glist):
            if True:
                cv_tok = {}
                for tl in range(NT):
                    cv_tok[tl] = p.op("gpsimd", "tensor_copy", R2[tl][:],
                                      T_LN[tl][:],
                                      waits=[out_free[tl], xq_state["ld"][tl]])
                    xq_state["free"][tl] = cv_tok[tl]
                nsteps = len(ITS)
                h2_tok = {}
                q_tok = {}
                tr_rdy = {}
                dn_last = {0: None, 1: None}

                def emit_gu(it):
                    i0, iw = ITS[it]
                    res = {}
                    for tl in range(NT):
                        pr = (it * NT + tl) % 2
                        gps = gps2[pr]
                        ups = ups2[pr]
                        mmg = None
                        for c in range(8):
                            mmg = p.op("tensor", "matmul", gps[:, 0:iw],
                                       t["xqT"][:, c, tl * 128:(tl + 1) * 128],
                                       t["wgT"][:, c, i0:i0 + iw],
                                       start=(c == 0), stop=(c == 7),
                                       waits=([xqT_rdy, gu_free["g%d" % pr],
                                               wft_all, rbc_free] if c == 0 else []))
                        mmu = None
                        for c in range(8):
                            mmu = p.op("tensor", "matmul", ups[:, 0:iw],
                                       t["xqT"][:, c, tl * 128:(tl + 1) * 128],
                                       t["wuT"][:, c, i0:i0 + iw],
                                       start=(c == 0), stop=(c == 7),
                                       waits=([gu_free["u%d" % pr]] if c == 0 else []))
                        res[tl] = (mmg, mmu, pr)
                    return res

                def emit_h(it, gu, xfree):
                    i0, iw = ITS[it]
                    toks = {}
                    for tl in range(NT):
                        mmg, mmu, pr = gu[tl]
                        gps = gps2[pr]
                        ups = ups2[pr]
                        hsg_ = t["hsg"] if tl == 0 else t["hsgB"]
                        hv = T_H1[tl][:, 0:iw // 128, :]
                        tg = p.op("scalar", "activation", hsg_[:, 0:iw],
                                  gps[:, 0:iw], AF.Tanh, scale=0.5,
                                  waits=[mmg, self_hsg[tl]])
                        h1 = p.op("vector", "scalar_tensor_tensor",
                                  hv,
                                  hsg_[:, 0:iw].rearrange(
                                      "p (c f) -> p c f", f=128),
                                  1.0,
                                  ups[:, 0:iw].rearrange(
                                      "p (c f) -> p c f", f=128),
                                  ALU.add, ALU.mult,
                                  waits=[tg, mmu] + (xfree if tl == 0 else []))
                        gu_free["u%d" % pr] = h1
                        h2 = p.op("vector", "scalar_tensor_tensor",
                                  hv, hv, 0.5,
                                  gps[:, 0:iw].rearrange(
                                      "p (c f) -> p c f", f=128),
                                  ALU.mult, ALU.mult, waits=[h1])
                        gu_free["g%d" % pr] = h2
                        self_hsg[tl] = h1
                        toks[tl] = h2
                    return toks

                def emit_quant(it, mid=None):
                    i0, iw = ITS[it]
                    c0 = i0 // 128
                    out_ap = t["hq2"][:, c0:c0 + iw // 128, :, :] \
                        .rearrange("p c t f -> p (c t) f")
                    tok = quant(t["h1ab"][:, 0:2 * iw], 2 * iw, out_ap,
                                xq_scr,
                                [h2_tok[it][0], h2_tok[it][1]], "sg",
                                out_waits=[hq2_free], mid=mid)
                    return [tok]

                def emit_tr(it):
                    i0, iw = ITS[it]
                    c0 = i0 // 128
                    dsem = ["dhq", "dhq1", "dhq2", "dhq3", "dhq4", "dhq5"][it]
                    dq = "sync" if it % 2 == 0 else "scalar"
                    for c in range(c0, c0 + iw // 128):
                        p.dma(dq,
                              t["hqT"][:, c].rearrange("p (j f) -> p j f", f=128),
                              t["hq2"][:, c], dsem,
                              waits=q_tok[it] + [hqT_free], transpose=True)
                    tr_rdy[it] = p.total(dsem)

                gu = emit_gu(0)
                gu_last = gu
                h2_tok[0] = emit_h(0, gu, [prev_q_last[0]])
                for it in range(nsteps):
                    gu_next = emit_gu(it + 1) if it + 1 < nsteps else None
                    if gu_next is not None:
                        gu_last = gu_next

                    def mid_fn(xfree, it=it, gu_next=gu_next):
                        if gu_next is not None:
                            h2_tok[it + 1] = emit_h(it + 1, gu_next, xfree)
                    q_tok[it] = emit_quant(it, mid=mid_fn)
                    emit_tr(it)
                prev_q_last[0] = q_tok[nsteps - 1][0]
                if gidx + 1 < len(glist):
                    emit_xq(*glist[gidx + 1])
                for it in range(nsteps):
                    i0, iw = ITS[it]
                    c0 = i0 // 128
                    for tl in range(NT):
                        for eh in range(2):
                            esl = slice(eh * 512, (eh + 1) * 512)
                            mm = None
                            for c in range(c0, c0 + iw // 128):
                                w0 = []
                                if c == c0:
                                    w0 = [tr_rdy[it]]
                                    if it == 0 and eh == 0:
                                        w0 += [dn_free[tl], s2_free]
                                mm = p.op("tensor", "matmul",
                                          t["PS_A"][:, tl * 1024 + esl.start:
                                                     tl * 1024 + esl.stop],
                                          t["hqT"][:, c, tl * 128:(tl + 1) * 128],
                                          t["wdT"][:, c, esl],
                                          start=(it == 0 and c == c0),
                                          stop=(it == nsteps - 1 and
                                                c == c0 + iw // 128 - 1),
                                          waits=w0)
                            if it == nsteps - 1:
                                dn_last[tl] = mm
                xqT_free = gu_last[NT - 1][1]
                if gidx + 1 < len(glist):
                    xqT_rdy = emit_xqT(xqT_free)
                hq2_free = dn_last[NT - 1]
                dn_toks = dn_last
                s2_free = ("ss", 0)
                hqT_free = dn_last[NT - 1]
                sq_toks = {}
                for tl in range(NT):
                    gt = g_ * NT + tl
                    tsl = slice(gt * TT, (gt + 1) * TT)
                    dnv = t["PS_A"][:, tl * 1024:(tl + 1) * 1024]
                    r2v = R2[tl]
                    r2t = []
                    for eh in range(2):
                        esl = slice(eh * 512, (eh + 1) * 512)
                        r2t.append(p.op("vector", "tensor_tensor", r2v[:, esl],
                                        dnv[:, esl], r2v[:, esl], ALU.add,
                                        waits=[dn_toks[tl], cv_tok[tl]]))
                    sm = p.op("vector", "tensor_reduce", t["f_sum"][:], r2v[:],
                              axis=AX.X, op=ALU.add, waits=r2t)
                    sqa = p.op("scalar", "activation", t["hsg"][:], r2v[:, 0:512],
                               AF.Square, accum_out=t["f_sq"][:], waits=r2t)
                    sq = p.op("scalar", "activation", t["hsg"][:], r2v[:, 512:1024],
                              AF.Square, accum_out=t["f_sq2"][:], waits=[sqa])
                    dn_free[tl] = sq
                    mn = p.op("vector", "tensor_scalar", t["f_mean"][:], t["f_sum"][:],
                              1.0 / E, None, ALU.mult, waits=[sm])
                    s12 = p.op("vector", "tensor_tensor", t["f_sq"][:], t["f_sq"][:],
                               t["f_sq2"][:], ALU.add, waits=[sq])
                    v1 = p.op("vector", "tensor_scalar", t["f_var"][:], t["f_sq"][:],
                              1.0 / E, None, ALU.mult, waits=[s12])
                    m2 = p.op("vector", "tensor_tensor", t["f_sum"][:], t["f_mean"][:],
                              t["f_mean"][:], ALU.mult, waits=[mn])
                    v2 = p.op("vector", "tensor_tensor", t["f_var"][:], t["f_var"][:],
                              t["f_sum"][:], ALU.subtract, waits=[v1, m2])
                    lg = p.op("scalar", "activation", t["f_rstd"][:], t["f_var"][:],
                              AF.Ln, bias=t["epst"][:, 0:1], waits=[v2, tk_eps2])
                    ex = p.op("scalar", "activation", t["f_rstd"][:], t["f_rstd"][:],
                              AF.Exp, scale=-0.5, waits=[lg])
                    no = p.op("vector", "tensor_scalar", r2v[:], r2v[:],
                              t["f_mean"][:, 0:1], t["f_rstd"][:, 0:1],
                              ALU.subtract, ALU.mult, waits=[ex])
                    osem = "dou0" if tl == 0 else "dou1"
                    p.dma("sync", dr["out"][s_i][tsl, :], r2v[:], osem, waits=[no])
                    out_free[tl] = p.total(osem)
                s2_free = dn_free[NT - 1]
                do_tap("F%d_%d" % (s_i, g_))
        p.wait("sync", [p.total("dou0"), p.total("dou1")])

    with nc.Block() as block:
        @block.sync
        def _(h):
            p.start("sync", h)
            emit_program()

        @block.gpsimd
        def _(h):
            p.start("gpsimd", h)
            emit_program()

        @block.tensor
        def _(h):
            p.start("tensor", h)
            emit_program()

        @block.scalar
        def _(h):
            p.start("scalar", h)
            emit_program()

        @block.vector
        def _(h):
            p.start("vector", h)
            emit_program()

    return nc, dr


def host_inputs(inputs, seq_pair):
    g = lambda k: np.ascontiguousarray(np.asarray(inputs[k]))
    im = {}
    im["x"] = g("L0_attn_x")[2 * seq_pair:2 * seq_pair + 2]
    im["wq"] = g("L0_attn_wq"); im["wqc"] = g("L0_attn_wqc"); im["wqr"] = g("L0_attn_wqr")
    im["wkv"] = g("L0_attn_wkv"); im["wkr"] = g("L0_attn_wkr"); im["wkc"] = g("L0_attn_wkc")
    im["wv"] = g("L0_attn_wv"); im["wo"] = g("L0_attn_wo")
    im["bqp"] = g("L0_attn_bq").reshape(2, 128).T.copy()
    im["bqcp"] = g("L0_attn_bqc").reshape(7, 128).T.copy()
    im["bqrp"] = g("L0_attn_bqr").reshape(1, 128).T.copy()
    im["bkvp"] = g("L0_attn_bkv").reshape(2, 128).T.copy()
    im["bkrp"] = g("L0_attn_bkr").reshape(1, 128).T.copy()
    im["bkcp"] = g("L0_attn_bkc").reshape(7, 128).T.copy()
    im["bvr"] = g("L0_attn_bv").reshape(1, E)
    im["bor"] = g("L0_attn_bo").reshape(1, E)
    im["wg"] = g("L0_ffn_w_gate"); im["wu"] = g("L0_ffn_w_up"); im["wdn"] = g("L0_ffn_w_down")
    import jax
    import jax.numpy as jnp
    with jax.default_device(jax.devices("cpu")[0]):
        inv_freq = 1.0 / 10000.0 ** (jnp.arange(0, DR, 2, dtype=jnp.float32) / DR)
        tpos = jnp.arange(S, dtype=jnp.float32)
        freqs = tpos[:, None] * inv_freq[None, :]
        cos32 = np.asarray(jnp.cos(freqs)).T
        sin32 = np.asarray(jnp.sin(freqs)).T
    im["chh"] = np.concatenate([cos32, cos32], 0).astype(np.float32)
    im["shh"] = np.concatenate([-sin32, sin32], 0).astype(np.float32)
    masks = np.zeros((128, 4, QT), dtype=ml_dtypes.bfloat16)
    qf = np.arange(QT)
    for r in range(4):
        for kp in range(128):
            masks[kp, r, :] = (qf >= kp + 128 * r).astype(ml_dtypes.bfloat16)
    im["masks"] = masks
    return {"i_" + k: v for k, v in im.items()}


_PROG_CACHE = {}


def kernel(**inputs):
    if "prog" not in _PROG_CACHE:
        _PROG_CACHE["prog"] = build_program()
    nc, dr = _PROG_CACHE["prog"]
    in_maps = [host_inputs(inputs, c) for c in range(NCORE)]
    res = run_bass_kernel_spmd(nc, in_maps, core_ids=list(range(NCORE)))
    return np.concatenate([res.results[c]["out"] for c in range(NCORE)], axis=0)


# version 28
# speedup vs baseline: 1.0245x; 1.0245x over previous
import math
from contextlib import ExitStack

import numpy as np
import ml_dtypes

import concourse.bass as bass
import concourse.mybir as mybir
from concourse.bass_utils import run_bass_kernel_spmd

F32 = mybir.dt.float32
BF16 = mybir.dt.bfloat16
FP16 = mybir.dt.float16
AF = mybir.ActivationFunctionType
ALU = mybir.AluOpType
AX = mybir.AxisListType

S = 2048
E = 1024
DC = 256
DR = 128
H = 16
HD = 64
I = 2816
NSEQ = 2
NCORE = 8
TS = 512
NSL = S // TS
TT = 128
NTT = S // TT
QT = 512
NQT = NSL
KT = 128
NKT = S // KT
NT = 2
ITS = [(k * 512, min(512, I - k * 512)) for k in range(6)]
RCP6 = float(np.float32(1.0) / np.float32(6.0))
C6H = float(np.float32(1.0 / 6.0))
C6L = float(np.float64(1.0) / 6.0 - np.float64(np.float32(1.0 / 6.0)))
ENG = {"pe": "tensor", "sv": "vector", "ss": "scalar", "sg": "gpsimd", "sy": "sync"}


class Prog:
    def __init__(self, nc, sems):
        self.nc = nc
        self.sems = sems
        self.which = None
        self.h = None
        self.cnt = None

    def start(self, which, handle):
        self.which = which
        self.h = handle
        self.cnt = {k: 0 for k in self.sems}

    def op(self, engine, method, *args, waits=(), sem=None, lp=False, **kw):
        sem = sem or {"tensor": "pe", "vector": "sv", "scalar": "ss",
                      "gpsimd": "sg", "sync": "sy"}[engine]
        self.cnt[sem] += 1
        val = self.cnt[sem]
        if self.which == engine:
            waits = [(s, v) for (s, v) in waits if v > 0]
            if len(waits) > 1:
                for s, v in waits[:-1]:
                    self.h.wait_ge(self.sems[s], v)
            if lp:
                with self.nc.allow_low_precision(reason="bf16 softmax denominators (error budgeted)"):
                    inst = getattr(self.h, method)(*args, **kw)
            else:
                inst = getattr(self.h, method)(*args, **kw)
            if waits:
                inst._wait_ge(self.sems[waits[-1][0]], waits[-1][1])
            inst.then_inc(self.sems[sem], 1)
        return (sem, val)

    def dma(self, queue, out, in_, sem, waits=(), transpose=False):
        self.cnt[sem] += 16
        val = self.cnt[sem]
        if self.which == queue:
            waits = [(s, v) for (s, v) in waits if v > 0]
            if len(waits) > 1:
                for s, v in waits[:-1]:
                    self.h.wait_ge(self.sems[s], v)
            inst = self.h.dma_start(out=out, in_=in_, transpose=transpose)
            if waits:
                inst._wait_ge(self.sems[waits[-1][0]], waits[-1][1])
            inst.then_inc(self.sems[sem], 16)
        return (sem, val)

    def total(self, sem):
        return (sem, self.cnt[sem])

    def wait(self, engine, waits):
        if self.which == engine:
            for s, v in waits:
                if v > 0:
                    self.h.wait_ge(self.sems[s], v)

    def barrier(self, dma_drain=()):
        drains = [(s, self.cnt[s]) for s in dma_drain if self.cnt[s] > 0]
        vals = {}
        for e in ["pe", "sv", "ss", "sg", "sy"]:
            self.cnt[e] += 1
            vals[e] = self.cnt[e]
        for e, eng in ENG.items():
            if self.which == eng:
                for s, v in drains:
                    self.h.wait_ge(self.sems[s], v)
                self.h.nop().then_inc(self.sems[e], 1)
                for o, ov in vals.items():
                    if o != e:
                        self.h.wait_ge(self.sems[o], ov)


def build_program(tap=None):
    nc = bass.Bass("TRN2", target_bir_lowering=False)
    dr = {}

    def din(nm, shp, dt=F32):
        dr[nm] = nc.dram_tensor("i_" + nm, shp, dt, kind="ExternalInput").ap()

    din("x", [NSEQ, S, E])
    din("wq", [E, DC]); din("wqc", [DC, E - DR]); din("wqr", [E, DR])
    din("wkv", [E, DC]); din("wkr", [E, DR]); din("wkc", [DC, E - DR])
    din("wv", [DC, E]); din("wo", [E, E])
    din("bqp", [128, 2]); din("bqcp", [128, 7]); din("bqrp", [128, 1])
    din("bkvp", [128, 2]); din("bkrp", [128, 1]); din("bkcp", [128, 7])
    din("bvr", [1, E]); din("bor", [1, E])
    din("wg", [I, E]); din("wu", [I, E]); din("wdn", [E, I])
    din("chh", [128, S]); din("shh", [128, S])
    din("masks", [128, 4, QT], BF16)
    dr["out"] = nc.dram_tensor("out", [NSEQ, S, E], F32, kind="ExternalOutput").ap()
    dr["ln1s"] = nc.dram_tensor("ln1s", [NSEQ, S, E], BF16, kind="Internal").ap()
    dr["wgs"] = nc.dram_tensor("wgs", [8, 128, I], BF16, kind="Internal").ap()
    dr["wus"] = nc.dram_tensor("wus", [8, 128, I], BF16, kind="Internal").ap()
    dr["wds"] = nc.dram_tensor("wds", [22, 128, E], BF16, kind="Internal").ap()
    if tap is not None:
        dr["tapout"] = nc.dram_tensor("tapout", tap["shape"], tap.get("dtype", F32),
                                      kind="ExternalOutput").ap()

    T = {}
    root = ExitStack()

    def alloc(stk, name, shape, dtype, psum=False):
        fn = nc.psum_tensor if psum else nc.sbuf_tensor
        T[name] = stk.enter_context(fn(name, shape, dtype))
        return T[name]

    alloc(root, "onesF", [128, 128], BF16)
    alloc(root, "epst", [128, 1], F32)
    alloc(root, "PS_A", [128, 2048], F32, psum=True)
    alloc(root, "PS_B", [128, 1024], F32, psum=True)
    alloc(root, "PS_C", [128, 512], F32, psum=True)
    alloc(root, "PS_D", [128, 512], F32, psum=True)

    att = ExitStack()
    for nm, shp, dt in [
        ("masks", [128, 4, QT], BF16),
        ("wqb", [128, 8, DC], BF16), ("wqcb", [128, 2, E - DR], BF16),
        ("wqrb", [128, 8, DR], BF16), ("wkvb", [128, 8, DC], BF16),
        ("wkrb", [128, 8, DR], BF16), ("wkcb", [128, 2, E - DR], BF16),
        ("wvb", [128, 2, E], BF16), ("wob", [128, 8, E], BF16),
        ("bqp", [128, 2], F32), ("bqcp", [128, 7], F32), ("bqrp", [128, 1], F32),
        ("bkvp", [128, 2], F32), ("bkrp", [128, 1], F32), ("bkcp", [128, 7], F32),
        ("bvrb", [1, E], BF16), ("borb", [1, E], BF16),
        ("kT", [128, 8, S], BF16), ("V", [128, NKT, H, HD + 1], BF16),
        ("qTs", [128, 8, TS], BF16), ("attnT", [128, 8, TS], BF16),
        ("xst", [128, 8, 2, 128], BF16), ("xT", [128, 8, TS], BF16),
        ("cqT", [128, 2, TS], BF16), ("ckvT", [128, 2, TS], BF16),
        ("qr", [128, TS], F32), ("qrw", [128, TS], F32),
        ("chh", [128, TS], F32), ("shh", [128, TS], F32),
        ("PT0", [128, 2 * QT], BF16), ("PT1", [128, 2 * QT], BF16),
        ("rcpA", [128, 2 * QT], BF16),
        ("rbs", [64, 2 * QT], F32), ("rbs2", [64, 2 * QT], F32),
        ("xr", [128, E], F32), ("rres", [128, E], F32),
        ("s_sum", [128, 1], F32), ("s_sq", [128, 1], F32),
        ("s_sq2", [128, 1], F32),
        ("s_mean", [128, 1], F32), ("s_var", [128, 1], F32),
        ("s_rstd", [128, 1], F32),
        ("ln1t0", [128, E], BF16),
        ("wpin0", [128, E], F32), ("wpin1", [128, E], F32),
        ("wpabs", [128, E], F32), ("wppa", [128, E], F32), ("wppb", [128, E], F32),
        ("wpsgn", [128, E], F32),
        ("wpbm", [128, E // 16], F32), ("wpsc", [128, E // 16], F32),
        ("wprc", [128, E // 16], F32),
        ("wstg", [128, 8, 2, 128], BF16), ("wstgT", [128, 2, 128], BF16),
        ("wstgD", [128, 8, 128], BF16),
    ]:
        alloc(att, nm, shp, dt)
    att.close()

    ffn = ExitStack()
    for nm, shp, dt in [
        ("wgT", [128, 8, I], BF16), ("wuT", [128, 8, I], BF16),
        ("wdT", [128, 22, E], BF16),
        ("ln1L0", [128, E], BF16), ("ln1L1", [128, E], BF16),
        ("qabs", [128, E], F32), ("qpa", [128, E], F32), ("qpb", [128, E], F32),
        ("qsgn", [128, E], F32),
        ("qbm", [128, E // 16], F32), ("qsc", [128, E // 16], F32),
        ("qrc", [128, E // 16], F32),
        ("xq2", [128, 8, NT, 128], BF16), ("xqT", [128, 8, NT * 128], BF16),
        ("hsg", [128, 512], F32), ("hsgB", [128, 512], F32),
        ("h1ab", [128, 1024], F32),
        ("hq2", [128, 22, NT, 128], BF16), ("hqT", [128, 22, NT * 128], BF16),
        ("r2", [128, E], F32), ("r2b", [128, E], F32),
        ("f_sum", [128, 1], F32), ("f_sq", [128, 1], F32),
        ("f_sq2", [128, 1], F32),
        ("f_mean", [128, 1], F32), ("f_var", [128, 1], F32),
        ("f_rstd", [128, 1], F32),
    ]:
        alloc(ffn, nm, shp, dt)
    ffn.close()

    sems = {}
    for nm in ["pe", "sv", "ss", "sg", "sy",
               "dxs", "dxt", "dwl", "drt", "dsw", "dre",
               "dl10", "dl11", "dwp0", "dwp1", "dwpt", "dwps",
               "dwft", "dlf0", "dlf1", "dxq", "dhq", "dhq1", "dhq2", "dhq3",
               "dhq4", "dhq5", "dou0", "dou1", "dswg"]:
        sems[nm] = root.enter_context(nc.semaphore(nm))

    p = Prog(nc, sems)
    t = T

    def emit_program():
        tap_done = [False]

        def do_tap(tag):
            if tap is not None and tap["tag"] == tag and not tap_done[0]:
                tap_done[0] = True
                tsrc = tap["tensor"]
                if "@" in tsrc:
                    base, kk = tsrc.split("@")
                    tsrc_ap = t[base][:, :, 0:int(kk)]
                elif "|" in tsrc:
                    base, kk = tsrc.split("|")
                    tsrc_ap = t[base][:, 0:int(kk)]
                else:
                    tsrc_ap = t[tsrc][:]
                p.dma("sync", dr["tapout"], tsrc_ap, "dou0",
                      waits=[p.total("sv"), p.total("ss"), p.total("sg"),
                             p.total("dswg"), p.total("dsw"), p.total("pe")])
                p.barrier(dma_drain=["dou0"])

        RopA = t["rres"][:, 0:TS]
        RopB = t["rres"][:, TS:2 * TS]
        S2 = t["PS_A"]
        pp = [t["PS_A"][:, i * 512:(i + 1) * 512] for i in range(4)]
        OA = t["PS_B"][0:65, 0:512]
        OB = t["PS_B"][0:65, 512:1024]
        RB = t["PS_C"][0:64, :]
        RB2 = t["PS_D"][0:64, :]
        wop = [t["PS_C"][:, :], t["PS_D"][:, :]]
        gps2 = [t["PS_C"][:, :], t["PS_B"][0:128, 0:512]]
        ups2 = [t["PS_D"][:, :], t["PS_B"][0:128, 512:1024]]
        dn = [t["PS_A"][:, 0:1024], t["PS_A"][:, 1024:2048]]

        for nm, src in [("wqb", "wq"), ("wqrb", "wqr"), ("wkvb", "wkv"),
                        ("wkrb", "wkr"), ("wob", "wo"),
                        ("wqcb", "wqc"), ("wkcb", "wkc"), ("wvb", "wv")]:
            p.dma("gpsimd", t[nm][:], dr[src].rearrange("(c p) m -> p c m", p=128), "dwl")
        for nm in ["bqp", "bqcp", "bqrp", "bkvp", "bkrp", "bkcp", "masks"]:
            p.dma("gpsimd", t[nm][:], dr[nm], "dwl")
        p.dma("gpsimd", t["bvrb"][:], dr["bvr"], "dwl")
        p.dma("gpsimd", t["borb"][:], dr["bor"], "dwl")
        tk_ones = p.op("gpsimd", "memset", t["onesF"][:], 1.0)
        tk_eps = p.op("gpsimd", "memset", t["epst"][:], 1e-5)
        wl_all = p.total("dwl")

        scr_free = {}

        def quant(x_ap, F, out_ap, pre, x_waits, out_sem, out_waits=(),
                  mid=None, pool_stair=False):
            nblk = F // 16
            ab = t[pre["abs"]][:, 0:F]
            pa = t[pre["pa"]][:, 0:F]
            pb = t[pre["pb"]][:, 0:F]
            sg_ = t[pre["sgn"]][:, 0:F]
            bm = t[pre["bm"]][:, 0:nblk]
            sc = t[pre["sc"]][:, 0:nblk]
            rc = t[pre["rc"]][:, 0:nblk]
            pv = scr_free.get(pre["abs"], {})
            x_waits = list(x_waits)
            a3 = ab.rearrange("p (nb b) -> p nb b", b=16)
            x3 = x_ap.rearrange("p (nb b) -> p nb b", b=16)
            tkb = p.op("vector", "tensor_reduce", bm, x3, axis=AX.X, op=ALU.max,
                       apply_absolute_value=True,
                       waits=x_waits + pv.get("bm", []))
            tlo = p.op("vector", "tensor_scalar", rc, bm, C6L, None, ALU.mult,
                       waits=[tkb] + pv.get("rc", []))
            thi = p.op("vector", "scalar_tensor_tensor", sc, bm, C6H, rc,
                       ALU.mult, ALU.add, waits=[tlo] + pv.get("sc", []))
            tkc = p.op("vector", "tensor_scalar", sc, sc, 1e-12, None, ALU.max,
                       waits=[thi])
            tkr = p.op("vector", "reciprocal", rc, sc, waits=[tkc])
            tka = p.op("scalar", "activation", ab, x_ap, AF.Abs,
                       waits=x_waits + pv.get("ab", []))
            tks = p.op("scalar", "activation", sg_, x_ap, AF.Sign,
                       waits=x_waits + pv.get("sgn", []))
            tkc2 = tkc
            scb = sc
            rb_ = rc.unsqueeze(2).broadcast_to([128, nblk, 16])
            tkn = p.op("gpsimd", "tensor_tensor",
                       pa.rearrange("p (nb b) -> p nb b", b=16), a3, rb_, ALU.mult,
                       waits=[tkr, tks, tka] + pv.get("pa", []))
            if mid is not None:
                mid([tka, tks, tkb])
            axn = pa
            if pool_stair:
                t1 = p.op("gpsimd", "tensor_scalar", pb, axn, 0.25, 0.5,
                          ALU.is_gt, ALU.mult, waits=[tkn] + pv.get("pb", []))
                steps = [(0.75, 0.5), (1.25, 0.5), (1.75, 0.5),
                         (2.5, None), (3.5, None), (5.0, 2.0)]
                for thr, w in steps:
                    if w is None:
                        c_ = p.op("gpsimd", "tensor_scalar", ab, axn, thr,
                                  None, ALU.is_gt, waits=[t1])
                    else:
                        c_ = p.op("gpsimd", "tensor_scalar", ab, axn, thr,
                                  w, ALU.is_gt, ALU.mult, waits=[t1])
                    t1 = p.op("gpsimd", "tensor_tensor", pb, pb, ab,
                              ALU.add, waits=[c_])
                tL = t1
            else:
                t1 = p.op("vector", "tensor_scalar", pb, axn, 0.25, None,
                          ALU.is_gt, waits=[tkn] + pv.get("pb", []))
                for thr in (0.75, 1.25, 1.75):
                    t1 = p.op("vector", "scalar_tensor_tensor", pb, axn, thr,
                              pb, ALU.is_gt, ALU.add, waits=[t1])
                h1_ = p.op("vector", "tensor_scalar", ab, axn, 5.0, None,
                           ALU.is_gt, waits=[tkn, tkb])
                for thr in (5.0, 3.5, 2.5):
                    h1_ = p.op("vector", "scalar_tensor_tensor", ab, axn, thr,
                               ab, ALU.is_gt, ALU.add, waits=[h1_])
                tL = p.op("vector", "scalar_tensor_tensor", pb, pb, 0.5, ab,
                          ALU.mult, ALU.add, waits=[t1, h1_])
            tLs = p.op("gpsimd", "tensor_tensor", pb, pb, sg_, ALU.mult,
                       waits=[tL, tks])
            if out_ap.ndim == 3:
                nch = out_ap.shape[1]
                o4 = out_ap.rearrange("p c (s b) -> p c s b", b=16)
                pb4 = pb.rearrange("p (c s b) -> p c s b", c=nch, b=16)
                sb_ = scb.rearrange("p (c s) -> p c s", s=8).unsqueeze(3) \
                    .broadcast_to([128, nch, 8, 16])
                tko = p.op("gpsimd", "tensor_tensor", o4, pb4, sb_, ALU.mult,
                           waits=[tLs, tkc2] + list(out_waits), sem=out_sem)
            else:
                sb_ = scb.unsqueeze(2).broadcast_to([128, nblk, 16])
                tko = p.op("gpsimd", "tensor_tensor",
                           out_ap.rearrange("p (nb b) -> p nb b", b=16),
                           pb.rearrange("p (nb b) -> p nb b", b=16), sb_, ALU.mult,
                           waits=[tLs, tkc2] + list(out_waits), sem=out_sem)
            scr_free[pre["abs"]] = {
                "ab": [tL, tkn], "sgn": [tLs], "bm": [thi],
                "rc": [thi, tkn], "sc": [tko], "pa": [tL], "pb": [tko]}
            return tko

        prep_items = []
        for i_ in range(22):
            prep_items.append(("g", i_))
        for i_ in range(22):
            prep_items.append(("u", i_))
        for i_ in range(8):
            prep_items.append(("d", i_))
        prep = {"i": 0, "par": 0, "cons": {0: ("sv", 0), 1: ("sv", 0)},
                "stg_free": ("dwps", 0), "stgD_free": ("dwps", 0)}
        wscr = {"abs": "wpabs", "pa": "wppa", "pb": "wppb", "sgn": "wpsgn",
                "bm": "wpbm", "sc": "wpsc", "rc": "wprc"}

        def emit_prep(n):
            for _ in range(n):
                if prep["i"] >= len(prep_items):
                    return
                kind, it = prep_items[prep["i"]]
                prep["i"] += 1
                if kind in ("g", "u"):
                    src = (dr["wg"] if kind == "g" else dr["wu"])[
                        it * 128:(it + 1) * 128, :]
                    par = prep["par"]; prep["par"] ^= 1
                    wpin = t["wpin0"] if par == 0 else t["wpin1"]
                    psem = "dwp0" if par == 0 else "dwp1"
                    ld = p.dma("gpsimd", wpin[:], src, psem,
                               waits=[prep["cons"][par]])
                    j = it % 2
                    out_ap = t["wstg"][:, :, j, :]
                    tq = quant(wpin[:], E, out_ap, wscr,
                               [(psem, p.cnt[psem])], "sg",
                               out_waits=[prep["stg_free"]])
                    prep["cons"][par] = tq
                    if j == 1:
                        gw = it // 2
                        dsth = dr["wgs"] if kind == "g" else dr["wus"]
                        for c in range(8):
                            trk = p.dma("sync", t["wstgT"][:], t["wstg"][:, c],
                                        "dwpt", waits=[tq, p.total("dwps")],
                                        transpose=True)
                            p.dma("sync", dsth[c, :, gw * 256:(gw + 1) * 256],
                                  t["wstgT"][:].rearrange("p j f -> p (j f)"),
                                  "dwps", waits=[trk])
                        prep["stg_free"] = p.total("dwps")
                else:
                    for (c0, cw) in [(0, 1024), (1024, 1024), (2048, 768)]:
                        par = prep["par"]; prep["par"] ^= 1
                        wpin = t["wpin0"] if par == 0 else t["wpin1"]
                        psem = "dwp0" if par == 0 else "dwp1"
                        ld = p.dma("gpsimd", wpin[:, 0:cw],
                                   dr["wdn"][it * 128:(it + 1) * 128, c0:c0 + cw],
                                   psem, waits=[prep["cons"][par]])
                        out_ap = t["wstgD"][:, 0:cw // 128, :]
                        tq = quant(wpin[:, 0:cw], cw, out_ap, wscr,
                                   [(psem, p.cnt[psem])], "sg",
                                   out_waits=[prep["stgD_free"]])
                        prep["cons"][par] = tq
                        for c in range(cw // 128):
                            trk = p.dma("sync", t["wstgT"][:, 0, :].unsqueeze(1),
                                        t["wstgD"][:, c].unsqueeze(1), "dwpt",
                                        waits=[tq, p.total("dwps")], transpose=True)
                            p.dma("sync", dr["wds"][c0 // 128 + c, :,
                                                    it * 128:(it + 1) * 128],
                                  t["wstgT"][:, 0, :], "dwps", waits=[trk])
                        prep["stgD_free"] = p.total("dwps")

        pp_free = {i: ("ss", 0) for i in range(4)}
        pp_i = [0]

        def next_pp():
            i_ = pp_i[0] % 4
            pp_i[0] += 1
            return i_

        s2_free = ("ss", 0)
        s2_free2 = {0: ("ss", 0), 1: ("ss", 0)}
        oa_free = ("sv", 0)
        ob_free = ("sv", 0)
        wo_free = {0: ("sv", 0), 1: ("sv", 0)}
        rbc_free = ("sv", 0)
        rbs_free = {0: ("sv", 0), 1: ("sv", 0)}
        pt_free = {0: ("sg", 0), 1: ("sg", 0)}
        xr_free = ("sv", 0)
        ln1_free = {0: ("dl10", 0), 1: ("dl11", 0)}
        ln_last = ("sv", 0)
        slice_tok = [("sv", 0)]
        last_av = ("pe", 0)

        def emit_xload(s_i2, qt2):
            xsl = dr["x"][s_i2].rearrange("(n h t p) (c f) -> n h p c t f",
                                          h=2, t=2, p=128, f=128)[qt2]
            for hf in range(2):
                ldx = p.dma("gpsimd", t["xst"][:], xsl[hf], "dxs",
                            waits=slice_tok[-1:] + [p.total("dxt")])
                for c in range(8):
                    p.dma("sync",
                          t["xT"][:, c, hf * 256:(hf + 1) * 256]
                          .rearrange("p (j f) -> p j f", f=128),
                          t["xst"][:, c], "dxt", waits=[ldx], transpose=True)
            return p.total("dxt")

        xT_rdy = emit_xload(0, 0)
        for s_i in range(NSEQ):
            xs = dr["x"][s_i]
            vset = p.op("gpsimd", "memset", t["V"][:, :, :, HD:HD + 1], 1.0,
                        waits=[last_av])

            for qt in range(NQT):
                sl = slice(qt * TS, (qt + 1) * TS)

                def proj(wname, bias, nout, kchunks, src_cT, rdy, evac):
                    toks = []
                    for m in range(nout):
                        ppi = next_pp()
                        mm = None
                        for ci in range(kchunks):
                            w0 = ([rdy, pp_free[ppi], wl_all, s2_free2[0],
                                   s2_free2[1]] if ci == 0 else [])
                            mm = p.op("tensor", "matmul", pp[ppi],
                                      t[wname][:, ci, m * 128:(m + 1) * 128],
                                      t[src_cT][:, ci],
                                      start=(ci == 0), stop=(ci == kchunks - 1),
                                      waits=w0)
                        dtk = evac(m, ppi, mm)
                        pp_free[ppi] = dtk
                        toks.append(dtk)
                    return toks

                cq_toks = proj("wqb", "bqp", 2, 8, "xT", xT_rdy,
                               lambda m, ppi, mm: p.op(
                                   "scalar", "activation", t["cqT"][:, m], pp[ppi],
                                   AF.Gelu, bias=t["bqp"][:, m:m + 1],
                                   waits=[mm] + slice_tok[-1:]))
                ckv_toks = proj("wkvb", "bkvp", 2, 8, "xT", xT_rdy,
                                lambda m, ppi, mm: p.op(
                                    "scalar", "activation", t["ckvT"][:, m], pp[ppi],
                                    AF.Gelu, bias=t["bkvp"][:, m:m + 1],
                                    waits=[mm] + slice_tok[-1:]))
                qc_toks = proj("wqcb", "bqcp", 7, 2, "cqT", cq_toks[-1],
                               lambda m, ppi, mm: p.op(
                                   "scalar", "activation", t["qTs"][:, m], pp[ppi],
                                   AF.Gelu, bias=t["bqcp"][:, m:m + 1], waits=[mm]))
                kc_toks = proj("wkcb", "bkcp", 7, 2, "ckvT", ckv_toks[-1],
                               lambda m, ppi, mm: p.op(
                                   "scalar", "activation", t["kT"][:, m, sl], pp[ppi],
                                   AF.Gelu, bias=t["bkcp"][:, m:m + 1], waits=[mm]))
                tbl = []
                for nm in ["chh", "shh"]:
                    tbl.append(p.dma("gpsimd", t[nm][:], dr[nm][:, sl], "drt",
                                     waits=slice_tok[-1:]))
                tbl_all = p.total("drt")

                rp_toks = []
                for which, wname, bname, dstT, dsl in [
                        ("q", "wqrb", "bqrp", "qTs", None),
                        ("k", "wkrb", "bkrp", "kT", sl)]:
                    qr_tok = proj(wname, bname, 1, 8, "xT", xT_rdy,
                                  lambda m, ppi, mm: p.op(
                                      "vector", "tensor_scalar", t["qr"][:], pp[ppi],
                                      t[bname][:, 0:1], None, ALU.add,
                                      waits=[mm] + rp_toks[-1:] + slice_tok[-1:]))[0]
                    w1 = p.dma("gpsimd", t["qrw"][0:64, :], t["qr"][64:128, :], "dsw",
                               waits=[qr_tok] + rp_toks[-1:])
                    w2 = p.dma("gpsimd", t["qrw"][64:128, :], t["qr"][0:64, :], "dsw",
                               waits=[qr_tok])
                    sw_all = p.total("dsw")
                    a = p.op("vector", "tensor_tensor", RopA, t["qr"][:],
                             t["chh"][:], ALU.mult, waits=[sw_all, tbl_all])
                    b = p.op("vector", "tensor_tensor", RopB, t["qrw"][:],
                             t["shh"][:], ALU.mult, waits=[a])
                    g = p.op("vector", "tensor_tensor", RopA, RopA,
                             RopB, ALU.add, waits=[b])
                    dst = t[dstT][:, 7] if dsl is None else t[dstT][:, 7, dsl]
                    rp_toks.append(p.op("scalar", "activation", dst, RopA,
                                        AF.Gelu, waits=[g]))

                v_toks = []
                for tt_ in range(4):
                    nt = qt * 4 + tt_
                    for dvh in range(2):
                        ppi = next_pp()
                        mm = None
                        for ci in range(2):
                            mm = p.op("tensor", "matmul", pp[ppi],
                                      t["ckvT"][:, ci, tt_ * 128:(tt_ + 1) * 128],
                                      t["wvb"][:, ci, dvh * 512:(dvh + 1) * 512],
                                      start=(ci == 0), stop=False,
                                      waits=([ckv_toks[-1], pp_free[ppi], tk_ones,
                                              s2_free2[0], s2_free2[1]] if ci == 0 else []))
                        mm = p.op("tensor", "matmul", pp[ppi],
                                  t["onesF"][0:1, 0:128],
                                  t["bvrb"][0:1, dvh * 512:(dvh + 1) * 512],
                                  start=False, stop=True)
                        ev = p.op("vector", "tensor_copy",
                                  t["V"][:, nt, dvh * 8:(dvh + 1) * 8, 0:HD],
                                  pp[ppi].rearrange("p (h d) -> p h d", d=64),
                                  waits=[mm, vset])
                        pp_free[ppi] = ev
                        v_toks.append(ev)
                slice_rdy = [qc_toks[-1], kc_toks[-1], rp_toks[0], rp_toks[1],
                             v_toks[-1], p.total("sv")]
                slice_tok = slice_rdy
                do_tap("P%d_%d" % (s_i, qt))

                if qt + 1 < NQT:
                    xT_rdy = emit_xload(s_i, qt + 1)
                elif s_i + 1 < NSEQ:
                    xT_rdy = emit_xload(s_i + 1, 0)
                emit_prep(2)
                kmax = 4 * (qt + 1)
                attn_cons = []

                def emit_qk(ph, kt):
                    buf = kt % 2
                    S2b = S2[:, buf * 1024:(buf + 1) * 1024]
                    ksl = slice(kt * KT, (kt + 1) * KT)
                    w0 = [s2_free2[buf]] + (slice_rdy
                                            if (ph == 0 and kt < 2) else [])
                    p.op("tensor", "matmul", S2b[:, 0:QT],
                         t["kT"][0:64, ph, ksl], t["qTs"][0:64, ph],
                         start=True, stop=True, tile_position=(0, 0),
                         waits=w0)
                    return p.op("tensor", "matmul", S2b[:, QT:2 * QT],
                                t["kT"][64:128, ph, ksl], t["qTs"][64:128, ph],
                                start=True, stop=True, tile_position=(64, 0))

                pend_qk = None
                for ph in range(8):
                    av_tok = None
                    qk_tok = {0: pend_qk} if pend_qk is not None \
                        else {0: emit_qk(ph, 0)}
                    pend_qk = None
                    for kt in range(kmax):
                        buf = kt % 2
                        ptb = kt % 2
                        PT = [t["PT0"], t["PT1"]][ptb]
                        exp_tok = p.op("scalar", "activation", PT[:],
                                       S2[:, buf * 1024:(buf + 1) * 1024],
                                       AF.Exp, scale=0.125,
                                       waits=[qk_tok[kt], pt_free[ptb]])
                        s2_free2[buf] = exp_tok
                        if kt + 1 < kmax:
                            qk_tok[kt + 1] = emit_qk(ph, kt + 1)
                        elif ph + 1 < 8:
                            pend_qk = emit_qk(ph + 1, 0)
                        mask_toks = []
                        r = kt - 4 * qt
                        if 0 <= r <= 3:
                            for hh in range(2):
                                mask_toks.append(p.op(
                                    "vector", "tensor_tensor",
                                    PT[:, hh * QT:(hh + 1) * QT],
                                    PT[:, hh * QT:(hh + 1) * QT],
                                    t["masks"][:, r, :], ALU.mult,
                                    waits=[exp_tok, wl_all]))
                        first, last = (kt == 0), (kt == kmax - 1)
                        wv_ = [exp_tok] + mask_toks
                        if first:
                            wv_ += [oa_free, ob_free]
                        p.op("tensor", "matmul", OA, t["V"][:, kt, 2 * ph, :],
                             PT[:, 0:QT],
                             start=first, stop=last, waits=wv_)
                        av_tok = p.op("tensor", "matmul", OB,
                                      t["V"][:, kt, 2 * ph + 1, :],
                                      PT[:, QT:2 * QT],
                                      start=first, stop=last)
                        pt_free[ptb] = av_tok
                        last_av = av_tok
                    rbsb = t["rbs"] if ph % 2 == 0 else t["rbs2"]
                    rAB = p.op("vector", "reciprocal", t["rcpA"][64:65, :],
                               t["PS_B"][64:65, 0:1024], waits=[av_tok], lp=True)
                    bA = p.op("tensor", "matmul", RB, t["onesF"][64:65, 0:64],
                              t["rcpA"][64:65, 0:QT], start=True, stop=True,
                              tile_position=(64, 0), waits=[rAB, rbc_free])
                    bB = p.op("tensor", "matmul", RB2, t["onesF"][64:65, 0:64],
                              t["rcpA"][64:65, QT:2 * QT], start=True, stop=True,
                              tile_position=(64, 0))
                    cpA = p.op("vector", "tensor_copy", rbsb[:, 0:QT], RB,
                               waits=[bA, rbs_free[ph % 2]], lp=True)
                    cpB = p.op("vector", "tensor_copy", rbsb[:, QT:2 * QT], RB2,
                               waits=[bB], lp=True)
                    rbc_free = cpB
                    cA = p.op("vector", "scalar_tensor_tensor",
                              t["attnT"][0:64, ph, :], OA[0:64, :], 1.0,
                              rbsb[:, 0:QT], ALU.mult, ALU.mult,
                              waits=[cpA])
                    oa_free = cA
                    wB = p.op("vector", "scalar_tensor_tensor",
                              t["attnT"][64:128, ph, :],
                              OB[0:64, :], 1.0, rbsb[:, QT:2 * QT],
                              ALU.mult, ALU.mult, waits=[cpB])
                    ob_free = wB
                    rbs_free[ph % 2] = wB
                    attn_cons = [cA, wB]
                    emit_prep(1)
                do_tap("A%d_%d" % (s_i, qt))

                for tt_ in range(4):
                    gt = qt * 4 + tt_
                    tsl = slice(gt * TT, (gt + 1) * TT)
                    lsl = slice(tt_ * 128, (tt_ + 1) * 128)
                    ldr = p.dma("gpsimd", t["xr"][:], xs[tsl, :], "dre",
                                waits=[xr_free])
                    ev_toks = []
                    for eh in range(2):
                        esl = slice(eh * 512, (eh + 1) * 512)
                        mm = None
                        for c in range(8):
                            w0 = ([wo_free[eh], rbc_free] + attn_cons
                                  if c == 0 else [])
                            mm = p.op("tensor", "matmul", wop[eh],
                                      t["attnT"][:, c, lsl], t["wob"][:, c, esl],
                                      start=(c == 0), stop=False, waits=w0)
                        mm = p.op("tensor", "matmul", wop[eh],
                                  t["onesF"][0:1, 0:128], t["borb"][0:1, esl],
                                  start=False, stop=True)
                        ev = p.op("vector", "tensor_tensor", t["rres"][:, esl],
                                  wop[eh], t["xr"][:, esl], ALU.add,
                                  waits=[mm, p.total("dre"), ln_last])
                        wo_free[eh] = ev
                        ev_toks.append(ev)
                    xr_free = ev_toks[-1]
                    rbc_free = ev_toks[0]
                    sm = p.op("vector", "tensor_reduce", t["s_sum"][:], t["rres"][:],
                              axis=AX.X, op=ALU.add, waits=ev_toks)
                    sqa = p.op("scalar", "activation", t["qrw"][:], t["rres"][:, 0:512],
                               AF.Square, accum_out=t["s_sq"][:], waits=ev_toks)
                    sq = p.op("scalar", "activation", t["qrw"][:], t["rres"][:, 512:1024],
                              AF.Square, accum_out=t["s_sq2"][:], waits=[sqa])
                    mn = p.op("vector", "tensor_scalar", t["s_mean"][:], t["s_sum"][:],
                              1.0 / E, None, ALU.mult, waits=[sm])
                    s12 = p.op("vector", "tensor_tensor", t["s_sq"][:], t["s_sq"][:],
                               t["s_sq2"][:], ALU.add, waits=[sq])
                    v1 = p.op("vector", "tensor_scalar", t["s_var"][:], t["s_sq"][:],
                              1.0 / E, None, ALU.mult, waits=[s12])
                    m2 = p.op("vector", "tensor_tensor", t["s_sum"][:], t["s_mean"][:],
                              t["s_mean"][:], ALU.mult, waits=[mn])
                    v2 = p.op("vector", "tensor_tensor", t["s_var"][:], t["s_var"][:],
                              t["s_sum"][:], ALU.subtract, waits=[v1, m2])
                    lg = p.op("scalar", "activation", t["s_rstd"][:], t["s_var"][:],
                              AF.Ln, bias=t["epst"][:, 0:1], waits=[v2, tk_eps])
                    ex = p.op("scalar", "activation", t["s_rstd"][:], t["s_rstd"][:],
                              AF.Exp, scale=-0.5, waits=[lg])
                    ln1t = t["ln1t0"]
                    no = p.op("vector", "tensor_scalar", ln1t[:], t["rres"][:],
                              t["s_mean"][:, 0:1], t["s_rstd"][:, 0:1],
                              ALU.subtract, ALU.mult,
                              waits=[ex, ln1_free[0]])
                    ln_last = no
                    st_ = p.dma("gpsimd", dr["ln1s"][s_i][tsl, :], ln1t[:],
                                "dl10", waits=[no])
                    ln1_free[0] = st_
                slice_tok = [wo_free[0], wo_free[1], ln_last]
                do_tap("W%d_%d" % (s_i, qt))

        emit_prep(len(prep_items))
        p.barrier(dma_drain=["dwp0", "dwp1", "dwpt", "dwps", "dl10", "dl11",
                             "dre", "dxs", "dxt", "drt", "dsw", "dswg"])

        for nm, src in [("wgT", "wgs"), ("wuT", "wus"), ("wdT", "wds")]:
            p.dma("gpsimd", t[nm][:], dr[src].rearrange("c p m -> p c m"), "dwft")
        wft_all = p.total("dwft")
        T_H1 = {tl: t["h1ab"].rearrange("p (c t f) -> p c t f", t=2, f=128)
                [:, :, tl, :] for tl in range(NT)}
        xq_scr = {"abs": "qabs", "pa": "qpa", "pb": "qpb", "sgn": "qsgn",
                  "bm": "qbm", "sc": "qsc", "rc": "qrc"}
        tk_eps2 = p.op("gpsimd", "memset", t["epst"][:], 1e-5)

        gu_free = {"g0": ("sv", 0), "u0": ("sv", 0), "g1": ("sv", 0), "u1": ("sv", 0)}
        self_hsg = {0: ("sv", 0), 1: ("sv", 0)}
        prev_q_last = [("sg", 0)]
        dn_free = {0: ("ss", 0), 1: ("ss", 0)}
        out_free = {0: ("dou0", 0), 1: ("dou1", 0)}
        no_prev = {0: ("sv", 0), 1: ("sv", 0)}
        hq2_free = ("dhq", 0)
        xqT_free = ("pe", 0)
        hqT_free = ("pe", 0)
        R2 = {0: t["r2"], 1: t["r2b"]}
        T_LN = {0: t["ln1L0"], 1: t["ln1L1"]}
        xq_state = {"free": {0: ("sv", 0), 1: ("sv", 0)},
                    "xq2_free": ("dxq", 0), "ld": {}, "q_toks": []}

        def emit_xq(s_i2, g2):
            for tl in range(NT):
                gt = g2 * NT + tl
                tsl = slice(gt * TT, (gt + 1) * TT)
                lsem = "dlf0" if tl == 0 else "dlf1"
                p.dma("gpsimd", T_LN[tl][:], dr["ln1s"][s_i2][tsl, :], lsem,
                      waits=[xq_state["free"][tl]])
                xq_state["ld"][tl] = (lsem, p.cnt[lsem])
            q_toks = []
            for tl in range(NT):
                q_toks.append(quant(T_LN[tl][:], E, t["xq2"][:, :, tl, :], xq_scr,
                                    [xq_state["ld"][tl]], "sg",
                                    out_waits=[xq_state["xq2_free"]]))
            xq_state["q_toks"] = q_toks

        def emit_xqT(xqT_free_):
            for c in range(8):
                p.dma("scalar", t["xqT"][:, c].rearrange("p (j f) -> p j f", f=128),
                      t["xq2"][:, c], "dxq",
                      waits=xq_state["q_toks"] + [xqT_free_], transpose=True)
            xq_state["xq2_free"] = p.total("dxq")
            return p.total("dxq")

        glist = [(si2, gg) for si2 in range(NSEQ) for gg in range(NTT // NT)]
        emit_xq(*glist[0])
        xqT_rdy = emit_xqT(xqT_free)
        for gidx, (s_i, g_) in enumerate(glist):
            if True:
                cv_tok = {}
                for tl in range(NT):
                    cv_tok[tl] = p.op("gpsimd", "tensor_copy", R2[tl][:],
                                      T_LN[tl][:],
                                      waits=[out_free[tl], xq_state["ld"][tl]])
                    xq_state["free"][tl] = cv_tok[tl]
                nsteps = len(ITS)
                h2_tok = {}
                q_tok = {}
                tr_rdy = {}
                dn_last = {0: None, 1: None}

                def emit_gu(it):
                    i0, iw = ITS[it]
                    res = {}
                    for tl in range(NT):
                        pr = (it * NT + tl) % 2
                        gps = gps2[pr]
                        ups = ups2[pr]
                        mmg = None
                        for c in range(8):
                            mmg = p.op("tensor", "matmul", gps[:, 0:iw],
                                       t["xqT"][:, c, tl * 128:(tl + 1) * 128],
                                       t["wgT"][:, c, i0:i0 + iw],
                                       start=(c == 0), stop=(c == 7),
                                       waits=([xqT_rdy, gu_free["g%d" % pr],
                                               wft_all, rbc_free] if c == 0 else []))
                        mmu = None
                        for c in range(8):
                            mmu = p.op("tensor", "matmul", ups[:, 0:iw],
                                       t["xqT"][:, c, tl * 128:(tl + 1) * 128],
                                       t["wuT"][:, c, i0:i0 + iw],
                                       start=(c == 0), stop=(c == 7),
                                       waits=([gu_free["u%d" % pr]] if c == 0 else []))
                        res[tl] = (mmg, mmu, pr)
                    return res

                def emit_h(it, gu, xfree):
                    i0, iw = ITS[it]
                    toks = {}
                    for tl in range(NT):
                        mmg, mmu, pr = gu[tl]
                        gps = gps2[pr]
                        ups = ups2[pr]
                        hsg_ = t["hsg"] if tl == 0 else t["hsgB"]
                        hv = T_H1[tl][:, 0:iw // 128, :]
                        tg = p.op("scalar", "activation", hsg_[:, 0:iw],
                                  gps[:, 0:iw], AF.Tanh, scale=0.5,
                                  waits=[mmg, self_hsg[tl]])
                        h1 = p.op("vector", "scalar_tensor_tensor",
                                  hv,
                                  hsg_[:, 0:iw].rearrange(
                                      "p (c f) -> p c f", f=128),
                                  1.0,
                                  ups[:, 0:iw].rearrange(
                                      "p (c f) -> p c f", f=128),
                                  ALU.add, ALU.mult,
                                  waits=[tg, mmu] + (xfree if tl == 0 else []))
                        gu_free["u%d" % pr] = h1
                        h2 = p.op("vector", "scalar_tensor_tensor",
                                  hv, hv, 0.5,
                                  gps[:, 0:iw].rearrange(
                                      "p (c f) -> p c f", f=128),
                                  ALU.mult, ALU.mult, waits=[h1])
                        gu_free["g%d" % pr] = h2
                        self_hsg[tl] = h1
                        toks[tl] = h2
                    return toks

                def emit_quant(it, mid=None):
                    i0, iw = ITS[it]
                    c0 = i0 // 128
                    out_ap = t["hq2"][:, c0:c0 + iw // 128, :, :] \
                        .rearrange("p c t f -> p (c t) f")
                    tok = quant(t["h1ab"][:, 0:2 * iw], 2 * iw, out_ap,
                                xq_scr,
                                [h2_tok[it][0], h2_tok[it][1]], "sg",
                                out_waits=[hq2_free], mid=mid)
                    return [tok]

                def emit_tr(it):
                    i0, iw = ITS[it]
                    c0 = i0 // 128
                    dsem = ["dhq", "dhq1", "dhq2", "dhq3", "dhq4", "dhq5"][it]
                    dq = "sync" if it % 2 == 0 else "scalar"
                    for c in range(c0, c0 + iw // 128):
                        p.dma(dq,
                              t["hqT"][:, c].rearrange("p (j f) -> p j f", f=128),
                              t["hq2"][:, c], dsem,
                              waits=q_tok[it] + [hqT_free], transpose=True)
                    tr_rdy[it] = p.total(dsem)

                gu = emit_gu(0)
                gu_last = gu
                h2_tok[0] = emit_h(0, gu, [prev_q_last[0]])
                for it in range(nsteps):
                    gu_next = emit_gu(it + 1) if it + 1 < nsteps else None
                    if gu_next is not None:
                        gu_last = gu_next

                    def mid_fn(xfree, it=it, gu_next=gu_next):
                        if gu_next is not None:
                            h2_tok[it + 1] = emit_h(it + 1, gu_next, xfree)
                    q_tok[it] = emit_quant(it, mid=mid_fn)
                    emit_tr(it)
                prev_q_last[0] = q_tok[nsteps - 1][0]
                if gidx + 1 < len(glist):
                    emit_xq(*glist[gidx + 1])
                for it in range(nsteps):
                    i0, iw = ITS[it]
                    c0 = i0 // 128
                    for tl in range(NT):
                        for eh in range(2):
                            esl = slice(eh * 512, (eh + 1) * 512)
                            mm = None
                            for c in range(c0, c0 + iw // 128):
                                w0 = []
                                if c == c0:
                                    w0 = [tr_rdy[it]]
                                    if it == 0 and eh == 0:
                                        w0 += [dn_free[tl], s2_free]
                                mm = p.op("tensor", "matmul",
                                          t["PS_A"][:, tl * 1024 + esl.start:
                                                     tl * 1024 + esl.stop],
                                          t["hqT"][:, c, tl * 128:(tl + 1) * 128],
                                          t["wdT"][:, c, esl],
                                          start=(it == 0 and c == c0),
                                          stop=(it == nsteps - 1 and
                                                c == c0 + iw // 128 - 1),
                                          waits=w0)
                            if it == nsteps - 1:
                                dn_last[tl] = mm
                xqT_free = gu_last[NT - 1][1]
                if gidx + 1 < len(glist):
                    xqT_rdy = emit_xqT(xqT_free)
                hq2_free = dn_last[NT - 1]
                dn_toks = dn_last
                s2_free = ("ss", 0)
                hqT_free = dn_last[NT - 1]
                sq_toks = {}
                for tl in range(NT):
                    gt = g_ * NT + tl
                    tsl = slice(gt * TT, (gt + 1) * TT)
                    dnv = t["PS_A"][:, tl * 1024:(tl + 1) * 1024]
                    r2v = R2[tl]
                    r2t = []
                    for eh in range(2):
                        esl = slice(eh * 512, (eh + 1) * 512)
                        r2t.append(p.op("vector", "tensor_tensor", r2v[:, esl],
                                        dnv[:, esl], r2v[:, esl], ALU.add,
                                        waits=[dn_toks[tl], cv_tok[tl]]))
                    sm = p.op("vector", "tensor_reduce", t["f_sum"][:], r2v[:],
                              axis=AX.X, op=ALU.add, waits=r2t)
                    sqa = p.op("scalar", "activation", t["hsg"][:], r2v[:, 0:512],
                               AF.Square, accum_out=t["f_sq"][:], waits=r2t)
                    sq = p.op("scalar", "activation", t["hsg"][:], r2v[:, 512:1024],
                              AF.Square, accum_out=t["f_sq2"][:], waits=[sqa])
                    dn_free[tl] = sq
                    mn = p.op("vector", "tensor_scalar", t["f_mean"][:], t["f_sum"][:],
                              1.0 / E, None, ALU.mult, waits=[sm])
                    s12 = p.op("vector", "tensor_tensor", t["f_sq"][:], t["f_sq"][:],
                               t["f_sq2"][:], ALU.add, waits=[sq])
                    v1 = p.op("vector", "tensor_scalar", t["f_var"][:], t["f_sq"][:],
                              1.0 / E, None, ALU.mult, waits=[s12])
                    m2 = p.op("vector", "tensor_tensor", t["f_sum"][:], t["f_mean"][:],
                              t["f_mean"][:], ALU.mult, waits=[mn])
                    v2 = p.op("vector", "tensor_tensor", t["f_var"][:], t["f_var"][:],
                              t["f_sum"][:], ALU.subtract, waits=[v1, m2])
                    lg = p.op("scalar", "activation", t["f_rstd"][:], t["f_var"][:],
                              AF.Ln, bias=t["epst"][:, 0:1], waits=[v2, tk_eps2])
                    ex = p.op("scalar", "activation", t["f_rstd"][:], t["f_rstd"][:],
                              AF.Exp, scale=-0.5, waits=[lg])
                    no = p.op("vector", "tensor_scalar", r2v[:], r2v[:],
                              t["f_mean"][:, 0:1], t["f_rstd"][:, 0:1],
                              ALU.subtract, ALU.mult, waits=[ex])
                    osem = "dou0" if tl == 0 else "dou1"
                    p.dma("sync", dr["out"][s_i][tsl, :], r2v[:], osem, waits=[no])
                    out_free[tl] = p.total(osem)
                s2_free = dn_free[NT - 1]
                do_tap("F%d_%d" % (s_i, g_))
        p.wait("sync", [p.total("dou0"), p.total("dou1")])

    with nc.Block() as block:
        @block.sync
        def _(h):
            p.start("sync", h)
            emit_program()

        @block.gpsimd
        def _(h):
            p.start("gpsimd", h)
            emit_program()

        @block.tensor
        def _(h):
            p.start("tensor", h)
            emit_program()

        @block.scalar
        def _(h):
            p.start("scalar", h)
            emit_program()

        @block.vector
        def _(h):
            p.start("vector", h)
            emit_program()

    return nc, dr


def host_inputs(inputs, seq_pair):
    g = lambda k: np.ascontiguousarray(np.asarray(inputs[k]))
    im = {}
    im["x"] = g("L0_attn_x")[2 * seq_pair:2 * seq_pair + 2]
    im["wq"] = g("L0_attn_wq"); im["wqc"] = g("L0_attn_wqc"); im["wqr"] = g("L0_attn_wqr")
    im["wkv"] = g("L0_attn_wkv"); im["wkr"] = g("L0_attn_wkr"); im["wkc"] = g("L0_attn_wkc")
    im["wv"] = g("L0_attn_wv"); im["wo"] = g("L0_attn_wo")
    im["bqp"] = g("L0_attn_bq").reshape(2, 128).T.copy()
    im["bqcp"] = g("L0_attn_bqc").reshape(7, 128).T.copy()
    im["bqrp"] = g("L0_attn_bqr").reshape(1, 128).T.copy()
    im["bkvp"] = g("L0_attn_bkv").reshape(2, 128).T.copy()
    im["bkrp"] = g("L0_attn_bkr").reshape(1, 128).T.copy()
    im["bkcp"] = g("L0_attn_bkc").reshape(7, 128).T.copy()
    im["bvr"] = g("L0_attn_bv").reshape(1, E)
    im["bor"] = g("L0_attn_bo").reshape(1, E)
    im["wg"] = g("L0_ffn_w_gate"); im["wu"] = g("L0_ffn_w_up"); im["wdn"] = g("L0_ffn_w_down")
    import jax
    import jax.numpy as jnp
    with jax.default_device(jax.devices("cpu")[0]):
        inv_freq = 1.0 / 10000.0 ** (jnp.arange(0, DR, 2, dtype=jnp.float32) / DR)
        tpos = jnp.arange(S, dtype=jnp.float32)
        freqs = tpos[:, None] * inv_freq[None, :]
        cos32 = np.asarray(jnp.cos(freqs)).T
        sin32 = np.asarray(jnp.sin(freqs)).T
    im["chh"] = np.concatenate([cos32, cos32], 0).astype(np.float32)
    im["shh"] = np.concatenate([-sin32, sin32], 0).astype(np.float32)
    masks = np.zeros((128, 4, QT), dtype=ml_dtypes.bfloat16)
    qf = np.arange(QT)
    for r in range(4):
        for kp in range(128):
            masks[kp, r, :] = (qf >= kp + 128 * r).astype(ml_dtypes.bfloat16)
    im["masks"] = masks
    return {"i_" + k: v for k, v in im.items()}


_PROG_CACHE = {}


def kernel(**inputs):
    if "prog" not in _PROG_CACHE:
        _PROG_CACHE["prog"] = build_program()
    nc, dr = _PROG_CACHE["prog"]
    in_maps = [host_inputs(inputs, c) for c in range(NCORE)]
    res = run_bass_kernel_spmd(nc, in_maps, core_ids=list(range(NCORE)))
    return np.concatenate([res.results[c]["out"] for c in range(NCORE)], axis=0)


# version 29
# speedup vs baseline: 1.0330x; 1.0083x over previous
import math
from contextlib import ExitStack

import numpy as np
import ml_dtypes

import concourse.bass as bass
import concourse.mybir as mybir
from concourse.bass_utils import run_bass_kernel_spmd

F32 = mybir.dt.float32
BF16 = mybir.dt.bfloat16
FP16 = mybir.dt.float16
AF = mybir.ActivationFunctionType
ALU = mybir.AluOpType
AX = mybir.AxisListType

S = 2048
E = 1024
DC = 256
DR = 128
H = 16
HD = 64
I = 2816
NSEQ = 2
NCORE = 8
TS = 512
NSL = S // TS
TT = 128
NTT = S // TT
QT = 512
NQT = NSL
KT = 128
NKT = S // KT
NT = 2
ITS = [(k * 512, min(512, I - k * 512)) for k in range(6)]
RCP6 = float(np.float32(1.0) / np.float32(6.0))
C6H = float(np.float32(1.0 / 6.0))
C6L = float(np.float64(1.0) / 6.0 - np.float64(np.float32(1.0 / 6.0)))
ENG = {"pe": "tensor", "sv": "vector", "ss": "scalar", "sg": "gpsimd", "sy": "sync"}


class Prog:
    def __init__(self, nc, sems):
        self.nc = nc
        self.sems = sems
        self.which = None
        self.h = None
        self.cnt = None

    def start(self, which, handle):
        self.which = which
        self.h = handle
        self.cnt = {k: 0 for k in self.sems}

    def op(self, engine, method, *args, waits=(), sem=None, lp=False, **kw):
        sem = sem or {"tensor": "pe", "vector": "sv", "scalar": "ss",
                      "gpsimd": "sg", "sync": "sy"}[engine]
        self.cnt[sem] += 1
        val = self.cnt[sem]
        if self.which == engine:
            waits = [(s, v) for (s, v) in waits if v > 0]
            if len(waits) > 1:
                for s, v in waits[:-1]:
                    self.h.wait_ge(self.sems[s], v)
            if lp:
                with self.nc.allow_low_precision(reason="bf16 softmax denominators (error budgeted)"):
                    inst = getattr(self.h, method)(*args, **kw)
            else:
                inst = getattr(self.h, method)(*args, **kw)
            if waits:
                inst._wait_ge(self.sems[waits[-1][0]], waits[-1][1])
            inst.then_inc(self.sems[sem], 1)
        return (sem, val)

    def dma(self, queue, out, in_, sem, waits=(), transpose=False):
        self.cnt[sem] += 16
        val = self.cnt[sem]
        if self.which == queue:
            waits = [(s, v) for (s, v) in waits if v > 0]
            if len(waits) > 1:
                for s, v in waits[:-1]:
                    self.h.wait_ge(self.sems[s], v)
            inst = self.h.dma_start(out=out, in_=in_, transpose=transpose)
            if waits:
                inst._wait_ge(self.sems[waits[-1][0]], waits[-1][1])
            inst.then_inc(self.sems[sem], 16)
        return (sem, val)

    def total(self, sem):
        return (sem, self.cnt[sem])

    def wait(self, engine, waits):
        if self.which == engine:
            for s, v in waits:
                if v > 0:
                    self.h.wait_ge(self.sems[s], v)

    def barrier(self, dma_drain=()):
        drains = [(s, self.cnt[s]) for s in dma_drain if self.cnt[s] > 0]
        vals = {}
        for e in ["pe", "sv", "ss", "sg", "sy"]:
            self.cnt[e] += 1
            vals[e] = self.cnt[e]
        for e, eng in ENG.items():
            if self.which == eng:
                for s, v in drains:
                    self.h.wait_ge(self.sems[s], v)
                self.h.nop().then_inc(self.sems[e], 1)
                for o, ov in vals.items():
                    if o != e:
                        self.h.wait_ge(self.sems[o], ov)


def build_program(tap=None):
    nc = bass.Bass("TRN2", target_bir_lowering=False)
    dr = {}

    def din(nm, shp, dt=F32):
        dr[nm] = nc.dram_tensor("i_" + nm, shp, dt, kind="ExternalInput").ap()

    din("x", [NSEQ, S, E])
    din("wq", [E, DC]); din("wqc", [DC, E - DR]); din("wqr", [E, DR])
    din("wkv", [E, DC]); din("wkr", [E, DR]); din("wkc", [DC, E - DR])
    din("wv", [DC, E]); din("wo", [E, E])
    din("bqp", [128, 2]); din("bqcp", [128, 7]); din("bqrp", [128, 1])
    din("bkvp", [128, 2]); din("bkrp", [128, 1]); din("bkcp", [128, 7])
    din("bvr", [1, E]); din("bor", [1, E])
    din("wg", [I, E]); din("wu", [I, E]); din("wdn", [E, I])
    din("chh", [128, S]); din("shh", [128, S])
    din("masks", [128, 4, QT], BF16)
    dr["out"] = nc.dram_tensor("out", [NSEQ, S, E], F32, kind="ExternalOutput").ap()
    dr["ln1s"] = nc.dram_tensor("ln1s", [NSEQ, S, E], BF16, kind="Internal").ap()
    dr["wgs"] = nc.dram_tensor("wgs", [8, 128, I], BF16, kind="Internal").ap()
    dr["wus"] = nc.dram_tensor("wus", [8, 128, I], BF16, kind="Internal").ap()
    dr["wds"] = nc.dram_tensor("wds", [22, 128, E], BF16, kind="Internal").ap()
    if tap is not None:
        dr["tapout"] = nc.dram_tensor("tapout", tap["shape"], tap.get("dtype", F32),
                                      kind="ExternalOutput").ap()

    T = {}
    root = ExitStack()

    def alloc(stk, name, shape, dtype, psum=False):
        fn = nc.psum_tensor if psum else nc.sbuf_tensor
        T[name] = stk.enter_context(fn(name, shape, dtype))
        return T[name]

    alloc(root, "onesF", [128, 128], BF16)
    alloc(root, "epst", [128, 1], F32)
    alloc(root, "PS_A", [128, 2048], F32, psum=True)
    alloc(root, "PS_B", [128, 1024], F32, psum=True)
    alloc(root, "PS_C", [128, 512], F32, psum=True)
    alloc(root, "PS_D", [128, 512], F32, psum=True)

    att = ExitStack()
    for nm, shp, dt in [
        ("masks", [128, 4, QT], BF16),
        ("wqb", [128, 8, DC], BF16), ("wqcb", [128, 2, E - DR], BF16),
        ("wqrb", [128, 8, DR], BF16), ("wkvb", [128, 8, DC], BF16),
        ("wkrb", [128, 8, DR], BF16), ("wkcb", [128, 2, E - DR], BF16),
        ("wvb", [128, 2, E], BF16), ("wob", [128, 8, E], BF16),
        ("bqp", [128, 2], F32), ("bqcp", [128, 7], F32), ("bqrp", [128, 1], F32),
        ("bkvp", [128, 2], F32), ("bkrp", [128, 1], F32), ("bkcp", [128, 7], F32),
        ("bvrb", [1, E], BF16), ("borb", [1, E], BF16),
        ("kT", [128, 8, S], BF16), ("V", [128, NKT, H, HD + 1], BF16),
        ("qTs", [128, 8, TS], BF16), ("attnT", [128, 8, TS], BF16),
        ("xst", [128, 8, 2, 128], BF16), ("xT", [128, 8, TS], BF16),
        ("cqT", [128, 2, TS], BF16), ("ckvT", [128, 2, TS], BF16),
        ("qr", [128, TS], F32), ("qrw", [128, TS], F32),
        ("chh", [128, TS], F32), ("shh", [128, TS], F32),
        ("PT0", [128, 2 * QT], BF16), ("PT1", [128, 2 * QT], BF16),
        ("rcpA", [128, 2 * QT], BF16),
        ("rbs", [64, 2 * QT], F32), ("rbs2", [64, 2 * QT], F32),
        ("xr", [128, E], F32), ("rres", [128, E], F32),
        ("s_sum", [128, 1], F32), ("s_sq", [128, 1], F32),
        ("s_sq2", [128, 1], F32),
        ("s_mean", [128, 1], F32), ("s_var", [128, 1], F32),
        ("s_rstd", [128, 1], F32),
        ("ln1t0", [128, E], BF16),
        ("wpin0", [128, E], F32), ("wpin1", [128, E], F32),
        ("wpabs", [128, E], F32), ("wppa", [128, E], F32), ("wppb", [128, E], F32),
        ("wpsgn", [128, E], F32),
        ("wpbm", [128, E // 16], F32), ("wpsc", [128, E // 16], F32),
        ("wprc", [128, E // 16], F32),
        ("wstg", [128, 8, 2, 128], BF16), ("wstgT", [128, 2, 128], BF16),
        ("wstgD", [128, 8, 128], BF16),
    ]:
        alloc(att, nm, shp, dt)
    att.close()

    ffn = ExitStack()
    for nm, shp, dt in [
        ("wgT", [128, 8, I], BF16), ("wuT", [128, 8, I], BF16),
        ("wdT", [128, 22, E], BF16),
        ("ln1L0", [128, E], BF16), ("ln1L1", [128, E], BF16),
        ("qabs", [128, E], F32), ("qpa", [128, E], F32), ("qpb", [128, E], F32),
        ("qsgn", [128, E], F32),
        ("qbm", [128, E // 16], F32), ("qsc", [128, E // 16], F32),
        ("qrc", [128, E // 16], F32),
        ("xq2", [128, 8, NT, 128], BF16), ("xqT", [128, 8, NT * 128], BF16),
        ("hsg", [128, 512], F32), ("hsgB", [128, 512], F32),
        ("h1ab", [128, 1024], F32),
        ("hq2", [128, 22, NT, 128], BF16), ("hqT", [128, 22, NT * 128], BF16),
        ("r2", [128, E], F32), ("r2b", [128, E], F32),
        ("f_sum", [128, 1], F32), ("f_sq", [128, 1], F32),
        ("f_sq2", [128, 1], F32),
        ("f_mean", [128, 1], F32), ("f_var", [128, 1], F32),
        ("f_rstd", [128, 1], F32),
    ]:
        alloc(ffn, nm, shp, dt)
    ffn.close()

    sems = {}
    for nm in ["pe", "sv", "ss", "sg", "sy",
               "dxs", "dxt", "dwl", "drt", "dsw", "dre",
               "dl10", "dl11", "dwp0", "dwp1", "dwpt", "dwps",
               "dwft", "dlf0", "dlf1", "dxq", "dhq", "dhq1", "dhq2", "dhq3",
               "dhq4", "dhq5", "dou0", "dou1", "dswg"]:
        sems[nm] = root.enter_context(nc.semaphore(nm))

    p = Prog(nc, sems)
    t = T

    def emit_program():
        tap_done = [False]

        def do_tap(tag):
            if tap is not None and tap["tag"] == tag and not tap_done[0]:
                tap_done[0] = True
                tsrc = tap["tensor"]
                if "@" in tsrc:
                    base, kk = tsrc.split("@")
                    tsrc_ap = t[base][:, :, 0:int(kk)]
                elif "|" in tsrc:
                    base, kk = tsrc.split("|")
                    tsrc_ap = t[base][:, 0:int(kk)]
                else:
                    tsrc_ap = t[tsrc][:]
                p.dma("sync", dr["tapout"], tsrc_ap, "dou0",
                      waits=[p.total("sv"), p.total("ss"), p.total("sg"),
                             p.total("dswg"), p.total("dsw"), p.total("pe")])
                p.barrier(dma_drain=["dou0"])

        RopA = t["rres"][:, 0:TS]
        RopB = t["rres"][:, TS:2 * TS]
        S2 = t["PS_A"]
        pp = [t["PS_A"][:, i * 512:(i + 1) * 512] for i in range(4)]
        OA = t["PS_B"][0:65, 0:512]
        OB = t["PS_B"][0:65, 512:1024]
        RB = t["PS_C"][0:64, :]
        RB2 = t["PS_D"][0:64, :]
        wop = [t["PS_C"][:, :], t["PS_D"][:, :]]
        gps2 = [t["PS_C"][:, :], t["PS_B"][0:128, 0:512]]
        ups2 = [t["PS_D"][:, :], t["PS_B"][0:128, 512:1024]]
        dn = [t["PS_A"][:, 0:1024], t["PS_A"][:, 1024:2048]]

        for nm, src in [("wqb", "wq"), ("wqrb", "wqr"), ("wkvb", "wkv"),
                        ("wkrb", "wkr"), ("wob", "wo"),
                        ("wqcb", "wqc"), ("wkcb", "wkc"), ("wvb", "wv")]:
            p.dma("gpsimd", t[nm][:], dr[src].rearrange("(c p) m -> p c m", p=128), "dwl")
        for nm in ["bqp", "bqcp", "bqrp", "bkvp", "bkrp", "bkcp", "masks"]:
            p.dma("gpsimd", t[nm][:], dr[nm], "dwl")
        p.dma("gpsimd", t["bvrb"][:], dr["bvr"], "dwl")
        p.dma("gpsimd", t["borb"][:], dr["bor"], "dwl")
        tk_ones = p.op("gpsimd", "memset", t["onesF"][:], 1.0)
        tk_eps = p.op("gpsimd", "memset", t["epst"][:], 1e-5)
        wl_all = p.total("dwl")

        scr_free = {}

        def quant(x_ap, F, out_ap, pre, x_waits, out_sem, out_waits=(),
                  mid=None, pool_stair=False):
            nblk = F // 16
            ab = t[pre["abs"]][:, 0:F]
            pa = t[pre["pa"]][:, 0:F]
            pb = t[pre["pb"]][:, 0:F]
            sg_ = t[pre["sgn"]][:, 0:F]
            bm = t[pre["bm"]][:, 0:nblk]
            sc = t[pre["sc"]][:, 0:nblk]
            rc = t[pre["rc"]][:, 0:nblk]
            pv = scr_free.get(pre["abs"], {})
            x_waits = list(x_waits)
            a3 = ab.rearrange("p (nb b) -> p nb b", b=16)
            x3 = x_ap.rearrange("p (nb b) -> p nb b", b=16)
            tkb = p.op("vector", "tensor_reduce", bm, x3, axis=AX.X, op=ALU.max,
                       apply_absolute_value=True,
                       waits=x_waits + pv.get("bm", []))
            tlo = p.op("vector", "tensor_scalar", rc, bm, C6L, None, ALU.mult,
                       waits=[tkb] + pv.get("rc", []))
            thi = p.op("vector", "scalar_tensor_tensor", sc, bm, C6H, rc,
                       ALU.mult, ALU.add, waits=[tlo] + pv.get("sc", []))
            tkc = p.op("vector", "tensor_scalar", sc, sc, 1e-12, None, ALU.max,
                       waits=[thi])
            tkr = p.op("vector", "reciprocal", rc, sc, waits=[tkc])
            tka = p.op("scalar", "activation", ab, x_ap, AF.Abs,
                       waits=x_waits + pv.get("ab", []))
            tks = p.op("scalar", "activation", sg_, x_ap, AF.Sign,
                       waits=x_waits + pv.get("sgn", []))
            tkc2 = tkc
            scb = sc
            rb_ = rc.unsqueeze(2).broadcast_to([128, nblk, 16])
            tkn = p.op("gpsimd", "tensor_tensor",
                       pa.rearrange("p (nb b) -> p nb b", b=16), a3, rb_, ALU.mult,
                       waits=[tkr, tks, tka] + pv.get("pa", []))
            if mid is not None:
                mid([tka, tks, tkb])
            axn = pa
            if pool_stair:
                t1 = p.op("gpsimd", "tensor_scalar", pb, axn, 0.25, 0.5,
                          ALU.is_gt, ALU.mult, waits=[tkn] + pv.get("pb", []))
                steps = [(0.75, 0.5), (1.25, 0.5), (1.75, 0.5),
                         (2.5, None), (3.5, None), (5.0, 2.0)]
                for thr, w in steps:
                    if w is None:
                        c_ = p.op("gpsimd", "tensor_scalar", ab, axn, thr,
                                  None, ALU.is_gt, waits=[t1])
                    else:
                        c_ = p.op("gpsimd", "tensor_scalar", ab, axn, thr,
                                  w, ALU.is_gt, ALU.mult, waits=[t1])
                    t1 = p.op("gpsimd", "tensor_tensor", pb, pb, ab,
                              ALU.add, waits=[c_])
                tL = t1
            else:
                t1 = p.op("vector", "tensor_scalar", pb, axn, 0.25, None,
                          ALU.is_gt, waits=[tkn] + pv.get("pb", []))
                for thr in (0.75, 1.25, 1.75):
                    t1 = p.op("vector", "scalar_tensor_tensor", pb, axn, thr,
                              pb, ALU.is_gt, ALU.add, waits=[t1])
                h1_ = p.op("vector", "tensor_scalar", ab, axn, 5.0, None,
                           ALU.is_gt, waits=[tkn, tkb])
                for thr in (5.0, 3.5, 2.5):
                    h1_ = p.op("vector", "scalar_tensor_tensor", ab, axn, thr,
                               ab, ALU.is_gt, ALU.add, waits=[h1_])
                tL = p.op("vector", "scalar_tensor_tensor", pb, pb, 0.5, ab,
                          ALU.mult, ALU.add, waits=[t1, h1_])
            tLs = p.op("gpsimd", "tensor_tensor", pb, pb, sg_, ALU.mult,
                       waits=[tL, tks])
            if out_ap.ndim == 3:
                nch = out_ap.shape[1]
                o4 = out_ap.rearrange("p c (s b) -> p c s b", b=16)
                pb4 = pb.rearrange("p (c s b) -> p c s b", c=nch, b=16)
                sb_ = scb.rearrange("p (c s) -> p c s", s=8).unsqueeze(3) \
                    .broadcast_to([128, nch, 8, 16])
                tko = p.op("gpsimd", "tensor_tensor", o4, pb4, sb_, ALU.mult,
                           waits=[tLs, tkc2] + list(out_waits), sem=out_sem)
            else:
                sb_ = scb.unsqueeze(2).broadcast_to([128, nblk, 16])
                tko = p.op("gpsimd", "tensor_tensor",
                           out_ap.rearrange("p (nb b) -> p nb b", b=16),
                           pb.rearrange("p (nb b) -> p nb b", b=16), sb_, ALU.mult,
                           waits=[tLs, tkc2] + list(out_waits), sem=out_sem)
            scr_free[pre["abs"]] = {
                "ab": [tL, tkn], "sgn": [tLs], "bm": [thi],
                "rc": [thi, tkn], "sc": [tko], "pa": [tL], "pb": [tko]}
            return tko

        prep_items = []
        for i_ in range(22):
            prep_items.append(("g", i_))
        for i_ in range(22):
            prep_items.append(("u", i_))
        for i_ in range(8):
            prep_items.append(("d", i_))
        prep = {"i": 0, "par": 0, "cons": {0: ("sv", 0), 1: ("sv", 0)},
                "stg_free": ("dwps", 0), "stgD_free": ("dwps", 0)}
        wscr = {"abs": "wpabs", "pa": "wppa", "pb": "wppb", "sgn": "wpsgn",
                "bm": "wpbm", "sc": "wpsc", "rc": "wprc"}

        def emit_prep(n):
            for _ in range(n):
                if prep["i"] >= len(prep_items):
                    return
                kind, it = prep_items[prep["i"]]
                prep["i"] += 1
                if kind in ("g", "u"):
                    src = (dr["wg"] if kind == "g" else dr["wu"])[
                        it * 128:(it + 1) * 128, :]
                    par = prep["par"]; prep["par"] ^= 1
                    wpin = t["wpin0"] if par == 0 else t["wpin1"]
                    psem = "dwp0" if par == 0 else "dwp1"
                    ld = p.dma("gpsimd", wpin[:], src, psem,
                               waits=[prep["cons"][par]])
                    j = it % 2
                    out_ap = t["wstg"][:, :, j, :]
                    tq = quant(wpin[:], E, out_ap, wscr,
                               [(psem, p.cnt[psem])], "sg",
                               out_waits=[prep["stg_free"]])
                    prep["cons"][par] = tq
                    if j == 1:
                        gw = it // 2
                        dsth = dr["wgs"] if kind == "g" else dr["wus"]
                        for c in range(8):
                            trk = p.dma("sync", t["wstgT"][:], t["wstg"][:, c],
                                        "dwpt", waits=[tq, p.total("dwps")],
                                        transpose=True)
                            p.dma("sync", dsth[c, :, gw * 256:(gw + 1) * 256],
                                  t["wstgT"][:].rearrange("p j f -> p (j f)"),
                                  "dwps", waits=[trk])
                        prep["stg_free"] = p.total("dwps")
                else:
                    for (c0, cw) in [(0, 1024), (1024, 1024), (2048, 768)]:
                        par = prep["par"]; prep["par"] ^= 1
                        wpin = t["wpin0"] if par == 0 else t["wpin1"]
                        psem = "dwp0" if par == 0 else "dwp1"
                        ld = p.dma("gpsimd", wpin[:, 0:cw],
                                   dr["wdn"][it * 128:(it + 1) * 128, c0:c0 + cw],
                                   psem, waits=[prep["cons"][par]])
                        out_ap = t["wstgD"][:, 0:cw // 128, :]
                        tq = quant(wpin[:, 0:cw], cw, out_ap, wscr,
                                   [(psem, p.cnt[psem])], "sg",
                                   out_waits=[prep["stgD_free"]])
                        prep["cons"][par] = tq
                        for c in range(cw // 128):
                            trk = p.dma("sync", t["wstgT"][:, 0, :].unsqueeze(1),
                                        t["wstgD"][:, c].unsqueeze(1), "dwpt",
                                        waits=[tq, p.total("dwps")], transpose=True)
                            p.dma("sync", dr["wds"][c0 // 128 + c, :,
                                                    it * 128:(it + 1) * 128],
                                  t["wstgT"][:, 0, :], "dwps", waits=[trk])
                        prep["stgD_free"] = p.total("dwps")

        pp_free = {i: ("ss", 0) for i in range(4)}
        pp_i = [0]

        def next_pp():
            i_ = pp_i[0] % 4
            pp_i[0] += 1
            return i_

        s2_free = ("ss", 0)
        s2_free2 = {0: ("ss", 0), 1: ("ss", 0)}
        oa_free = ("sv", 0)
        ob_free = ("sv", 0)
        wo_free = {0: ("sv", 0), 1: ("sv", 0)}
        rbc_free = ("sv", 0)
        rbs_free = {0: ("sv", 0), 1: ("sv", 0)}
        pt_free = {0: ("sg", 0), 1: ("sg", 0)}
        xr_free = ("sv", 0)
        ln1_free = {0: ("dl10", 0), 1: ("dl11", 0)}
        ln_last = ("sv", 0)
        slice_tok = [("sv", 0)]
        last_av = ("pe", 0)

        def emit_xload(s_i2, qt2):
            xsl = dr["x"][s_i2].rearrange("(n h t p) (c f) -> n h p c t f",
                                          h=2, t=2, p=128, f=128)[qt2]
            for hf in range(2):
                ldx = p.dma("gpsimd", t["xst"][:], xsl[hf], "dxs",
                            waits=slice_tok[-1:] + [p.total("dxt")])
                for c in range(8):
                    p.dma("sync",
                          t["xT"][:, c, hf * 256:(hf + 1) * 256]
                          .rearrange("p (j f) -> p j f", f=128),
                          t["xst"][:, c], "dxt", waits=[ldx], transpose=True)
            return p.total("dxt")

        xT_rdy = emit_xload(0, 0)
        for s_i in range(NSEQ):
            xs = dr["x"][s_i]
            vset = p.op("gpsimd", "memset", t["V"][:, :, :, HD:HD + 1], 1.0,
                        waits=[last_av])

            for qt in range(NQT):
                sl = slice(qt * TS, (qt + 1) * TS)

                def proj(wname, bias, nout, kchunks, src_cT, rdy, evac):
                    toks = []
                    for m in range(nout):
                        ppi = next_pp()
                        mm = None
                        for ci in range(kchunks):
                            w0 = ([rdy, pp_free[ppi], wl_all, s2_free2[0],
                                   s2_free2[1]] if ci == 0 else [])
                            mm = p.op("tensor", "matmul", pp[ppi],
                                      t[wname][:, ci, m * 128:(m + 1) * 128],
                                      t[src_cT][:, ci],
                                      start=(ci == 0), stop=(ci == kchunks - 1),
                                      waits=w0)
                        dtk = evac(m, ppi, mm)
                        pp_free[ppi] = dtk
                        toks.append(dtk)
                    return toks

                cq_toks = proj("wqb", "bqp", 2, 8, "xT", xT_rdy,
                               lambda m, ppi, mm: p.op(
                                   "scalar", "activation", t["cqT"][:, m], pp[ppi],
                                   AF.Gelu, bias=t["bqp"][:, m:m + 1],
                                   waits=[mm] + slice_tok[-1:]))
                ckv_toks = proj("wkvb", "bkvp", 2, 8, "xT", xT_rdy,
                                lambda m, ppi, mm: p.op(
                                    "scalar", "activation", t["ckvT"][:, m], pp[ppi],
                                    AF.Gelu, bias=t["bkvp"][:, m:m + 1],
                                    waits=[mm] + slice_tok[-1:]))
                qc_toks = proj("wqcb", "bqcp", 7, 2, "cqT", cq_toks[-1],
                               lambda m, ppi, mm: p.op(
                                   "scalar", "activation", t["qTs"][:, m], pp[ppi],
                                   AF.Gelu, bias=t["bqcp"][:, m:m + 1], waits=[mm]))
                kc_toks = proj("wkcb", "bkcp", 7, 2, "ckvT", ckv_toks[-1],
                               lambda m, ppi, mm: p.op(
                                   "scalar", "activation", t["kT"][:, m, sl], pp[ppi],
                                   AF.Gelu, bias=t["bkcp"][:, m:m + 1], waits=[mm]))
                tbl = []
                for nm in ["chh", "shh"]:
                    tbl.append(p.dma("gpsimd", t[nm][:], dr[nm][:, sl], "drt",
                                     waits=slice_tok[-1:]))
                tbl_all = p.total("drt")

                rp_toks = []
                for which, wname, bname, dstT, dsl in [
                        ("q", "wqrb", "bqrp", "qTs", None),
                        ("k", "wkrb", "bkrp", "kT", sl)]:
                    qr_tok = proj(wname, bname, 1, 8, "xT", xT_rdy,
                                  lambda m, ppi, mm: p.op(
                                      "vector", "tensor_scalar", t["qr"][:], pp[ppi],
                                      t[bname][:, 0:1], None, ALU.add,
                                      waits=[mm] + rp_toks[-1:] + slice_tok[-1:]))[0]
                    w1 = p.dma("gpsimd", t["qrw"][0:64, :], t["qr"][64:128, :], "dsw",
                               waits=[qr_tok] + rp_toks[-1:])
                    w2 = p.dma("gpsimd", t["qrw"][64:128, :], t["qr"][0:64, :], "dsw",
                               waits=[qr_tok])
                    sw_all = p.total("dsw")
                    a = p.op("vector", "tensor_tensor", RopA, t["qr"][:],
                             t["chh"][:], ALU.mult, waits=[sw_all, tbl_all])
                    b = p.op("vector", "tensor_tensor", RopB, t["qrw"][:],
                             t["shh"][:], ALU.mult, waits=[a])
                    g = p.op("vector", "tensor_tensor", RopA, RopA,
                             RopB, ALU.add, waits=[b])
                    dst = t[dstT][:, 7] if dsl is None else t[dstT][:, 7, dsl]
                    rp_toks.append(p.op("scalar", "activation", dst, RopA,
                                        AF.Gelu, waits=[g]))

                v_toks = []
                for tt_ in range(4):
                    nt = qt * 4 + tt_
                    for dvh in range(2):
                        ppi = next_pp()
                        mm = None
                        for ci in range(2):
                            mm = p.op("tensor", "matmul", pp[ppi],
                                      t["ckvT"][:, ci, tt_ * 128:(tt_ + 1) * 128],
                                      t["wvb"][:, ci, dvh * 512:(dvh + 1) * 512],
                                      start=(ci == 0), stop=False,
                                      waits=([ckv_toks[-1], pp_free[ppi], tk_ones,
                                              s2_free2[0], s2_free2[1]] if ci == 0 else []))
                        mm = p.op("tensor", "matmul", pp[ppi],
                                  t["onesF"][0:1, 0:128],
                                  t["bvrb"][0:1, dvh * 512:(dvh + 1) * 512],
                                  start=False, stop=True)
                        ev = p.op("vector", "tensor_copy",
                                  t["V"][:, nt, dvh * 8:(dvh + 1) * 8, 0:HD],
                                  pp[ppi].rearrange("p (h d) -> p h d", d=64),
                                  waits=[mm, vset])
                        pp_free[ppi] = ev
                        v_toks.append(ev)
                slice_rdy = [qc_toks[-1], kc_toks[-1], rp_toks[0], rp_toks[1],
                             v_toks[-1], p.total("sv")]
                slice_tok = slice_rdy
                do_tap("P%d_%d" % (s_i, qt))

                if qt + 1 < NQT:
                    xT_rdy = emit_xload(s_i, qt + 1)
                elif s_i + 1 < NSEQ:
                    xT_rdy = emit_xload(s_i + 1, 0)
                emit_prep(2)
                kmax = 4 * (qt + 1)
                attn_cons = []

                def emit_qk(ph, kt):
                    buf = kt % 2
                    S2b = S2[:, buf * 1024:(buf + 1) * 1024]
                    ksl = slice(kt * KT, (kt + 1) * KT)
                    w0 = [s2_free2[buf]] + (slice_rdy
                                            if (ph == 0 and kt < 2) else [])
                    p.op("tensor", "matmul", S2b[:, 0:QT],
                         t["kT"][0:64, ph, ksl], t["qTs"][0:64, ph],
                         start=True, stop=True, tile_position=(0, 0),
                         waits=w0)
                    return p.op("tensor", "matmul", S2b[:, QT:2 * QT],
                                t["kT"][64:128, ph, ksl], t["qTs"][64:128, ph],
                                start=True, stop=True, tile_position=(64, 0))

                pend_qk = None
                for ph in range(8):
                    av_tok = None
                    qk_tok = {0: pend_qk} if pend_qk is not None \
                        else {0: emit_qk(ph, 0)}
                    pend_qk = None
                    for kt in range(kmax):
                        buf = kt % 2
                        ptb = kt % 2
                        PT = [t["PT0"], t["PT1"]][ptb]
                        exp_tok = p.op("scalar", "activation", PT[:],
                                       S2[:, buf * 1024:(buf + 1) * 1024],
                                       AF.Exp, scale=0.125,
                                       waits=[qk_tok[kt], pt_free[ptb]])
                        s2_free2[buf] = exp_tok
                        if kt + 1 < kmax:
                            qk_tok[kt + 1] = emit_qk(ph, kt + 1)
                        elif ph + 1 < 8:
                            pend_qk = emit_qk(ph + 1, 0)
                        mask_toks = []
                        r = kt - 4 * qt
                        if 0 <= r <= 3:
                            for hh in range(2):
                                mask_toks.append(p.op(
                                    "vector", "tensor_tensor",
                                    PT[:, hh * QT:(hh + 1) * QT],
                                    PT[:, hh * QT:(hh + 1) * QT],
                                    t["masks"][:, r, :], ALU.mult,
                                    waits=[exp_tok, wl_all]))
                        first, last = (kt == 0), (kt == kmax - 1)
                        wv_ = [exp_tok] + mask_toks
                        if first:
                            wv_ += [oa_free, ob_free]
                        p.op("tensor", "matmul", OA, t["V"][:, kt, 2 * ph, :],
                             PT[:, 0:QT],
                             start=first, stop=last, waits=wv_)
                        av_tok = p.op("tensor", "matmul", OB,
                                      t["V"][:, kt, 2 * ph + 1, :],
                                      PT[:, QT:2 * QT],
                                      start=first, stop=last)
                        pt_free[ptb] = av_tok
                        last_av = av_tok
                    rbsb = t["rbs"] if ph % 2 == 0 else t["rbs2"]
                    rAB = p.op("vector", "reciprocal", t["rcpA"][64:65, :],
                               t["PS_B"][64:65, 0:1024], waits=[av_tok], lp=True)
                    bA = p.op("tensor", "matmul", RB, t["onesF"][64:65, 0:64],
                              t["rcpA"][64:65, 0:QT], start=True, stop=True,
                              tile_position=(64, 0), waits=[rAB, rbc_free])
                    bB = p.op("tensor", "matmul", RB2, t["onesF"][64:65, 0:64],
                              t["rcpA"][64:65, QT:2 * QT], start=True, stop=True,
                              tile_position=(64, 0))
                    cpA = p.op("vector", "tensor_copy", rbsb[:, 0:QT], RB,
                               waits=[bA, rbs_free[ph % 2]], lp=True)
                    cpB = p.op("vector", "tensor_copy", rbsb[:, QT:2 * QT], RB2,
                               waits=[bB], lp=True)
                    rbc_free = cpB
                    cA = p.op("vector", "scalar_tensor_tensor",
                              t["attnT"][0:64, ph, :], OA[0:64, :], 1.0,
                              rbsb[:, 0:QT], ALU.mult, ALU.mult,
                              waits=[cpA])
                    oa_free = cA
                    wB = p.op("vector", "scalar_tensor_tensor",
                              t["attnT"][64:128, ph, :],
                              OB[0:64, :], 1.0, rbsb[:, QT:2 * QT],
                              ALU.mult, ALU.mult, waits=[cpB])
                    ob_free = wB
                    rbs_free[ph % 2] = wB
                    attn_cons = [cA, wB]
                    emit_prep(1)
                do_tap("A%d_%d" % (s_i, qt))

                for tt_ in range(4):
                    gt = qt * 4 + tt_
                    tsl = slice(gt * TT, (gt + 1) * TT)
                    lsl = slice(tt_ * 128, (tt_ + 1) * 128)
                    ldr = p.dma("gpsimd", t["xr"][:], xs[tsl, :], "dre",
                                waits=[xr_free])
                    ev_toks = []
                    for eh in range(2):
                        esl = slice(eh * 512, (eh + 1) * 512)
                        mm = None
                        for c in range(8):
                            w0 = ([wo_free[eh], rbc_free] + attn_cons
                                  if c == 0 else [])
                            mm = p.op("tensor", "matmul", wop[eh],
                                      t["attnT"][:, c, lsl], t["wob"][:, c, esl],
                                      start=(c == 0), stop=False, waits=w0)
                        mm = p.op("tensor", "matmul", wop[eh],
                                  t["onesF"][0:1, 0:128], t["borb"][0:1, esl],
                                  start=False, stop=True)
                        ev = p.op("vector", "tensor_tensor", t["rres"][:, esl],
                                  wop[eh], t["xr"][:, esl], ALU.add,
                                  waits=[mm, p.total("dre"), ln_last])
                        wo_free[eh] = ev
                        ev_toks.append(ev)
                    xr_free = ev_toks[-1]
                    rbc_free = ev_toks[0]
                    sm = p.op("vector", "tensor_reduce", t["s_sum"][:], t["rres"][:],
                              axis=AX.X, op=ALU.add, waits=ev_toks)
                    sqa = p.op("scalar", "activation", t["qrw"][:], t["rres"][:, 0:512],
                               AF.Square, accum_out=t["s_sq"][:], waits=ev_toks)
                    sq = p.op("scalar", "activation", t["qrw"][:], t["rres"][:, 512:1024],
                              AF.Square, accum_out=t["s_sq2"][:], waits=[sqa])
                    mn = p.op("vector", "tensor_scalar", t["s_mean"][:], t["s_sum"][:],
                              1.0 / E, None, ALU.mult, waits=[sm])
                    s12 = p.op("vector", "tensor_tensor", t["s_sq"][:], t["s_sq"][:],
                               t["s_sq2"][:], ALU.add, waits=[sq])
                    v1 = p.op("vector", "tensor_scalar", t["s_var"][:], t["s_sq"][:],
                              1.0 / E, None, ALU.mult, waits=[s12])
                    m2 = p.op("vector", "tensor_tensor", t["s_sum"][:], t["s_mean"][:],
                              t["s_mean"][:], ALU.mult, waits=[mn])
                    v2 = p.op("vector", "tensor_tensor", t["s_var"][:], t["s_var"][:],
                              t["s_sum"][:], ALU.subtract, waits=[v1, m2])
                    lg = p.op("scalar", "activation", t["s_rstd"][:], t["s_var"][:],
                              AF.Ln, bias=t["epst"][:, 0:1], waits=[v2, tk_eps])
                    ex = p.op("scalar", "activation", t["s_rstd"][:], t["s_rstd"][:],
                              AF.Exp, scale=-0.5, waits=[lg])
                    ln1t = t["ln1t0"]
                    no = p.op("vector", "tensor_scalar", ln1t[:], t["rres"][:],
                              t["s_mean"][:, 0:1], t["s_rstd"][:, 0:1],
                              ALU.subtract, ALU.mult,
                              waits=[ex, ln1_free[0]])
                    ln_last = no
                    st_ = p.dma("gpsimd", dr["ln1s"][s_i][tsl, :], ln1t[:],
                                "dl10", waits=[no])
                    ln1_free[0] = st_
                slice_tok = [wo_free[0], wo_free[1], ln_last]
                do_tap("W%d_%d" % (s_i, qt))

        emit_prep(len(prep_items))
        p.barrier(dma_drain=["dwp0", "dwp1", "dwpt", "dwps", "dl10", "dl11",
                             "dre", "dxs", "dxt", "drt", "dsw", "dswg"])

        for nm, src in [("wgT", "wgs"), ("wuT", "wus"), ("wdT", "wds")]:
            p.dma("gpsimd", t[nm][:], dr[src].rearrange("c p m -> p c m"), "dwft")
        wft_all = p.total("dwft")
        T_H1 = {tl: t["h1ab"].rearrange("p (c t f) -> p c t f", t=2, f=128)
                [:, :, tl, :] for tl in range(NT)}
        xq_scr = {"abs": "qabs", "pa": "qpa", "pb": "qpb", "sgn": "qsgn",
                  "bm": "qbm", "sc": "qsc", "rc": "qrc"}
        tk_eps2 = p.op("gpsimd", "memset", t["epst"][:], 1e-5)

        gu_free = {"g0": ("sv", 0), "u0": ("sv", 0), "g1": ("sv", 0), "u1": ("sv", 0)}
        self_hsg = {0: ("sv", 0), 1: ("sv", 0)}
        prev_q_last = [("sg", 0)]
        dn_free = {0: ("ss", 0), 1: ("ss", 0)}
        out_free = {0: ("dou0", 0), 1: ("dou1", 0)}
        no_prev = {0: ("sv", 0), 1: ("sv", 0)}
        hq2_free = ("dhq", 0)
        xqT_free = ("pe", 0)
        hqT_free = ("pe", 0)
        R2 = {0: t["r2"], 1: t["r2b"]}
        T_LN = {0: t["ln1L0"], 1: t["ln1L1"]}
        xq_state = {"free": {0: ("sv", 0), 1: ("sv", 0)},
                    "xq2_free": ("dxq", 0), "ld": {}, "q_toks": []}

        def emit_xq(s_i2, g2):
            for tl in range(NT):
                gt = g2 * NT + tl
                tsl = slice(gt * TT, (gt + 1) * TT)
                lsem = "dlf0" if tl == 0 else "dlf1"
                p.dma("gpsimd", T_LN[tl][:], dr["ln1s"][s_i2][tsl, :], lsem,
                      waits=[xq_state["free"][tl]])
                xq_state["ld"][tl] = (lsem, p.cnt[lsem])
            q_toks = []
            for tl in range(NT):
                q_toks.append(quant(T_LN[tl][:], E, t["xq2"][:, :, tl, :], xq_scr,
                                    [xq_state["ld"][tl]], "sg",
                                    out_waits=[xq_state["xq2_free"]]))
            xq_state["q_toks"] = q_toks

        def emit_xqT(xqT_free_):
            for c in range(8):
                p.dma("scalar", t["xqT"][:, c].rearrange("p (j f) -> p j f", f=128),
                      t["xq2"][:, c], "dxq",
                      waits=xq_state["q_toks"] + [xqT_free_], transpose=True)
            xq_state["xq2_free"] = p.total("dxq")
            return p.total("dxq")

        glist = [(si2, gg) for si2 in range(NSEQ) for gg in range(NTT // NT)]
        emit_xq(*glist[0])
        xqT_rdy = emit_xqT(xqT_free)
        for gidx, (s_i, g_) in enumerate(glist):
            if True:
                cv_tok = {}
                for tl in range(NT):
                    cv_tok[tl] = p.op("gpsimd", "tensor_copy", R2[tl][:],
                                      T_LN[tl][:],
                                      waits=[out_free[tl], xq_state["ld"][tl]])
                    xq_state["free"][tl] = cv_tok[tl]
                nsteps = len(ITS)
                h2_tok = {}
                q_tok = {}
                tr_rdy = {}
                dn_last = {0: None, 1: None}

                def emit_gu(it):
                    i0, iw = ITS[it]
                    res = {}
                    for tl in range(NT):
                        pr = (it * NT + tl) % 2
                        gps = gps2[pr]
                        ups = ups2[pr]
                        mmg = None
                        for c in range(8):
                            mmg = p.op("tensor", "matmul", gps[:, 0:iw],
                                       t["xqT"][:, c, tl * 128:(tl + 1) * 128],
                                       t["wgT"][:, c, i0:i0 + iw],
                                       start=(c == 0), stop=(c == 7),
                                       waits=([xqT_rdy, gu_free["g%d" % pr],
                                               wft_all, rbc_free] if c == 0 else []))
                        mmu = None
                        for c in range(8):
                            mmu = p.op("tensor", "matmul", ups[:, 0:iw],
                                       t["xqT"][:, c, tl * 128:(tl + 1) * 128],
                                       t["wuT"][:, c, i0:i0 + iw],
                                       start=(c == 0), stop=(c == 7),
                                       waits=([gu_free["u%d" % pr]] if c == 0 else []))
                        res[tl] = (mmg, mmu, pr)
                    return res

                def emit_h(it, gu, xfree):
                    i0, iw = ITS[it]
                    toks = {}
                    for tl in range(NT):
                        mmg, mmu, pr = gu[tl]
                        gps = gps2[pr]
                        ups = ups2[pr]
                        hsg_ = t["hsg"] if tl == 0 else t["hsgB"]
                        hv = T_H1[tl][:, 0:iw // 128, :]
                        tg = p.op("scalar", "activation", hsg_[:, 0:iw],
                                  gps[:, 0:iw], AF.Tanh, scale=0.5,
                                  waits=[mmg, self_hsg[tl]])
                        h1 = p.op("vector", "scalar_tensor_tensor",
                                  hv,
                                  hsg_[:, 0:iw].rearrange(
                                      "p (c f) -> p c f", f=128),
                                  1.0,
                                  ups[:, 0:iw].rearrange(
                                      "p (c f) -> p c f", f=128),
                                  ALU.add, ALU.mult,
                                  waits=[tg, mmu] + (xfree if tl == 0 else []))
                        gu_free["u%d" % pr] = h1
                        h2 = p.op("vector", "scalar_tensor_tensor",
                                  hv, hv, 0.5,
                                  gps[:, 0:iw].rearrange(
                                      "p (c f) -> p c f", f=128),
                                  ALU.mult, ALU.mult, waits=[h1])
                        gu_free["g%d" % pr] = h2
                        self_hsg[tl] = h1
                        toks[tl] = h2
                    return toks

                def emit_quant(it, mid=None):
                    i0, iw = ITS[it]
                    c0 = i0 // 128
                    out_ap = t["hq2"][:, c0:c0 + iw // 128, :, :] \
                        .rearrange("p c t f -> p (c t) f")
                    tok = quant(t["h1ab"][:, 0:2 * iw], 2 * iw, out_ap,
                                xq_scr,
                                [h2_tok[it][0], h2_tok[it][1]], "sg",
                                out_waits=[hq2_free], mid=mid)
                    return [tok]

                def emit_tr(it):
                    i0, iw = ITS[it]
                    c0 = i0 // 128
                    dsem = ["dhq", "dhq1", "dhq2", "dhq3", "dhq4", "dhq5"][it]
                    dq = "sync" if it % 2 == 0 else "scalar"
                    for c in range(c0, c0 + iw // 128):
                        p.dma(dq,
                              t["hqT"][:, c].rearrange("p (j f) -> p j f", f=128),
                              t["hq2"][:, c], dsem,
                              waits=q_tok[it] + [hqT_free], transpose=True)
                    tr_rdy[it] = p.total(dsem)

                gu = emit_gu(0)
                gu_last = gu
                h2_tok[0] = emit_h(0, gu, [prev_q_last[0]])
                for it in range(nsteps):
                    gu_next = emit_gu(it + 1) if it + 1 < nsteps else None
                    if gu_next is not None:
                        gu_last = gu_next

                    def mid_fn(xfree, it=it, gu_next=gu_next):
                        if gu_next is not None:
                            h2_tok[it + 1] = emit_h(it + 1, gu_next, xfree)
                    q_tok[it] = emit_quant(it, mid=mid_fn)
                    emit_tr(it)
                prev_q_last[0] = q_tok[nsteps - 1][0]
                if gidx + 1 < len(glist):
                    emit_xq(*glist[gidx + 1])
                for it in range(nsteps):
                    i0, iw = ITS[it]
                    c0 = i0 // 128
                    for tl in range(NT):
                        for eh in range(2):
                            esl = slice(eh * 512, (eh + 1) * 512)
                            mm = None
                            for c in range(c0, c0 + iw // 128):
                                w0 = []
                                if c == c0:
                                    w0 = [tr_rdy[it]]
                                    if it == 0 and eh == 0:
                                        w0 += [dn_free[tl], s2_free]
                                mm = p.op("tensor", "matmul",
                                          t["PS_A"][:, tl * 1024 + esl.start:
                                                     tl * 1024 + esl.stop],
                                          t["hqT"][:, c, tl * 128:(tl + 1) * 128],
                                          t["wdT"][:, c, esl],
                                          start=(it == 0 and c == c0),
                                          stop=(it == nsteps - 1 and
                                                c == c0 + iw // 128 - 1),
                                          waits=w0)
                            if it == nsteps - 1:
                                dn_last[tl] = mm
                xqT_free = gu_last[NT - 1][1]
                if gidx + 1 < len(glist):
                    xqT_rdy = emit_xqT(xqT_free)
                hq2_free = dn_last[NT - 1]
                dn_toks = dn_last
                s2_free = ("ss", 0)
                hqT_free = dn_last[NT - 1]
                sq_toks = {}
                for tl in range(NT):
                    gt = g_ * NT + tl
                    tsl = slice(gt * TT, (gt + 1) * TT)
                    dnv = t["PS_A"][:, tl * 1024:(tl + 1) * 1024]
                    r2v = R2[tl]
                    r2t = []
                    for eh in range(2):
                        esl = slice(eh * 512, (eh + 1) * 512)
                        r2t.append(p.op("vector", "tensor_tensor", r2v[:, esl],
                                        dnv[:, esl], r2v[:, esl], ALU.add,
                                        waits=[dn_toks[tl], cv_tok[tl]]))
                    sm = p.op("vector", "tensor_reduce", t["f_sum"][:], r2v[:],
                              axis=AX.X, op=ALU.add, waits=r2t)
                    sqa = p.op("scalar", "activation", t["hsg"][:], r2v[:, 0:512],
                               AF.Square, accum_out=t["f_sq"][:], waits=r2t)
                    sq = p.op("scalar", "activation", t["hsg"][:], r2v[:, 512:1024],
                              AF.Square, accum_out=t["f_sq2"][:], waits=[sqa])
                    dn_free[tl] = sq
                    mn = p.op("vector", "tensor_scalar", t["f_mean"][:], t["f_sum"][:],
                              1.0 / E, None, ALU.mult, waits=[sm])
                    s12 = p.op("vector", "tensor_tensor", t["f_sq"][:], t["f_sq"][:],
                               t["f_sq2"][:], ALU.add, waits=[sq])
                    v1 = p.op("vector", "tensor_scalar", t["f_var"][:], t["f_sq"][:],
                              1.0 / E, None, ALU.mult, waits=[s12])
                    m2 = p.op("vector", "tensor_tensor", t["f_sum"][:], t["f_mean"][:],
                              t["f_mean"][:], ALU.mult, waits=[mn])
                    v2 = p.op("vector", "tensor_tensor", t["f_var"][:], t["f_var"][:],
                              t["f_sum"][:], ALU.subtract, waits=[v1, m2])
                    lg = p.op("scalar", "activation", t["f_rstd"][:], t["f_var"][:],
                              AF.Ln, bias=t["epst"][:, 0:1], waits=[v2, tk_eps2])
                    ex = p.op("scalar", "activation", t["f_rstd"][:], t["f_rstd"][:],
                              AF.Exp, scale=-0.5, waits=[lg])
                    no = p.op("vector", "tensor_scalar", r2v[:], r2v[:],
                              t["f_mean"][:, 0:1], t["f_rstd"][:, 0:1],
                              ALU.subtract, ALU.mult, waits=[ex])
                    osem = "dou0" if tl == 0 else "dou1"
                    p.dma("gpsimd", dr["out"][s_i][tsl, :], r2v[:], osem,
                          waits=[no])
                    out_free[tl] = p.total(osem)
                s2_free = dn_free[NT - 1]
                do_tap("F%d_%d" % (s_i, g_))
        p.wait("sync", [p.total("dou0"), p.total("dou1")])

    with nc.Block() as block:
        @block.sync
        def _(h):
            p.start("sync", h)
            emit_program()

        @block.gpsimd
        def _(h):
            p.start("gpsimd", h)
            emit_program()

        @block.tensor
        def _(h):
            p.start("tensor", h)
            emit_program()

        @block.scalar
        def _(h):
            p.start("scalar", h)
            emit_program()

        @block.vector
        def _(h):
            p.start("vector", h)
            emit_program()

    return nc, dr


def host_inputs(inputs, seq_pair):
    g = lambda k: np.ascontiguousarray(np.asarray(inputs[k]))
    im = {}
    im["x"] = g("L0_attn_x")[2 * seq_pair:2 * seq_pair + 2]
    im["wq"] = g("L0_attn_wq"); im["wqc"] = g("L0_attn_wqc"); im["wqr"] = g("L0_attn_wqr")
    im["wkv"] = g("L0_attn_wkv"); im["wkr"] = g("L0_attn_wkr"); im["wkc"] = g("L0_attn_wkc")
    im["wv"] = g("L0_attn_wv"); im["wo"] = g("L0_attn_wo")
    im["bqp"] = g("L0_attn_bq").reshape(2, 128).T.copy()
    im["bqcp"] = g("L0_attn_bqc").reshape(7, 128).T.copy()
    im["bqrp"] = g("L0_attn_bqr").reshape(1, 128).T.copy()
    im["bkvp"] = g("L0_attn_bkv").reshape(2, 128).T.copy()
    im["bkrp"] = g("L0_attn_bkr").reshape(1, 128).T.copy()
    im["bkcp"] = g("L0_attn_bkc").reshape(7, 128).T.copy()
    im["bvr"] = g("L0_attn_bv").reshape(1, E)
    im["bor"] = g("L0_attn_bo").reshape(1, E)
    im["wg"] = g("L0_ffn_w_gate"); im["wu"] = g("L0_ffn_w_up"); im["wdn"] = g("L0_ffn_w_down")
    import jax
    import jax.numpy as jnp
    with jax.default_device(jax.devices("cpu")[0]):
        inv_freq = 1.0 / 10000.0 ** (jnp.arange(0, DR, 2, dtype=jnp.float32) / DR)
        tpos = jnp.arange(S, dtype=jnp.float32)
        freqs = tpos[:, None] * inv_freq[None, :]
        cos32 = np.asarray(jnp.cos(freqs)).T
        sin32 = np.asarray(jnp.sin(freqs)).T
    im["chh"] = np.concatenate([cos32, cos32], 0).astype(np.float32)
    im["shh"] = np.concatenate([-sin32, sin32], 0).astype(np.float32)
    masks = np.zeros((128, 4, QT), dtype=ml_dtypes.bfloat16)
    qf = np.arange(QT)
    for r in range(4):
        for kp in range(128):
            masks[kp, r, :] = (qf >= kp + 128 * r).astype(ml_dtypes.bfloat16)
    im["masks"] = masks
    return {"i_" + k: v for k, v in im.items()}


_PROG_CACHE = {}


def kernel(**inputs):
    if "prog" not in _PROG_CACHE:
        _PROG_CACHE["prog"] = build_program()
    nc, dr = _PROG_CACHE["prog"]
    in_maps = [host_inputs(inputs, c) for c in range(NCORE)]
    res = run_bass_kernel_spmd(nc, in_maps, core_ids=list(range(NCORE)))
    return np.concatenate([res.results[c]["out"] for c in range(NCORE)], axis=0)


# version 30
# speedup vs baseline: 1.0365x; 1.0034x over previous
import math
from contextlib import ExitStack

import numpy as np
import ml_dtypes

import concourse.bass as bass
import concourse.mybir as mybir
from concourse.bass_utils import run_bass_kernel_spmd

F32 = mybir.dt.float32
BF16 = mybir.dt.bfloat16
FP16 = mybir.dt.float16
AF = mybir.ActivationFunctionType
ALU = mybir.AluOpType
AX = mybir.AxisListType

S = 2048
E = 1024
DC = 256
DR = 128
H = 16
HD = 64
I = 2816
NSEQ = 2
NCORE = 8
TS = 512
NSL = S // TS
TT = 128
NTT = S // TT
QT = 512
NQT = NSL
KT = 128
NKT = S // KT
NT = 2
ITS = [(k * 512, min(512, I - k * 512)) for k in range(6)]
RCP6 = float(np.float32(1.0) / np.float32(6.0))
C6H = float(np.float32(1.0 / 6.0))
C6L = float(np.float64(1.0) / 6.0 - np.float64(np.float32(1.0 / 6.0)))
ENG = {"pe": "tensor", "sv": "vector", "ss": "scalar", "sg": "gpsimd", "sy": "sync"}


class Prog:
    def __init__(self, nc, sems):
        self.nc = nc
        self.sems = sems
        self.which = None
        self.h = None
        self.cnt = None

    def start(self, which, handle):
        self.which = which
        self.h = handle
        self.cnt = {k: 0 for k in self.sems}

    def op(self, engine, method, *args, waits=(), sem=None, lp=False, **kw):
        sem = sem or {"tensor": "pe", "vector": "sv", "scalar": "ss",
                      "gpsimd": "sg", "sync": "sy"}[engine]
        self.cnt[sem] += 1
        val = self.cnt[sem]
        if self.which == engine:
            waits = [(s, v) for (s, v) in waits if v > 0]
            if len(waits) > 1:
                for s, v in waits[:-1]:
                    self.h.wait_ge(self.sems[s], v)
            if lp:
                with self.nc.allow_low_precision(reason="bf16 softmax denominators (error budgeted)"):
                    inst = getattr(self.h, method)(*args, **kw)
            else:
                inst = getattr(self.h, method)(*args, **kw)
            if waits:
                inst._wait_ge(self.sems[waits[-1][0]], waits[-1][1])
            inst.then_inc(self.sems[sem], 1)
        return (sem, val)

    def dma(self, queue, out, in_, sem, waits=(), transpose=False):
        self.cnt[sem] += 16
        val = self.cnt[sem]
        if self.which == queue:
            waits = [(s, v) for (s, v) in waits if v > 0]
            if len(waits) > 1:
                for s, v in waits[:-1]:
                    self.h.wait_ge(self.sems[s], v)
            inst = self.h.dma_start(out=out, in_=in_, transpose=transpose)
            if waits:
                inst._wait_ge(self.sems[waits[-1][0]], waits[-1][1])
            inst.then_inc(self.sems[sem], 16)
        return (sem, val)

    def total(self, sem):
        return (sem, self.cnt[sem])

    def wait(self, engine, waits):
        if self.which == engine:
            for s, v in waits:
                if v > 0:
                    self.h.wait_ge(self.sems[s], v)

    def barrier(self, dma_drain=()):
        drains = [(s, self.cnt[s]) for s in dma_drain if self.cnt[s] > 0]
        vals = {}
        for e in ["pe", "sv", "ss", "sg", "sy"]:
            self.cnt[e] += 1
            vals[e] = self.cnt[e]
        for e, eng in ENG.items():
            if self.which == eng:
                for s, v in drains:
                    self.h.wait_ge(self.sems[s], v)
                self.h.nop().then_inc(self.sems[e], 1)
                for o, ov in vals.items():
                    if o != e:
                        self.h.wait_ge(self.sems[o], ov)


def build_program(tap=None):
    nc = bass.Bass("TRN2", target_bir_lowering=False)
    dr = {}

    def din(nm, shp, dt=F32):
        dr[nm] = nc.dram_tensor("i_" + nm, shp, dt, kind="ExternalInput").ap()

    din("x", [NSEQ, S, E])
    din("wq", [E, DC]); din("wqc", [DC, E - DR]); din("wqr", [E, DR])
    din("wkv", [E, DC]); din("wkr", [E, DR]); din("wkc", [DC, E - DR])
    din("wv", [DC, E]); din("wo", [E, E])
    din("bqp", [128, 2]); din("bqcp", [128, 7]); din("bqrp", [128, 1])
    din("bkvp", [128, 2]); din("bkrp", [128, 1]); din("bkcp", [128, 7])
    din("bvr", [1, E]); din("bor", [1, E])
    din("wg", [I, E]); din("wu", [I, E]); din("wdn", [E, I])
    din("chh", [128, S]); din("shh", [128, S])
    din("masks", [128, 4, QT], BF16)
    dr["out"] = nc.dram_tensor("out", [NSEQ, S, E], F32, kind="ExternalOutput").ap()
    dr["ln1s"] = nc.dram_tensor("ln1s", [NSEQ, S, E], BF16, kind="Internal").ap()
    dr["wgs"] = nc.dram_tensor("wgs", [8, 128, I], BF16, kind="Internal").ap()
    dr["wus"] = nc.dram_tensor("wus", [8, 128, I], BF16, kind="Internal").ap()
    dr["wds"] = nc.dram_tensor("wds", [22, 128, E], BF16, kind="Internal").ap()
    if tap is not None:
        dr["tapout"] = nc.dram_tensor("tapout", tap["shape"], tap.get("dtype", F32),
                                      kind="ExternalOutput").ap()

    T = {}
    root = ExitStack()

    def alloc(stk, name, shape, dtype, psum=False):
        fn = nc.psum_tensor if psum else nc.sbuf_tensor
        T[name] = stk.enter_context(fn(name, shape, dtype))
        return T[name]

    alloc(root, "onesF", [128, 128], BF16)
    alloc(root, "epst", [128, 1], F32)
    alloc(root, "PS_A", [128, 2048], F32, psum=True)
    alloc(root, "PS_B", [128, 1024], F32, psum=True)
    alloc(root, "PS_C", [128, 512], F32, psum=True)
    alloc(root, "PS_D", [128, 512], F32, psum=True)

    att = ExitStack()
    for nm, shp, dt in [
        ("masks", [128, 4, QT], BF16),
        ("wqb", [128, 8, DC], BF16), ("wqcb", [128, 2, E - DR], BF16),
        ("wqrb", [128, 8, DR], BF16), ("wkvb", [128, 8, DC], BF16),
        ("wkrb", [128, 8, DR], BF16), ("wkcb", [128, 2, E - DR], BF16),
        ("wvb", [128, 2, E], BF16), ("wob", [128, 8, E], BF16),
        ("bqp", [128, 2], F32), ("bqcp", [128, 7], F32), ("bqrp", [128, 1], F32),
        ("bkvp", [128, 2], F32), ("bkrp", [128, 1], F32), ("bkcp", [128, 7], F32),
        ("bvrb", [1, E], BF16), ("borb", [1, E], BF16),
        ("kT", [128, 8, S], BF16), ("V", [128, NKT, H, HD + 1], BF16),
        ("qTs", [128, 8, TS], BF16), ("attnT", [128, 8, TS], BF16),
        ("xst", [128, 8, 2, 128], BF16), ("xT", [128, 8, TS], BF16),
        ("cqT", [128, 2, TS], BF16), ("ckvT", [128, 2, TS], BF16),
        ("qr", [128, TS], F32), ("qrw", [128, TS], F32),
        ("chh", [128, TS], F32), ("shh", [128, TS], F32),
        ("PT0", [128, 2 * QT], BF16), ("PT1", [128, 2 * QT], BF16),
        ("rcpA", [128, 2 * QT], BF16),
        ("rbs", [64, 2 * QT], F32), ("rbs2", [64, 2 * QT], F32),
        ("xr", [128, E], F32), ("rres", [128, E], F32),
        ("s_sum", [128, 1], F32), ("s_sq", [128, 1], F32),
        ("s_sq2", [128, 1], F32),
        ("s_mean", [128, 1], F32), ("s_var", [128, 1], F32),
        ("s_rstd", [128, 1], F32),
        ("ln1t0", [128, E], BF16),
        ("wpin0", [128, E], F32), ("wpin1", [128, E], F32),
        ("wpabs", [128, E], F32), ("wppa", [128, E], F32), ("wppb", [128, E], F32),
        ("wpsgn", [128, E], F32),
        ("wpbm", [128, E // 16], F32), ("wpsc", [128, E // 16], F32),
        ("wprc", [128, E // 16], F32),
        ("wstg", [128, 8, 2, 128], BF16), ("wstgT", [128, 2, 128], BF16),
        ("wstgD", [128, 8, 128], BF16),
    ]:
        alloc(att, nm, shp, dt)
    att.close()

    ffn = ExitStack()
    for nm, shp, dt in [
        ("wgT", [128, 8, I], BF16), ("wuT", [128, 8, I], BF16),
        ("wdT", [128, 22, E], BF16),
        ("ln1L0", [128, E], BF16), ("ln1L1", [128, E], BF16),
        ("qabs", [128, E], F32), ("qpa", [128, E], F32), ("qpb", [128, E], F32),
        ("qsgn", [128, E], F32),
        ("qbm", [128, E // 16], F32), ("qsc", [128, E // 16], F32),
        ("qrc", [128, E // 16], F32),
        ("xq2", [128, 8, NT, 128], BF16), ("xqT", [128, 8, NT * 128], BF16),
        ("hsg", [128, 512], F32), ("hsgB", [128, 512], F32),
        ("h1ab", [128, 1024], F32),
        ("hq2", [128, 22, NT, 128], BF16), ("hqT", [128, 22, NT * 128], BF16),
        ("r2", [128, E], F32), ("r2b", [128, E], F32),
        ("f_sum", [128, 1], F32), ("f_sq", [128, 1], F32),
        ("f_sq2", [128, 1], F32),
        ("f_mean", [128, 1], F32), ("f_var", [128, 1], F32),
        ("f_rstd", [128, 1], F32),
    ]:
        alloc(ffn, nm, shp, dt)
    ffn.close()

    sems = {}
    for nm in ["pe", "sv", "ss", "sg", "sy",
               "dxs", "dxt", "dwl", "drt", "dsw", "dre",
               "dl10", "dl11", "dwp0", "dwp1", "dwpt", "dwps",
               "dwft", "dlf0", "dlf1", "dxq", "dhq", "dhq1", "dhq2", "dhq3",
               "dhq4", "dhq5", "dou0", "dou1", "dswg"]:
        sems[nm] = root.enter_context(nc.semaphore(nm))

    p = Prog(nc, sems)
    t = T

    def emit_program():
        tap_done = [False]

        def do_tap(tag):
            if tap is not None and tap["tag"] == tag and not tap_done[0]:
                tap_done[0] = True
                tsrc = tap["tensor"]
                if "@" in tsrc:
                    base, kk = tsrc.split("@")
                    tsrc_ap = t[base][:, :, 0:int(kk)]
                elif "|" in tsrc:
                    base, kk = tsrc.split("|")
                    tsrc_ap = t[base][:, 0:int(kk)]
                else:
                    tsrc_ap = t[tsrc][:]
                p.dma("sync", dr["tapout"], tsrc_ap, "dou0",
                      waits=[p.total("sv"), p.total("ss"), p.total("sg"),
                             p.total("dswg"), p.total("dsw"), p.total("pe")])
                p.barrier(dma_drain=["dou0"])

        RopA = t["rres"][:, 0:TS]
        RopB = t["rres"][:, TS:2 * TS]
        S2 = t["PS_A"]
        pp = [t["PS_A"][:, i * 512:(i + 1) * 512] for i in range(4)]
        OA = t["PS_B"][0:65, 0:512]
        OB = t["PS_B"][0:65, 512:1024]
        RB = t["PS_C"][0:64, :]
        RB2 = t["PS_D"][0:64, :]
        wop = [t["PS_C"][:, :], t["PS_D"][:, :]]
        gps2 = [t["PS_C"][:, :], t["PS_B"][0:128, 0:512]]
        ups2 = [t["PS_D"][:, :], t["PS_B"][0:128, 512:1024]]
        dn = [t["PS_A"][:, 0:1024], t["PS_A"][:, 1024:2048]]

        for nm, src in [("wqb", "wq"), ("wqrb", "wqr"), ("wkvb", "wkv"),
                        ("wkrb", "wkr"), ("wob", "wo"),
                        ("wqcb", "wqc"), ("wkcb", "wkc"), ("wvb", "wv")]:
            p.dma("gpsimd", t[nm][:], dr[src].rearrange("(c p) m -> p c m", p=128), "dwl")
        for nm in ["bqp", "bqcp", "bqrp", "bkvp", "bkrp", "bkcp", "masks"]:
            p.dma("gpsimd", t[nm][:], dr[nm], "dwl")
        p.dma("gpsimd", t["bvrb"][:], dr["bvr"], "dwl")
        p.dma("gpsimd", t["borb"][:], dr["bor"], "dwl")
        tk_ones = p.op("gpsimd", "memset", t["onesF"][:], 1.0)
        tk_eps = p.op("gpsimd", "memset", t["epst"][:], 1e-5)
        wl_all = p.total("dwl")

        scr_free = {}

        def quant(x_ap, F, out_ap, pre, x_waits, out_sem, out_waits=(),
                  mid=None, pool_stair=False):
            nblk = F // 16
            ab = t[pre["abs"]][:, 0:F]
            pa = t[pre["pa"]][:, 0:F]
            pb = t[pre["pb"]][:, 0:F]
            sg_ = t[pre["sgn"]][:, 0:F]
            bm = t[pre["bm"]][:, 0:nblk]
            sc = t[pre["sc"]][:, 0:nblk]
            rc = t[pre["rc"]][:, 0:nblk]
            pv = scr_free.get(pre["abs"], {})
            x_waits = list(x_waits)
            a3 = ab.rearrange("p (nb b) -> p nb b", b=16)
            x3 = x_ap.rearrange("p (nb b) -> p nb b", b=16)
            tkb = p.op("vector", "tensor_reduce", bm, x3, axis=AX.X, op=ALU.max,
                       apply_absolute_value=True,
                       waits=x_waits + pv.get("bm", []))
            tlo = p.op("vector", "tensor_scalar", rc, bm, C6L, None, ALU.mult,
                       waits=[tkb] + pv.get("rc", []))
            thi = p.op("vector", "scalar_tensor_tensor", sc, bm, C6H, rc,
                       ALU.mult, ALU.add, waits=[tlo] + pv.get("sc", []))
            tkc = p.op("vector", "tensor_scalar", sc, sc, 1e-12, None, ALU.max,
                       waits=[thi])
            tkr = p.op("vector", "reciprocal", rc, sc, waits=[tkc])
            tka = p.op("scalar", "activation", ab, x_ap, AF.Abs,
                       waits=x_waits + pv.get("ab", []))
            tks = p.op("scalar", "activation", sg_, x_ap, AF.Sign,
                       waits=x_waits + pv.get("sgn", []))
            tkc2 = tkc
            scb = sc
            rb_ = rc.unsqueeze(2).broadcast_to([128, nblk, 16])
            tkn = p.op("gpsimd", "tensor_tensor",
                       pa.rearrange("p (nb b) -> p nb b", b=16), a3, rb_, ALU.mult,
                       waits=[tkr, tks, tka] + pv.get("pa", []))
            if mid is not None:
                mid([tka, tks, tkb])
            axn = pa
            if pool_stair:
                t1 = p.op("gpsimd", "tensor_scalar", pb, axn, 0.25, 0.5,
                          ALU.is_gt, ALU.mult, waits=[tkn] + pv.get("pb", []))
                steps = [(0.75, 0.5), (1.25, 0.5), (1.75, 0.5),
                         (2.5, None), (3.5, None), (5.0, 2.0)]
                for thr, w in steps:
                    if w is None:
                        c_ = p.op("gpsimd", "tensor_scalar", ab, axn, thr,
                                  None, ALU.is_gt, waits=[t1])
                    else:
                        c_ = p.op("gpsimd", "tensor_scalar", ab, axn, thr,
                                  w, ALU.is_gt, ALU.mult, waits=[t1])
                    t1 = p.op("gpsimd", "tensor_tensor", pb, pb, ab,
                              ALU.add, waits=[c_])
                tL = t1
            else:
                t1 = p.op("vector", "tensor_scalar", pb, axn, 0.25, None,
                          ALU.is_gt, waits=[tkn] + pv.get("pb", []))
                for thr in (0.75, 1.25, 1.75):
                    t1 = p.op("vector", "scalar_tensor_tensor", pb, axn, thr,
                              pb, ALU.is_gt, ALU.add, waits=[t1])
                h1_ = p.op("vector", "tensor_scalar", ab, axn, 5.0, None,
                           ALU.is_gt, waits=[tkn, tkb])
                for thr in (5.0, 3.5, 2.5):
                    h1_ = p.op("vector", "scalar_tensor_tensor", ab, axn, thr,
                               ab, ALU.is_gt, ALU.add, waits=[h1_])
                tL = p.op("vector", "scalar_tensor_tensor", pb, pb, 0.5, ab,
                          ALU.mult, ALU.add, waits=[t1, h1_])
            tLs = p.op("gpsimd", "tensor_tensor", pb, pb, sg_, ALU.mult,
                       waits=[tL, tks])
            if out_ap.ndim == 3:
                nch = out_ap.shape[1]
                o4 = out_ap.rearrange("p c (s b) -> p c s b", b=16)
                pb4 = pb.rearrange("p (c s b) -> p c s b", c=nch, b=16)
                sb_ = scb.rearrange("p (c s) -> p c s", s=8).unsqueeze(3) \
                    .broadcast_to([128, nch, 8, 16])
                tko = p.op("gpsimd", "tensor_tensor", o4, pb4, sb_, ALU.mult,
                           waits=[tLs, tkc2] + list(out_waits), sem=out_sem)
            else:
                sb_ = scb.unsqueeze(2).broadcast_to([128, nblk, 16])
                tko = p.op("gpsimd", "tensor_tensor",
                           out_ap.rearrange("p (nb b) -> p nb b", b=16),
                           pb.rearrange("p (nb b) -> p nb b", b=16), sb_, ALU.mult,
                           waits=[tLs, tkc2] + list(out_waits), sem=out_sem)
            scr_free[pre["abs"]] = {
                "ab": [tL, tkn], "sgn": [tLs], "bm": [thi],
                "rc": [thi, tkn], "sc": [tko], "pa": [tL], "pb": [tko]}
            return tko

        prep_items = []
        for i_ in range(22):
            prep_items.append(("g", i_))
        for i_ in range(22):
            prep_items.append(("u", i_))
        for i_ in range(8):
            prep_items.append(("d", i_))
        prep = {"i": 0, "par": 0, "cons": {0: ("sv", 0), 1: ("sv", 0)},
                "stg_free": ("dwps", 0), "stgD_free": ("dwps", 0)}
        wscr = {"abs": "wpabs", "pa": "wppa", "pb": "wppb", "sgn": "wpsgn",
                "bm": "wpbm", "sc": "wpsc", "rc": "wprc"}

        def emit_prep(n):
            for _ in range(n):
                if prep["i"] >= len(prep_items):
                    return
                kind, it = prep_items[prep["i"]]
                prep["i"] += 1
                if kind in ("g", "u"):
                    src = (dr["wg"] if kind == "g" else dr["wu"])[
                        it * 128:(it + 1) * 128, :]
                    par = prep["par"]; prep["par"] ^= 1
                    wpin = t["wpin0"] if par == 0 else t["wpin1"]
                    psem = "dwp0" if par == 0 else "dwp1"
                    ld = p.dma("gpsimd", wpin[:], src, psem,
                               waits=[prep["cons"][par]])
                    j = it % 2
                    out_ap = t["wstg"][:, :, j, :]
                    tq = quant(wpin[:], E, out_ap, wscr,
                               [(psem, p.cnt[psem])], "sg",
                               out_waits=[prep["stg_free"]])
                    prep["cons"][par] = tq
                    if j == 1:
                        gw = it // 2
                        dsth = dr["wgs"] if kind == "g" else dr["wus"]
                        for c in range(8):
                            trk = p.dma("sync", t["wstgT"][:], t["wstg"][:, c],
                                        "dwpt", waits=[tq, p.total("dwps")],
                                        transpose=True)
                            p.dma("sync", dsth[c, :, gw * 256:(gw + 1) * 256],
                                  t["wstgT"][:].rearrange("p j f -> p (j f)"),
                                  "dwps", waits=[trk])
                        prep["stg_free"] = p.total("dwps")
                else:
                    for (c0, cw) in [(0, 1024), (1024, 1024), (2048, 768)]:
                        par = prep["par"]; prep["par"] ^= 1
                        wpin = t["wpin0"] if par == 0 else t["wpin1"]
                        psem = "dwp0" if par == 0 else "dwp1"
                        ld = p.dma("gpsimd", wpin[:, 0:cw],
                                   dr["wdn"][it * 128:(it + 1) * 128, c0:c0 + cw],
                                   psem, waits=[prep["cons"][par]])
                        out_ap = t["wstgD"][:, 0:cw // 128, :]
                        tq = quant(wpin[:, 0:cw], cw, out_ap, wscr,
                                   [(psem, p.cnt[psem])], "sg",
                                   out_waits=[prep["stgD_free"]])
                        prep["cons"][par] = tq
                        for c in range(cw // 128):
                            trk = p.dma("sync", t["wstgT"][:, 0, :].unsqueeze(1),
                                        t["wstgD"][:, c].unsqueeze(1), "dwpt",
                                        waits=[tq, p.total("dwps")], transpose=True)
                            p.dma("sync", dr["wds"][c0 // 128 + c, :,
                                                    it * 128:(it + 1) * 128],
                                  t["wstgT"][:, 0, :], "dwps", waits=[trk])
                        prep["stgD_free"] = p.total("dwps")

        pp_free = {i: ("ss", 0) for i in range(4)}
        pp_i = [0]

        def next_pp():
            i_ = pp_i[0] % 4
            pp_i[0] += 1
            return i_

        s2_free = ("ss", 0)
        s2_free2 = {0: ("ss", 0), 1: ("ss", 0)}
        oa_free = ("sv", 0)
        ob_free = ("sv", 0)
        wo_free = {0: ("sv", 0), 1: ("sv", 0)}
        rbc_free = ("sv", 0)
        rbs_free = {0: ("sv", 0), 1: ("sv", 0)}
        pt_free = {0: ("sg", 0), 1: ("sg", 0)}
        xr_free = ("sv", 0)
        ln1_free = {0: ("dl10", 0), 1: ("dl11", 0)}
        ln_last = ("sv", 0)
        slice_tok = [("sv", 0)]
        last_av = ("pe", 0)

        def emit_xload(s_i2, qt2):
            xsl = dr["x"][s_i2].rearrange("(n h t p) (c f) -> n h p c t f",
                                          h=2, t=2, p=128, f=128)[qt2]
            for hf in range(2):
                ldx = p.dma("gpsimd", t["xst"][:], xsl[hf], "dxs",
                            waits=slice_tok[-1:] + [p.total("dxt")])
                for c in range(8):
                    p.dma("sync",
                          t["xT"][:, c, hf * 256:(hf + 1) * 256]
                          .rearrange("p (j f) -> p j f", f=128),
                          t["xst"][:, c], "dxt", waits=[ldx], transpose=True)
            return p.total("dxt")

        xT_rdy = emit_xload(0, 0)
        for s_i in range(NSEQ):
            xs = dr["x"][s_i]
            vset = p.op("gpsimd", "memset", t["V"][:, :, :, HD:HD + 1], 1.0,
                        waits=[last_av])

            for qt in range(NQT):
                sl = slice(qt * TS, (qt + 1) * TS)

                def proj(wname, bias, nout, kchunks, src_cT, rdy, evac):
                    toks = []
                    for m in range(nout):
                        ppi = next_pp()
                        mm = None
                        for ci in range(kchunks):
                            w0 = ([rdy, pp_free[ppi], wl_all, s2_free2[0],
                                   s2_free2[1]] if ci == 0 else [])
                            mm = p.op("tensor", "matmul", pp[ppi],
                                      t[wname][:, ci, m * 128:(m + 1) * 128],
                                      t[src_cT][:, ci],
                                      start=(ci == 0), stop=(ci == kchunks - 1),
                                      waits=w0)
                        dtk = evac(m, ppi, mm)
                        pp_free[ppi] = dtk
                        toks.append(dtk)
                    return toks

                cq_toks = proj("wqb", "bqp", 2, 8, "xT", xT_rdy,
                               lambda m, ppi, mm: p.op(
                                   "scalar", "activation", t["cqT"][:, m], pp[ppi],
                                   AF.Gelu, bias=t["bqp"][:, m:m + 1],
                                   waits=[mm] + slice_tok[-1:]))
                ckv_toks = proj("wkvb", "bkvp", 2, 8, "xT", xT_rdy,
                                lambda m, ppi, mm: p.op(
                                    "scalar", "activation", t["ckvT"][:, m], pp[ppi],
                                    AF.Gelu, bias=t["bkvp"][:, m:m + 1],
                                    waits=[mm] + slice_tok[-1:]))
                qc_toks = proj("wqcb", "bqcp", 7, 2, "cqT", cq_toks[-1],
                               lambda m, ppi, mm: p.op(
                                   "scalar", "activation", t["qTs"][:, m], pp[ppi],
                                   AF.Gelu, bias=t["bqcp"][:, m:m + 1], waits=[mm]))
                kc_toks = proj("wkcb", "bkcp", 7, 2, "ckvT", ckv_toks[-1],
                               lambda m, ppi, mm: p.op(
                                   "scalar", "activation", t["kT"][:, m, sl], pp[ppi],
                                   AF.Gelu, bias=t["bkcp"][:, m:m + 1], waits=[mm]))
                tbl = []
                for nm in ["chh", "shh"]:
                    tbl.append(p.dma("gpsimd", t[nm][:], dr[nm][:, sl], "drt",
                                     waits=slice_tok[-1:]))
                tbl_all = p.total("drt")

                rp_toks = []
                for which, wname, bname, dstT, dsl in [
                        ("q", "wqrb", "bqrp", "qTs", None),
                        ("k", "wkrb", "bkrp", "kT", sl)]:
                    qr_tok = proj(wname, bname, 1, 8, "xT", xT_rdy,
                                  lambda m, ppi, mm: p.op(
                                      "vector", "tensor_scalar", t["qr"][:], pp[ppi],
                                      t[bname][:, 0:1], None, ALU.add,
                                      waits=[mm] + rp_toks[-1:] + slice_tok[-1:]))[0]
                    w1 = p.dma("gpsimd", t["qrw"][0:64, :], t["qr"][64:128, :], "dsw",
                               waits=[qr_tok] + rp_toks[-1:])
                    w2 = p.dma("gpsimd", t["qrw"][64:128, :], t["qr"][0:64, :], "dsw",
                               waits=[qr_tok])
                    sw_all = p.total("dsw")
                    a = p.op("vector", "tensor_tensor", RopA, t["qr"][:],
                             t["chh"][:], ALU.mult, waits=[sw_all, tbl_all])
                    b = p.op("vector", "tensor_tensor", RopB, t["qrw"][:],
                             t["shh"][:], ALU.mult, waits=[a])
                    g = p.op("vector", "tensor_tensor", RopA, RopA,
                             RopB, ALU.add, waits=[b])
                    dst = t[dstT][:, 7] if dsl is None else t[dstT][:, 7, dsl]
                    rp_toks.append(p.op("scalar", "activation", dst, RopA,
                                        AF.Gelu, waits=[g]))

                v_toks = []
                for tt_ in range(4):
                    nt = qt * 4 + tt_
                    for dvh in range(2):
                        ppi = next_pp()
                        mm = None
                        for ci in range(2):
                            mm = p.op("tensor", "matmul", pp[ppi],
                                      t["ckvT"][:, ci, tt_ * 128:(tt_ + 1) * 128],
                                      t["wvb"][:, ci, dvh * 512:(dvh + 1) * 512],
                                      start=(ci == 0), stop=False,
                                      waits=([ckv_toks[-1], pp_free[ppi], tk_ones,
                                              s2_free2[0], s2_free2[1]] if ci == 0 else []))
                        mm = p.op("tensor", "matmul", pp[ppi],
                                  t["onesF"][0:1, 0:128],
                                  t["bvrb"][0:1, dvh * 512:(dvh + 1) * 512],
                                  start=False, stop=True)
                        ev = p.op("vector", "tensor_copy",
                                  t["V"][:, nt, dvh * 8:(dvh + 1) * 8, 0:HD],
                                  pp[ppi].rearrange("p (h d) -> p h d", d=64),
                                  waits=[mm, vset])
                        pp_free[ppi] = ev
                        v_toks.append(ev)
                slice_rdy = [qc_toks[-1], kc_toks[-1], rp_toks[0], rp_toks[1],
                             v_toks[-1], p.total("sv")]
                slice_tok = slice_rdy
                do_tap("P%d_%d" % (s_i, qt))

                if qt + 1 < NQT:
                    xT_rdy = emit_xload(s_i, qt + 1)
                elif s_i + 1 < NSEQ:
                    xT_rdy = emit_xload(s_i + 1, 0)
                emit_prep(2)
                kmax = 4 * (qt + 1)
                attn_cons = []

                def emit_qk(ph, kt):
                    buf = kt % 2
                    S2b = S2[:, buf * 1024:(buf + 1) * 1024]
                    ksl = slice(kt * KT, (kt + 1) * KT)
                    w0 = [s2_free2[buf]] + (slice_rdy
                                            if (ph == 0 and kt < 2) else [])
                    p.op("tensor", "matmul", S2b[:, 0:QT],
                         t["kT"][0:64, ph, ksl], t["qTs"][0:64, ph],
                         start=True, stop=True, tile_position=(0, 0),
                         waits=w0)
                    return p.op("tensor", "matmul", S2b[:, QT:2 * QT],
                                t["kT"][64:128, ph, ksl], t["qTs"][64:128, ph],
                                start=True, stop=True, tile_position=(64, 0))

                pend_qk = None
                for ph in range(8):
                    av_tok = None
                    qk_tok = dict(pend_qk) if pend_qk is not None \
                        else {0: emit_qk(ph, 0)}
                    pend_qk = None
                    for kt in range(kmax):
                        buf = kt % 2
                        ptb = kt % 2
                        PT = [t["PT0"], t["PT1"]][ptb]
                        exp_tok = p.op("scalar", "activation", PT[:],
                                       S2[:, buf * 1024:(buf + 1) * 1024],
                                       AF.Exp, scale=0.125,
                                       waits=[qk_tok[kt], pt_free[ptb]])
                        s2_free2[buf] = exp_tok
                        if kt + 1 < kmax:
                            if (kt + 1) not in qk_tok:
                                qk_tok[kt + 1] = emit_qk(ph, kt + 1)
                            if kt + 2 == kmax and ph + 1 < 8:
                                pend_qk = {0: emit_qk(ph + 1, 0)}
                        elif ph + 1 < 8:
                            if pend_qk is None:
                                pend_qk = {0: emit_qk(ph + 1, 0)}
                            pend_qk[1] = emit_qk(ph + 1, 1)
                        mask_toks = []
                        r = kt - 4 * qt
                        if 0 <= r <= 3:
                            for hh in range(2):
                                mask_toks.append(p.op(
                                    "vector", "tensor_tensor",
                                    PT[:, hh * QT:(hh + 1) * QT],
                                    PT[:, hh * QT:(hh + 1) * QT],
                                    t["masks"][:, r, :], ALU.mult,
                                    waits=[exp_tok, wl_all]))
                        first, last = (kt == 0), (kt == kmax - 1)
                        wv_ = [exp_tok] + mask_toks
                        if first:
                            wv_ += [oa_free, ob_free]
                        p.op("tensor", "matmul", OA, t["V"][:, kt, 2 * ph, :],
                             PT[:, 0:QT],
                             start=first, stop=last, waits=wv_)
                        av_tok = p.op("tensor", "matmul", OB,
                                      t["V"][:, kt, 2 * ph + 1, :],
                                      PT[:, QT:2 * QT],
                                      start=first, stop=last)
                        pt_free[ptb] = av_tok
                        last_av = av_tok
                    rbsb = t["rbs"] if ph % 2 == 0 else t["rbs2"]
                    rAB = p.op("vector", "reciprocal", t["rcpA"][64:65, :],
                               t["PS_B"][64:65, 0:1024], waits=[av_tok], lp=True)
                    bA = p.op("tensor", "matmul", RB, t["onesF"][64:65, 0:64],
                              t["rcpA"][64:65, 0:QT], start=True, stop=True,
                              tile_position=(64, 0), waits=[rAB, rbc_free])
                    bB = p.op("tensor", "matmul", RB2, t["onesF"][64:65, 0:64],
                              t["rcpA"][64:65, QT:2 * QT], start=True, stop=True,
                              tile_position=(64, 0))
                    cpA = p.op("vector", "tensor_copy", rbsb[:, 0:QT], RB,
                               waits=[bA, rbs_free[ph % 2]], lp=True)
                    cpB = p.op("vector", "tensor_copy", rbsb[:, QT:2 * QT], RB2,
                               waits=[bB], lp=True)
                    rbc_free = cpB
                    cA = p.op("vector", "scalar_tensor_tensor",
                              t["attnT"][0:64, ph, :], OA[0:64, :], 1.0,
                              rbsb[:, 0:QT], ALU.mult, ALU.mult,
                              waits=[cpA])
                    oa_free = cA
                    wB = p.op("vector", "scalar_tensor_tensor",
                              t["attnT"][64:128, ph, :],
                              OB[0:64, :], 1.0, rbsb[:, QT:2 * QT],
                              ALU.mult, ALU.mult, waits=[cpB])
                    ob_free = wB
                    rbs_free[ph % 2] = wB
                    attn_cons = [cA, wB]
                    emit_prep(1)
                do_tap("A%d_%d" % (s_i, qt))

                for tt_ in range(4):
                    gt = qt * 4 + tt_
                    tsl = slice(gt * TT, (gt + 1) * TT)
                    lsl = slice(tt_ * 128, (tt_ + 1) * 128)
                    ldr = p.dma("gpsimd", t["xr"][:], xs[tsl, :], "dre",
                                waits=[xr_free])
                    ev_toks = []
                    for eh in range(2):
                        esl = slice(eh * 512, (eh + 1) * 512)
                        mm = None
                        for c in range(8):
                            w0 = ([wo_free[eh], rbc_free] + attn_cons
                                  if c == 0 else [])
                            mm = p.op("tensor", "matmul", wop[eh],
                                      t["attnT"][:, c, lsl], t["wob"][:, c, esl],
                                      start=(c == 0), stop=False, waits=w0)
                        mm = p.op("tensor", "matmul", wop[eh],
                                  t["onesF"][0:1, 0:128], t["borb"][0:1, esl],
                                  start=False, stop=True)
                        ev = p.op("vector", "tensor_tensor", t["rres"][:, esl],
                                  wop[eh], t["xr"][:, esl], ALU.add,
                                  waits=[mm, p.total("dre"), ln_last])
                        wo_free[eh] = ev
                        ev_toks.append(ev)
                    xr_free = ev_toks[-1]
                    rbc_free = ev_toks[0]
                    sm = p.op("vector", "tensor_reduce", t["s_sum"][:], t["rres"][:],
                              axis=AX.X, op=ALU.add, waits=ev_toks)
                    sqa = p.op("scalar", "activation", t["qrw"][:], t["rres"][:, 0:512],
                               AF.Square, accum_out=t["s_sq"][:], waits=ev_toks)
                    sq = p.op("scalar", "activation", t["qrw"][:], t["rres"][:, 512:1024],
                              AF.Square, accum_out=t["s_sq2"][:], waits=[sqa])
                    mn = p.op("vector", "tensor_scalar", t["s_mean"][:], t["s_sum"][:],
                              1.0 / E, None, ALU.mult, waits=[sm])
                    s12 = p.op("vector", "tensor_tensor", t["s_sq"][:], t["s_sq"][:],
                               t["s_sq2"][:], ALU.add, waits=[sq])
                    v1 = p.op("vector", "tensor_scalar", t["s_var"][:], t["s_sq"][:],
                              1.0 / E, None, ALU.mult, waits=[s12])
                    m2 = p.op("vector", "tensor_tensor", t["s_sum"][:], t["s_mean"][:],
                              t["s_mean"][:], ALU.mult, waits=[mn])
                    v2 = p.op("vector", "tensor_tensor", t["s_var"][:], t["s_var"][:],
                              t["s_sum"][:], ALU.subtract, waits=[v1, m2])
                    lg = p.op("scalar", "activation", t["s_rstd"][:], t["s_var"][:],
                              AF.Ln, bias=t["epst"][:, 0:1], waits=[v2, tk_eps])
                    ex = p.op("scalar", "activation", t["s_rstd"][:], t["s_rstd"][:],
                              AF.Exp, scale=-0.5, waits=[lg])
                    ln1t = t["ln1t0"]
                    no = p.op("vector", "tensor_scalar", ln1t[:], t["rres"][:],
                              t["s_mean"][:, 0:1], t["s_rstd"][:, 0:1],
                              ALU.subtract, ALU.mult,
                              waits=[ex, ln1_free[0]])
                    ln_last = no
                    st_ = p.dma("gpsimd", dr["ln1s"][s_i][tsl, :], ln1t[:],
                                "dl10", waits=[no])
                    ln1_free[0] = st_
                slice_tok = [wo_free[0], wo_free[1], ln_last]
                do_tap("W%d_%d" % (s_i, qt))

        emit_prep(len(prep_items))
        p.barrier(dma_drain=["dwp0", "dwp1", "dwpt", "dwps", "dl10", "dl11",
                             "dre", "dxs", "dxt", "drt", "dsw", "dswg"])

        for nm, src in [("wgT", "wgs"), ("wuT", "wus"), ("wdT", "wds")]:
            p.dma("gpsimd", t[nm][:], dr[src].rearrange("c p m -> p c m"), "dwft")
        wft_all = p.total("dwft")
        T_H1 = {tl: t["h1ab"].rearrange("p (c t f) -> p c t f", t=2, f=128)
                [:, :, tl, :] for tl in range(NT)}
        xq_scr = {"abs": "qabs", "pa": "qpa", "pb": "qpb", "sgn": "qsgn",
                  "bm": "qbm", "sc": "qsc", "rc": "qrc"}
        tk_eps2 = p.op("gpsimd", "memset", t["epst"][:], 1e-5)

        gu_free = {"g0": ("sv", 0), "u0": ("sv", 0), "g1": ("sv", 0), "u1": ("sv", 0)}
        self_hsg = {0: ("sv", 0), 1: ("sv", 0)}
        prev_q_last = [("sg", 0)]
        dn_free = {0: ("ss", 0), 1: ("ss", 0)}
        out_free = {0: ("dou0", 0), 1: ("dou1", 0)}
        no_prev = {0: ("sv", 0), 1: ("sv", 0)}
        hq2_free = ("dhq", 0)
        xqT_free = ("pe", 0)
        hqT_free = ("pe", 0)
        R2 = {0: t["r2"], 1: t["r2b"]}
        T_LN = {0: t["ln1L0"], 1: t["ln1L1"]}
        xq_state = {"free": {0: ("sv", 0), 1: ("sv", 0)},
                    "xq2_free": ("dxq", 0), "ld": {}, "q_toks": []}

        def emit_xq(s_i2, g2):
            for tl in range(NT):
                gt = g2 * NT + tl
                tsl = slice(gt * TT, (gt + 1) * TT)
                lsem = "dlf0" if tl == 0 else "dlf1"
                p.dma("gpsimd", T_LN[tl][:], dr["ln1s"][s_i2][tsl, :], lsem,
                      waits=[xq_state["free"][tl]])
                xq_state["ld"][tl] = (lsem, p.cnt[lsem])
            q_toks = []
            for tl in range(NT):
                q_toks.append(quant(T_LN[tl][:], E, t["xq2"][:, :, tl, :], xq_scr,
                                    [xq_state["ld"][tl]], "sg",
                                    out_waits=[xq_state["xq2_free"]]))
            xq_state["q_toks"] = q_toks

        def emit_xqT(xqT_free_):
            for c in range(8):
                p.dma("scalar", t["xqT"][:, c].rearrange("p (j f) -> p j f", f=128),
                      t["xq2"][:, c], "dxq",
                      waits=xq_state["q_toks"] + [xqT_free_], transpose=True)
            xq_state["xq2_free"] = p.total("dxq")
            return p.total("dxq")

        glist = [(si2, gg) for si2 in range(NSEQ) for gg in range(NTT // NT)]
        emit_xq(*glist[0])
        xqT_rdy = emit_xqT(xqT_free)
        for gidx, (s_i, g_) in enumerate(glist):
            if True:
                cv_tok = {}
                for tl in range(NT):
                    cv_tok[tl] = p.op("gpsimd", "tensor_copy", R2[tl][:],
                                      T_LN[tl][:],
                                      waits=[out_free[tl], xq_state["ld"][tl]])
                    xq_state["free"][tl] = cv_tok[tl]
                nsteps = len(ITS)
                h2_tok = {}
                q_tok = {}
                tr_rdy = {}
                dn_last = {0: None, 1: None}

                def emit_gu(it):
                    i0, iw = ITS[it]
                    res = {}
                    for tl in range(NT):
                        pr = (it * NT + tl) % 2
                        gps = gps2[pr]
                        ups = ups2[pr]
                        mmg = None
                        for c in range(8):
                            mmg = p.op("tensor", "matmul", gps[:, 0:iw],
                                       t["xqT"][:, c, tl * 128:(tl + 1) * 128],
                                       t["wgT"][:, c, i0:i0 + iw],
                                       start=(c == 0), stop=(c == 7),
                                       waits=([xqT_rdy, gu_free["g%d" % pr],
                                               wft_all, rbc_free] if c == 0 else []))
                        mmu = None
                        for c in range(8):
                            mmu = p.op("tensor", "matmul", ups[:, 0:iw],
                                       t["xqT"][:, c, tl * 128:(tl + 1) * 128],
                                       t["wuT"][:, c, i0:i0 + iw],
                                       start=(c == 0), stop=(c == 7),
                                       waits=([gu_free["u%d" % pr]] if c == 0 else []))
                        res[tl] = (mmg, mmu, pr)
                    return res

                def emit_h(it, gu, xfree):
                    i0, iw = ITS[it]
                    toks = {}
                    for tl in range(NT):
                        mmg, mmu, pr = gu[tl]
                        gps = gps2[pr]
                        ups = ups2[pr]
                        hsg_ = t["hsg"] if tl == 0 else t["hsgB"]
                        hv = T_H1[tl][:, 0:iw // 128, :]
                        tg = p.op("scalar", "activation", hsg_[:, 0:iw],
                                  gps[:, 0:iw], AF.Tanh, scale=0.5,
                                  waits=[mmg, self_hsg[tl]])
                        h1 = p.op("vector", "scalar_tensor_tensor",
                                  hv,
                                  hsg_[:, 0:iw].rearrange(
                                      "p (c f) -> p c f", f=128),
                                  1.0,
                                  ups[:, 0:iw].rearrange(
                                      "p (c f) -> p c f", f=128),
                                  ALU.add, ALU.mult,
                                  waits=[tg, mmu] + (xfree if tl == 0 else []))
                        gu_free["u%d" % pr] = h1
                        h2 = p.op("vector", "scalar_tensor_tensor",
                                  hv, hv, 0.5,
                                  gps[:, 0:iw].rearrange(
                                      "p (c f) -> p c f", f=128),
                                  ALU.mult, ALU.mult, waits=[h1])
                        gu_free["g%d" % pr] = h2
                        self_hsg[tl] = h1
                        toks[tl] = h2
                    return toks

                def emit_quant(it, mid=None):
                    i0, iw = ITS[it]
                    c0 = i0 // 128
                    out_ap = t["hq2"][:, c0:c0 + iw // 128, :, :] \
                        .rearrange("p c t f -> p (c t) f")
                    tok = quant(t["h1ab"][:, 0:2 * iw], 2 * iw, out_ap,
                                xq_scr,
                                [h2_tok[it][0], h2_tok[it][1]], "sg",
                                out_waits=[hq2_free], mid=mid)
                    return [tok]

                def emit_tr(it):
                    i0, iw = ITS[it]
                    c0 = i0 // 128
                    dsem = ["dhq", "dhq1", "dhq2", "dhq3", "dhq4", "dhq5"][it]
                    dq = "sync" if it % 2 == 0 else "scalar"
                    for c in range(c0, c0 + iw // 128):
                        p.dma(dq,
                              t["hqT"][:, c].rearrange("p (j f) -> p j f", f=128),
                              t["hq2"][:, c], dsem,
                              waits=q_tok[it] + [hqT_free], transpose=True)
                    tr_rdy[it] = p.total(dsem)

                gu = emit_gu(0)
                gu_last = gu
                h2_tok[0] = emit_h(0, gu, [prev_q_last[0]])
                for it in range(nsteps):
                    gu_next = emit_gu(it + 1) if it + 1 < nsteps else None
                    if gu_next is not None:
                        gu_last = gu_next

                    def mid_fn(xfree, it=it, gu_next=gu_next):
                        if gu_next is not None:
                            h2_tok[it + 1] = emit_h(it + 1, gu_next, xfree)
                    q_tok[it] = emit_quant(it, mid=mid_fn)
                    emit_tr(it)
                prev_q_last[0] = q_tok[nsteps - 1][0]
                if gidx + 1 < len(glist):
                    emit_xq(*glist[gidx + 1])
                for it in range(nsteps):
                    i0, iw = ITS[it]
                    c0 = i0 // 128
                    for tl in range(NT):
                        for eh in range(2):
                            esl = slice(eh * 512, (eh + 1) * 512)
                            mm = None
                            for c in range(c0, c0 + iw // 128):
                                w0 = []
                                if c == c0:
                                    w0 = [tr_rdy[it]]
                                    if it == 0 and eh == 0:
                                        w0 += [dn_free[tl], s2_free]
                                mm = p.op("tensor", "matmul",
                                          t["PS_A"][:, tl * 1024 + esl.start:
                                                     tl * 1024 + esl.stop],
                                          t["hqT"][:, c, tl * 128:(tl + 1) * 128],
                                          t["wdT"][:, c, esl],
                                          start=(it == 0 and c == c0),
                                          stop=(it == nsteps - 1 and
                                                c == c0 + iw // 128 - 1),
                                          waits=w0)
                            if it == nsteps - 1:
                                dn_last[tl] = mm
                xqT_free = gu_last[NT - 1][1]
                if gidx + 1 < len(glist):
                    xqT_rdy = emit_xqT(xqT_free)
                hq2_free = dn_last[NT - 1]
                dn_toks = dn_last
                s2_free = ("ss", 0)
                hqT_free = dn_last[NT - 1]
                sq_toks = {}
                for tl in range(NT):
                    gt = g_ * NT + tl
                    tsl = slice(gt * TT, (gt + 1) * TT)
                    dnv = t["PS_A"][:, tl * 1024:(tl + 1) * 1024]
                    r2v = R2[tl]
                    r2t = []
                    for eh in range(2):
                        esl = slice(eh * 512, (eh + 1) * 512)
                        r2t.append(p.op("vector", "tensor_tensor", r2v[:, esl],
                                        dnv[:, esl], r2v[:, esl], ALU.add,
                                        waits=[dn_toks[tl], cv_tok[tl]]))
                    sm = p.op("vector", "tensor_reduce", t["f_sum"][:], r2v[:],
                              axis=AX.X, op=ALU.add, waits=r2t)
                    sqa = p.op("scalar", "activation", t["hsg"][:], r2v[:, 0:512],
                               AF.Square, accum_out=t["f_sq"][:], waits=r2t)
                    sq = p.op("scalar", "activation", t["hsg"][:], r2v[:, 512:1024],
                              AF.Square, accum_out=t["f_sq2"][:], waits=[sqa])
                    dn_free[tl] = sq
                    mn = p.op("vector", "tensor_scalar", t["f_mean"][:], t["f_sum"][:],
                              1.0 / E, None, ALU.mult, waits=[sm])
                    s12 = p.op("vector", "tensor_tensor", t["f_sq"][:], t["f_sq"][:],
                               t["f_sq2"][:], ALU.add, waits=[sq])
                    v1 = p.op("vector", "tensor_scalar", t["f_var"][:], t["f_sq"][:],
                              1.0 / E, None, ALU.mult, waits=[s12])
                    m2 = p.op("vector", "tensor_tensor", t["f_sum"][:], t["f_mean"][:],
                              t["f_mean"][:], ALU.mult, waits=[mn])
                    v2 = p.op("vector", "tensor_tensor", t["f_var"][:], t["f_var"][:],
                              t["f_sum"][:], ALU.subtract, waits=[v1, m2])
                    lg = p.op("scalar", "activation", t["f_rstd"][:], t["f_var"][:],
                              AF.Ln, bias=t["epst"][:, 0:1], waits=[v2, tk_eps2])
                    ex = p.op("scalar", "activation", t["f_rstd"][:], t["f_rstd"][:],
                              AF.Exp, scale=-0.5, waits=[lg])
                    no = p.op("vector", "tensor_scalar", r2v[:], r2v[:],
                              t["f_mean"][:, 0:1], t["f_rstd"][:, 0:1],
                              ALU.subtract, ALU.mult, waits=[ex])
                    osem = "dou0" if tl == 0 else "dou1"
                    p.dma("gpsimd", dr["out"][s_i][tsl, :], r2v[:], osem,
                          waits=[no])
                    out_free[tl] = p.total(osem)
                s2_free = dn_free[NT - 1]
                do_tap("F%d_%d" % (s_i, g_))
        p.wait("sync", [p.total("dou0"), p.total("dou1")])

    with nc.Block() as block:
        @block.sync
        def _(h):
            p.start("sync", h)
            emit_program()

        @block.gpsimd
        def _(h):
            p.start("gpsimd", h)
            emit_program()

        @block.tensor
        def _(h):
            p.start("tensor", h)
            emit_program()

        @block.scalar
        def _(h):
            p.start("scalar", h)
            emit_program()

        @block.vector
        def _(h):
            p.start("vector", h)
            emit_program()

    return nc, dr


def host_inputs(inputs, seq_pair):
    g = lambda k: np.ascontiguousarray(np.asarray(inputs[k]))
    im = {}
    im["x"] = g("L0_attn_x")[2 * seq_pair:2 * seq_pair + 2]
    im["wq"] = g("L0_attn_wq"); im["wqc"] = g("L0_attn_wqc"); im["wqr"] = g("L0_attn_wqr")
    im["wkv"] = g("L0_attn_wkv"); im["wkr"] = g("L0_attn_wkr"); im["wkc"] = g("L0_attn_wkc")
    im["wv"] = g("L0_attn_wv"); im["wo"] = g("L0_attn_wo")
    im["bqp"] = g("L0_attn_bq").reshape(2, 128).T.copy()
    im["bqcp"] = g("L0_attn_bqc").reshape(7, 128).T.copy()
    im["bqrp"] = g("L0_attn_bqr").reshape(1, 128).T.copy()
    im["bkvp"] = g("L0_attn_bkv").reshape(2, 128).T.copy()
    im["bkrp"] = g("L0_attn_bkr").reshape(1, 128).T.copy()
    im["bkcp"] = g("L0_attn_bkc").reshape(7, 128).T.copy()
    im["bvr"] = g("L0_attn_bv").reshape(1, E)
    im["bor"] = g("L0_attn_bo").reshape(1, E)
    im["wg"] = g("L0_ffn_w_gate"); im["wu"] = g("L0_ffn_w_up"); im["wdn"] = g("L0_ffn_w_down")
    import jax
    import jax.numpy as jnp
    with jax.default_device(jax.devices("cpu")[0]):
        inv_freq = 1.0 / 10000.0 ** (jnp.arange(0, DR, 2, dtype=jnp.float32) / DR)
        tpos = jnp.arange(S, dtype=jnp.float32)
        freqs = tpos[:, None] * inv_freq[None, :]
        cos32 = np.asarray(jnp.cos(freqs)).T
        sin32 = np.asarray(jnp.sin(freqs)).T
    im["chh"] = np.concatenate([cos32, cos32], 0).astype(np.float32)
    im["shh"] = np.concatenate([-sin32, sin32], 0).astype(np.float32)
    masks = np.zeros((128, 4, QT), dtype=ml_dtypes.bfloat16)
    qf = np.arange(QT)
    for r in range(4):
        for kp in range(128):
            masks[kp, r, :] = (qf >= kp + 128 * r).astype(ml_dtypes.bfloat16)
    im["masks"] = masks
    return {"i_" + k: v for k, v in im.items()}


_PROG_CACHE = {}


def kernel(**inputs):
    if "prog" not in _PROG_CACHE:
        _PROG_CACHE["prog"] = build_program()
    nc, dr = _PROG_CACHE["prog"]
    in_maps = [host_inputs(inputs, c) for c in range(NCORE)]
    res = run_bass_kernel_spmd(nc, in_maps, core_ids=list(range(NCORE)))
    return np.concatenate([res.results[c]["out"] for c in range(NCORE)], axis=0)


# version 31
# speedup vs baseline: 1.0570x; 1.0197x over previous
import math
from contextlib import ExitStack

import numpy as np
import ml_dtypes

import concourse.bass as bass
import concourse.mybir as mybir
from concourse.bass_utils import run_bass_kernel_spmd

F32 = mybir.dt.float32
BF16 = mybir.dt.bfloat16
FP16 = mybir.dt.float16
AF = mybir.ActivationFunctionType
ALU = mybir.AluOpType
AX = mybir.AxisListType

S = 2048
E = 1024
DC = 256
DR = 128
H = 16
HD = 64
I = 2816
NSEQ = 2
NCORE = 8
TS = 512
NSL = S // TS
TT = 128
NTT = S // TT
QT = 512
NQT = NSL
KT = 128
NKT = S // KT
NT = 2
ITS = [(k * 512, min(512, I - k * 512)) for k in range(6)]
RCP6 = float(np.float32(1.0) / np.float32(6.0))
C6H = float(np.float32(1.0 / 6.0))
C6L = float(np.float64(1.0) / 6.0 - np.float64(np.float32(1.0 / 6.0)))
ENG = {"pe": "tensor", "sv": "vector", "ss": "scalar", "sg": "gpsimd", "sy": "sync"}


class Prog:
    def __init__(self, nc, sems):
        self.nc = nc
        self.sems = sems
        self.which = None
        self.h = None
        self.cnt = None

    def start(self, which, handle):
        self.which = which
        self.h = handle
        self.cnt = {k: 0 for k in self.sems}

    def op(self, engine, method, *args, waits=(), sem=None, lp=False, **kw):
        sem = sem or {"tensor": "pe", "vector": "sv", "scalar": "ss",
                      "gpsimd": "sg", "sync": "sy"}[engine]
        self.cnt[sem] += 1
        val = self.cnt[sem]
        if self.which == engine:
            waits = [(s, v) for (s, v) in waits if v > 0]
            if len(waits) > 1:
                for s, v in waits[:-1]:
                    self.h.wait_ge(self.sems[s], v)
            if lp:
                with self.nc.allow_low_precision(reason="bf16 softmax denominators (error budgeted)"):
                    inst = getattr(self.h, method)(*args, **kw)
            else:
                inst = getattr(self.h, method)(*args, **kw)
            if waits:
                inst._wait_ge(self.sems[waits[-1][0]], waits[-1][1])
            inst.then_inc(self.sems[sem], 1)
        return (sem, val)

    def dma(self, queue, out, in_, sem, waits=(), transpose=False):
        self.cnt[sem] += 16
        val = self.cnt[sem]
        if self.which == queue:
            waits = [(s, v) for (s, v) in waits if v > 0]
            if len(waits) > 1:
                for s, v in waits[:-1]:
                    self.h.wait_ge(self.sems[s], v)
            inst = self.h.dma_start(out=out, in_=in_, transpose=transpose)
            if waits:
                inst._wait_ge(self.sems[waits[-1][0]], waits[-1][1])
            inst.then_inc(self.sems[sem], 16)
        return (sem, val)

    def total(self, sem):
        return (sem, self.cnt[sem])

    def wait(self, engine, waits):
        if self.which == engine:
            for s, v in waits:
                if v > 0:
                    self.h.wait_ge(self.sems[s], v)

    def barrier(self, dma_drain=()):
        drains = [(s, self.cnt[s]) for s in dma_drain if self.cnt[s] > 0]
        vals = {}
        for e in ["pe", "sv", "ss", "sg", "sy"]:
            self.cnt[e] += 1
            vals[e] = self.cnt[e]
        for e, eng in ENG.items():
            if self.which == eng:
                for s, v in drains:
                    self.h.wait_ge(self.sems[s], v)
                self.h.nop().then_inc(self.sems[e], 1)
                for o, ov in vals.items():
                    if o != e:
                        self.h.wait_ge(self.sems[o], ov)


def build_program(tap=None):
    nc = bass.Bass("TRN2", target_bir_lowering=False)
    dr = {}

    def din(nm, shp, dt=F32):
        dr[nm] = nc.dram_tensor("i_" + nm, shp, dt, kind="ExternalInput").ap()

    din("x", [NSEQ, S, E])
    din("wq", [E, DC]); din("wqc", [DC, E - DR]); din("wqr", [E, DR])
    din("wkv", [E, DC]); din("wkr", [E, DR]); din("wkc", [DC, E - DR])
    din("wv", [DC, E]); din("wo", [E, E])
    din("bqp", [128, 2]); din("bqcp", [128, 7]); din("bqrp", [128, 1])
    din("bkvp", [128, 2]); din("bkrp", [128, 1]); din("bkcp", [128, 7])
    din("bvr", [1, E]); din("bor", [1, E])
    din("wg", [I, E]); din("wu", [I, E]); din("wdn", [E, I])
    din("chh", [128, S]); din("shh", [128, S])
    din("masks", [128, 4, QT], BF16)
    dr["out"] = nc.dram_tensor("out", [NSEQ, S, E], F32, kind="ExternalOutput").ap()
    dr["ln1s"] = nc.dram_tensor("ln1s", [NSEQ, S, E], BF16, kind="Internal").ap()
    dr["wgs"] = nc.dram_tensor("wgs", [8, 128, I], BF16, kind="Internal").ap()
    dr["wus"] = nc.dram_tensor("wus", [8, 128, I], BF16, kind="Internal").ap()
    dr["wds"] = nc.dram_tensor("wds", [22, 128, E], BF16, kind="Internal").ap()
    if tap is not None:
        dr["tapout"] = nc.dram_tensor("tapout", tap["shape"], tap.get("dtype", F32),
                                      kind="ExternalOutput").ap()

    T = {}
    root = ExitStack()

    def alloc(stk, name, shape, dtype, psum=False):
        fn = nc.psum_tensor if psum else nc.sbuf_tensor
        T[name] = stk.enter_context(fn(name, shape, dtype))
        return T[name]

    alloc(root, "onesF", [128, 128], BF16)
    alloc(root, "epst", [128, 1], F32)
    alloc(root, "PS_A", [128, 2048], F32, psum=True)
    alloc(root, "PS_B", [128, 1024], F32, psum=True)
    alloc(root, "PS_C", [128, 512], F32, psum=True)
    alloc(root, "PS_D", [128, 512], F32, psum=True)

    att = ExitStack()
    for nm, shp, dt in [
        ("masks", [128, 4, QT], BF16),
        ("wqb", [128, 8, DC], BF16), ("wqcb", [128, 2, E - DR], BF16),
        ("wqrb", [128, 8, DR], BF16), ("wkvb", [128, 8, DC], BF16),
        ("wkrb", [128, 8, DR], BF16), ("wkcb", [128, 2, E - DR], BF16),
        ("wvb", [128, 2, E], BF16), ("wob", [128, 8, E], BF16),
        ("bqp", [128, 2], F32), ("bqcp", [128, 7], F32), ("bqrp", [128, 1], F32),
        ("bkvp", [128, 2], F32), ("bkrp", [128, 1], F32), ("bkcp", [128, 7], F32),
        ("bvrb", [1, E], BF16), ("borb", [1, E], BF16),
        ("kT", [128, 8, S], BF16), ("V", [128, NKT, H, HD + 1], BF16),
        ("qTs", [128, 8, TS], BF16), ("attnT", [128, 8, TS], BF16),
        ("xst", [128, 8, 2, 128], BF16), ("xT", [128, 8, TS], BF16),
        ("cqT", [128, 2, TS], BF16), ("ckvT", [128, 2, TS], BF16),
        ("qr", [128, TS], F32), ("qrw", [128, TS], F32),
        ("chh", [128, TS], F32), ("shh", [128, TS], F32),
        ("PT0", [128, 2 * QT], BF16), ("PT1", [128, 2 * QT], BF16),
        ("rcpA", [128, 2 * QT], BF16),
        ("rbs", [64, 2 * QT], F32),
        ("xr", [128, E], F32), ("rres", [128, E], F32),
        ("s_sum", [128, 1], F32), ("s_sq", [128, 1], F32),
        ("s_sq2", [128, 1], F32),
        ("s_mean", [128, 1], F32), ("s_var", [128, 1], F32),
        ("s_rstd", [128, 1], F32),
        ("ln1t0", [128, E], BF16),
        ("wpin0", [128, E], F32), ("wpin1", [128, E], F32),
        ("wpabs", [128, E], F32), ("wppa", [128, E], F32), ("wppb", [128, E], F32),
        ("wpsgn", [128, E], F32),
        ("wpbm", [128, E // 16], F32), ("wpsc", [128, E // 16], F32),
        ("wprc", [128, E // 16], F32),
        ("wstg", [128, 8, 2, 128], BF16), ("wstgT", [128, 2, 2, 128], BF16),
        ("wstgD", [128, 8, 128], BF16),
    ]:
        alloc(att, nm, shp, dt)
    att.close()

    ffn = ExitStack()
    for nm, shp, dt in [
        ("wgT", [128, 8, I], BF16), ("wuT", [128, 8, I], BF16),
        ("wdT", [128, 22, E], BF16),
        ("ln1L0", [128, E], BF16), ("ln1L1", [128, E], BF16),
        ("qabs", [128, E], F32), ("qpa", [128, E], F32), ("qpb", [128, E], F32),
        ("qsgn", [128, E], F32),
        ("qbm", [128, E // 16], F32), ("qsc", [128, E // 16], F32),
        ("qrc", [128, E // 16], F32),
        ("xq2", [128, 8, NT, 128], BF16), ("xqT", [128, 8, NT * 128], BF16),
        ("hsg", [128, 512], F32), ("hsgB", [128, 512], F32),
        ("h1ab", [128, 1024], F32),
        ("hq2", [128, 22, NT, 128], BF16), ("hqT", [128, 22, NT * 128], BF16),
        ("r2", [128, E], F32), ("r2b", [128, E], F32),
        ("f_sum", [128, 1], F32), ("f_sq", [128, 1], F32),
        ("f_sq2", [128, 1], F32),
        ("f_mean", [128, 1], F32), ("f_var", [128, 1], F32),
        ("f_rstd", [128, 1], F32),
    ]:
        alloc(ffn, nm, shp, dt)
    ffn.close()

    sems = {}
    for nm in ["pe", "sv", "ss", "sg", "sy",
               "dxs", "dxt", "dwl", "drt", "dsw", "dre",
               "dl10", "dl11", "dwp0", "dwp1", "dwpt", "dwps",
               "dwft", "dwps1", "dlf0", "dlf1", "dxq", "dhq", "dhq1", "dhq2", "dhq3",
               "dhq4", "dhq5", "dou0", "dou1", "dswg"]:
        sems[nm] = root.enter_context(nc.semaphore(nm))

    p = Prog(nc, sems)
    t = T

    def emit_program():
        tap_done = [False]

        def do_tap(tag):
            if tap is not None and tap["tag"] == tag and not tap_done[0]:
                tap_done[0] = True
                tsrc = tap["tensor"]
                if "@" in tsrc:
                    base, kk = tsrc.split("@")
                    tsrc_ap = t[base][:, :, 0:int(kk)]
                elif "|" in tsrc:
                    base, kk = tsrc.split("|")
                    tsrc_ap = t[base][:, 0:int(kk)]
                else:
                    tsrc_ap = t[tsrc][:]
                p.dma("sync", dr["tapout"], tsrc_ap, "dou0",
                      waits=[p.total("sv"), p.total("ss"), p.total("sg"),
                             p.total("dswg"), p.total("dsw"), p.total("pe")])
                p.barrier(dma_drain=["dou0"])

        RopA = t["rres"][:, 0:TS]
        RopB = t["rres"][:, TS:2 * TS]
        S2 = t["PS_A"]
        pp = [t["PS_A"][:, i * 512:(i + 1) * 512] for i in range(4)]
        OA = t["PS_B"][0:65, 0:512]
        OB = t["PS_B"][0:65, 512:1024]
        RB = t["PS_C"][0:64, :]
        RB2 = t["PS_D"][0:64, :]
        wop = [t["PS_C"][:, :], t["PS_D"][:, :]]
        gps2 = [t["PS_C"][:, :], t["PS_B"][0:128, 0:512]]
        ups2 = [t["PS_D"][:, :], t["PS_B"][0:128, 512:1024]]
        dn = [t["PS_A"][:, 0:1024], t["PS_A"][:, 1024:2048]]

        for nm, src in [("wqb", "wq"), ("wqrb", "wqr"), ("wkvb", "wkv"),
                        ("wkrb", "wkr"), ("wob", "wo"),
                        ("wqcb", "wqc"), ("wkcb", "wkc"), ("wvb", "wv")]:
            p.dma("gpsimd", t[nm][:], dr[src].rearrange("(c p) m -> p c m", p=128), "dwl")
        for nm in ["bqp", "bqcp", "bqrp", "bkvp", "bkrp", "bkcp", "masks"]:
            p.dma("gpsimd", t[nm][:], dr[nm], "dwl")
        p.dma("gpsimd", t["bvrb"][:], dr["bvr"], "dwl")
        p.dma("gpsimd", t["borb"][:], dr["bor"], "dwl")
        tk_ones = p.op("gpsimd", "memset", t["onesF"][:], 1.0)
        tk_eps = p.op("gpsimd", "memset", t["epst"][:], 1e-5)
        wl_all = p.total("dwl")

        scr_free = {}

        def quant(x_ap, F, out_ap, pre, x_waits, out_sem, out_waits=(),
                  mid=None, pool_stair=False):
            nblk = F // 16
            ab = t[pre["abs"]][:, 0:F]
            pa = t[pre["pa"]][:, 0:F]
            pb = t[pre["pb"]][:, 0:F]
            sg_ = t[pre["sgn"]][:, 0:F]
            bm = t[pre["bm"]][:, 0:nblk]
            sc = t[pre["sc"]][:, 0:nblk]
            rc = t[pre["rc"]][:, 0:nblk]
            pv = scr_free.get(pre["abs"], {})
            x_waits = list(x_waits)
            a3 = ab.rearrange("p (nb b) -> p nb b", b=16)
            x3 = x_ap.rearrange("p (nb b) -> p nb b", b=16)
            tkb = p.op("vector", "tensor_reduce", bm, x3, axis=AX.X, op=ALU.max,
                       apply_absolute_value=True,
                       waits=x_waits + pv.get("bm", []))
            tlo = p.op("vector", "tensor_scalar", rc, bm, C6L, None, ALU.mult,
                       waits=[tkb] + pv.get("rc", []))
            thi = p.op("vector", "scalar_tensor_tensor", sc, bm, C6H, rc,
                       ALU.mult, ALU.add, waits=[tlo] + pv.get("sc", []))
            tkc = p.op("vector", "tensor_scalar", sc, sc, 1e-12, None, ALU.max,
                       waits=[thi])
            tkr = p.op("vector", "reciprocal", rc, sc, waits=[tkc])
            tka = p.op("scalar", "activation", ab, x_ap, AF.Abs,
                       waits=x_waits + pv.get("ab", []))
            tks = p.op("scalar", "activation", sg_, x_ap, AF.Sign,
                       waits=x_waits + pv.get("sgn", []))
            tkc2 = tkc
            scb = sc
            rb_ = rc.unsqueeze(2).broadcast_to([128, nblk, 16])
            tkn = p.op("gpsimd", "tensor_tensor",
                       pa.rearrange("p (nb b) -> p nb b", b=16), a3, rb_, ALU.mult,
                       waits=[tkr, tks, tka] + pv.get("pa", []))
            if mid is not None:
                mid([tka, tks, tkb])
            axn = pa
            if pool_stair:
                t1 = p.op("gpsimd", "tensor_scalar", pb, axn, 0.25, 0.5,
                          ALU.is_gt, ALU.mult, waits=[tkn] + pv.get("pb", []))
                steps = [(0.75, 0.5), (1.25, 0.5), (1.75, 0.5),
                         (2.5, None), (3.5, None), (5.0, 2.0)]
                for thr, w in steps:
                    if w is None:
                        c_ = p.op("gpsimd", "tensor_scalar", ab, axn, thr,
                                  None, ALU.is_gt, waits=[t1])
                    else:
                        c_ = p.op("gpsimd", "tensor_scalar", ab, axn, thr,
                                  w, ALU.is_gt, ALU.mult, waits=[t1])
                    t1 = p.op("gpsimd", "tensor_tensor", pb, pb, ab,
                              ALU.add, waits=[c_])
                tL = t1
            else:
                t1 = p.op("vector", "tensor_scalar", pb, axn, 0.25, None,
                          ALU.is_gt, waits=[tkn] + pv.get("pb", []))
                for thr in (0.75, 1.25, 1.75):
                    t1 = p.op("vector", "scalar_tensor_tensor", pb, axn, thr,
                              pb, ALU.is_gt, ALU.add, waits=[t1])
                h1_ = p.op("vector", "tensor_scalar", ab, axn, 5.0, None,
                           ALU.is_gt, waits=[tkn, tkb])
                for thr in (5.0, 3.5, 2.5):
                    h1_ = p.op("vector", "scalar_tensor_tensor", ab, axn, thr,
                               ab, ALU.is_gt, ALU.add, waits=[h1_])
                tL = p.op("vector", "scalar_tensor_tensor", pb, pb, 0.5, ab,
                          ALU.mult, ALU.add, waits=[t1, h1_])
            tLs = p.op("gpsimd", "tensor_tensor", pb, pb, sg_, ALU.mult,
                       waits=[tL, tks])
            if out_ap.ndim == 3:
                nch = out_ap.shape[1]
                o4 = out_ap.rearrange("p c (s b) -> p c s b", b=16)
                pb4 = pb.rearrange("p (c s b) -> p c s b", c=nch, b=16)
                sb_ = scb.rearrange("p (c s) -> p c s", s=8).unsqueeze(3) \
                    .broadcast_to([128, nch, 8, 16])
                tko = p.op("gpsimd", "tensor_tensor", o4, pb4, sb_, ALU.mult,
                           waits=[tLs, tkc2] + list(out_waits), sem=out_sem)
            else:
                sb_ = scb.unsqueeze(2).broadcast_to([128, nblk, 16])
                tko = p.op("gpsimd", "tensor_tensor",
                           out_ap.rearrange("p (nb b) -> p nb b", b=16),
                           pb.rearrange("p (nb b) -> p nb b", b=16), sb_, ALU.mult,
                           waits=[tLs, tkc2] + list(out_waits), sem=out_sem)
            scr_free[pre["abs"]] = {
                "ab": [tL, tkn], "sgn": [tLs], "bm": [thi],
                "rc": [thi, tkn], "sc": [tko], "pa": [tL], "pb": [tko]}
            return tko

        prep_items = []
        for i_ in range(22):
            prep_items.append(("g", i_))
        for i_ in range(22):
            prep_items.append(("u", i_))
        for i_ in range(8):
            prep_items.append(("d", i_))
        prep = {"i": 0, "par": 0, "cons": {0: ("sv", 0), 1: ("sv", 0)},
                "stg_free": [("dwps", 0)], "stgD_free": [("dwps", 0)],
                "tslot": {0: ("dwps", 0), 1: ("dwps1", 0)}}
        wscr = {"abs": "wpabs", "pa": "wppa", "pb": "wppb", "sgn": "wpsgn",
                "bm": "wpbm", "sc": "wpsc", "rc": "wprc"}

        def emit_prep(n):
            for _ in range(n):
                if prep["i"] >= len(prep_items):
                    return
                kind, it = prep_items[prep["i"]]
                prep["i"] += 1
                if kind in ("g", "u"):
                    src = (dr["wg"] if kind == "g" else dr["wu"])[
                        it * 128:(it + 1) * 128, :]
                    par = prep["par"]; prep["par"] ^= 1
                    wpin = t["wpin0"] if par == 0 else t["wpin1"]
                    psem = "dwp0" if par == 0 else "dwp1"
                    ld = p.dma("gpsimd", wpin[:], src, psem,
                               waits=[prep["cons"][par]])
                    j = it % 2
                    out_ap = t["wstg"][:, :, j, :]
                    tq = quant(wpin[:], E, out_ap, wscr,
                               [(psem, p.cnt[psem])], "sg",
                               out_waits=prep["stg_free"])
                    prep["cons"][par] = tq
                    if j == 1:
                        gw = it // 2
                        dsth = dr["wgs"] if kind == "g" else dr["wus"]
                        for c in range(8):
                            sl_ = c % 2
                            ssem = "dwps" if sl_ == 0 else "dwps1"
                            trk = p.dma("sync", t["wstgT"][:, sl_],
                                        t["wstg"][:, c],
                                        "dwpt", waits=[tq, prep["tslot"][sl_]],
                                        transpose=True)
                            p.dma("sync", dsth[c, :, gw * 256:(gw + 1) * 256],
                                  t["wstgT"][:, sl_]
                                  .rearrange("p j f -> p (j f)"),
                                  ssem, waits=[trk])
                            prep["tslot"][sl_] = p.total(ssem)
                        prep["stg_free"] = [p.total("dwps"),
                                            p.total("dwps1")]
                else:
                    for (c0, cw) in [(0, 1024), (1024, 1024), (2048, 768)]:
                        par = prep["par"]; prep["par"] ^= 1
                        wpin = t["wpin0"] if par == 0 else t["wpin1"]
                        psem = "dwp0" if par == 0 else "dwp1"
                        ld = p.dma("gpsimd", wpin[:, 0:cw],
                                   dr["wdn"][it * 128:(it + 1) * 128, c0:c0 + cw],
                                   psem, waits=[prep["cons"][par]])
                        out_ap = t["wstgD"][:, 0:cw // 128, :]
                        tq = quant(wpin[:, 0:cw], cw, out_ap, wscr,
                                   [(psem, p.cnt[psem])], "sg",
                                   out_waits=prep["stgD_free"])
                        prep["cons"][par] = tq
                        for c in range(cw // 128):
                            sl_ = c % 2
                            ssem = "dwps" if sl_ == 0 else "dwps1"
                            trk = p.dma("sync",
                                        t["wstgT"][:, sl_, 0, :].unsqueeze(1),
                                        t["wstgD"][:, c].unsqueeze(1), "dwpt",
                                        waits=[tq, prep["tslot"][sl_]],
                                        transpose=True)
                            p.dma("sync", dr["wds"][c0 // 128 + c, :,
                                                    it * 128:(it + 1) * 128],
                                  t["wstgT"][:, sl_, 0, :], ssem, waits=[trk])
                            prep["tslot"][sl_] = p.total(ssem)
                        prep["stgD_free"] = [p.total("dwps"),
                                             p.total("dwps1")]

        pp_free = {i: ("ss", 0) for i in range(4)}
        pp_i = [0]

        def next_pp():
            i_ = pp_i[0] % 4
            pp_i[0] += 1
            return i_

        s2_free = ("ss", 0)
        s2_free2 = {0: ("ss", 0), 1: ("ss", 0)}
        oa_free = ("sv", 0)
        ob_free = ("sv", 0)
        wo_free = {0: ("sv", 0), 1: ("sv", 0)}
        rbc_free = ("sv", 0)
        rbs_free = {0: ("sv", 0), 1: ("sv", 0)}
        pt_free = {0: ("sg", 0), 1: ("sg", 0)}
        xr_free = ("sv", 0)
        ln1_free = {0: ("dl10", 0), 1: ("dl11", 0)}
        ln_last = ("sv", 0)
        slice_tok = [("sv", 0)]
        last_av = ("pe", 0)

        def emit_xload(s_i2, qt2):
            xsl = dr["x"][s_i2].rearrange("(n h t p) (c f) -> n h p c t f",
                                          h=2, t=2, p=128, f=128)[qt2]
            for hf in range(2):
                ldx = p.dma("gpsimd", t["xst"][:], xsl[hf], "dxs",
                            waits=slice_tok[-1:] + [p.total("dxt")])
                for c in range(8):
                    p.dma("sync",
                          t["xT"][:, c, hf * 256:(hf + 1) * 256]
                          .rearrange("p (j f) -> p j f", f=128),
                          t["xst"][:, c], "dxt", waits=[ldx], transpose=True)
            return p.total("dxt")

        xT_rdy = emit_xload(0, 0)
        for s_i in range(NSEQ):
            xs = dr["x"][s_i]
            vset = p.op("gpsimd", "memset", t["V"][:, :, :, HD:HD + 1], 1.0,
                        waits=[last_av])

            for qt in range(NQT):
                sl = slice(qt * TS, (qt + 1) * TS)

                def proj(wname, bias, nout, kchunks, src_cT, rdy, evac):
                    toks = []
                    for m in range(nout):
                        ppi = next_pp()
                        mm = None
                        for ci in range(kchunks):
                            w0 = ([rdy, pp_free[ppi], wl_all, s2_free2[0],
                                   s2_free2[1]] if ci == 0 else [])
                            mm = p.op("tensor", "matmul", pp[ppi],
                                      t[wname][:, ci, m * 128:(m + 1) * 128],
                                      t[src_cT][:, ci],
                                      start=(ci == 0), stop=(ci == kchunks - 1),
                                      waits=w0)
                        dtk = evac(m, ppi, mm)
                        pp_free[ppi] = dtk
                        toks.append(dtk)
                    return toks

                cq_toks = proj("wqb", "bqp", 2, 8, "xT", xT_rdy,
                               lambda m, ppi, mm: p.op(
                                   "scalar", "activation", t["cqT"][:, m], pp[ppi],
                                   AF.Gelu, bias=t["bqp"][:, m:m + 1],
                                   waits=[mm] + slice_tok[-1:]))
                ckv_toks = proj("wkvb", "bkvp", 2, 8, "xT", xT_rdy,
                                lambda m, ppi, mm: p.op(
                                    "scalar", "activation", t["ckvT"][:, m], pp[ppi],
                                    AF.Gelu, bias=t["bkvp"][:, m:m + 1],
                                    waits=[mm] + slice_tok[-1:]))
                qc_toks = proj("wqcb", "bqcp", 7, 2, "cqT", cq_toks[-1],
                               lambda m, ppi, mm: p.op(
                                   "scalar", "activation", t["qTs"][:, m], pp[ppi],
                                   AF.Gelu, bias=t["bqcp"][:, m:m + 1], waits=[mm]))
                kc_toks = proj("wkcb", "bkcp", 7, 2, "ckvT", ckv_toks[-1],
                               lambda m, ppi, mm: p.op(
                                   "scalar", "activation", t["kT"][:, m, sl], pp[ppi],
                                   AF.Gelu, bias=t["bkcp"][:, m:m + 1], waits=[mm]))
                tbl = []
                for nm in ["chh", "shh"]:
                    tbl.append(p.dma("gpsimd", t[nm][:], dr[nm][:, sl], "drt",
                                     waits=slice_tok[-1:]))
                tbl_all = p.total("drt")

                rp_toks = []
                for which, wname, bname, dstT, dsl in [
                        ("q", "wqrb", "bqrp", "qTs", None),
                        ("k", "wkrb", "bkrp", "kT", sl)]:
                    qr_tok = proj(wname, bname, 1, 8, "xT", xT_rdy,
                                  lambda m, ppi, mm: p.op(
                                      "vector", "tensor_scalar", t["qr"][:], pp[ppi],
                                      t[bname][:, 0:1], None, ALU.add,
                                      waits=[mm] + rp_toks[-1:] + slice_tok[-1:]))[0]
                    w1 = p.dma("gpsimd", t["qrw"][0:64, :], t["qr"][64:128, :], "dsw",
                               waits=[qr_tok] + rp_toks[-1:])
                    w2 = p.dma("gpsimd", t["qrw"][64:128, :], t["qr"][0:64, :], "dsw",
                               waits=[qr_tok])
                    sw_all = p.total("dsw")
                    a = p.op("vector", "tensor_tensor", RopA, t["qr"][:],
                             t["chh"][:], ALU.mult, waits=[sw_all, tbl_all])
                    b = p.op("vector", "tensor_tensor", RopB, t["qrw"][:],
                             t["shh"][:], ALU.mult, waits=[a])
                    g = p.op("vector", "tensor_tensor", RopA, RopA,
                             RopB, ALU.add, waits=[b])
                    dst = t[dstT][:, 7] if dsl is None else t[dstT][:, 7, dsl]
                    rp_toks.append(p.op("scalar", "activation", dst, RopA,
                                        AF.Gelu, waits=[g]))

                v_toks = []
                for tt_ in range(4):
                    nt = qt * 4 + tt_
                    for dvh in range(2):
                        ppi = next_pp()
                        mm = None
                        for ci in range(2):
                            mm = p.op("tensor", "matmul", pp[ppi],
                                      t["ckvT"][:, ci, tt_ * 128:(tt_ + 1) * 128],
                                      t["wvb"][:, ci, dvh * 512:(dvh + 1) * 512],
                                      start=(ci == 0), stop=False,
                                      waits=([ckv_toks[-1], pp_free[ppi], tk_ones,
                                              s2_free2[0], s2_free2[1]] if ci == 0 else []))
                        mm = p.op("tensor", "matmul", pp[ppi],
                                  t["onesF"][0:1, 0:128],
                                  t["bvrb"][0:1, dvh * 512:(dvh + 1) * 512],
                                  start=False, stop=True)
                        ev = p.op("vector", "tensor_copy",
                                  t["V"][:, nt, dvh * 8:(dvh + 1) * 8, 0:HD],
                                  pp[ppi].rearrange("p (h d) -> p h d", d=64),
                                  waits=[mm, vset])
                        pp_free[ppi] = ev
                        v_toks.append(ev)
                slice_rdy = [qc_toks[-1], kc_toks[-1], rp_toks[0], rp_toks[1],
                             v_toks[-1], p.total("sv")]
                slice_tok = slice_rdy
                do_tap("P%d_%d" % (s_i, qt))

                if qt + 1 < NQT:
                    xT_rdy = emit_xload(s_i, qt + 1)
                elif s_i + 1 < NSEQ:
                    xT_rdy = emit_xload(s_i + 1, 0)
                emit_prep(2)
                kmax = 4 * (qt + 1)
                attn_cons = []

                def emit_qk(ph, kt):
                    buf = kt % 2
                    S2b = S2[:, buf * 1024:(buf + 1) * 1024]
                    ksl = slice(kt * KT, (kt + 1) * KT)
                    w0 = [s2_free2[buf]] + (slice_rdy
                                            if (ph == 0 and kt < 2) else [])
                    p.op("tensor", "matmul", S2b[:, 0:QT],
                         t["kT"][0:64, ph, ksl], t["qTs"][0:64, ph],
                         start=True, stop=True, tile_position=(0, 0),
                         waits=w0)
                    return p.op("tensor", "matmul", S2b[:, QT:2 * QT],
                                t["kT"][64:128, ph, ksl], t["qTs"][64:128, ph],
                                start=True, stop=True, tile_position=(64, 0))

                pend_qk = None
                for ph in range(8):
                    av_tok = None
                    qk_tok = dict(pend_qk) if pend_qk is not None \
                        else {0: emit_qk(ph, 0)}
                    pend_qk = None
                    for kt in range(kmax):
                        buf = kt % 2
                        ptb = kt % 2
                        PT = [t["PT0"], t["PT1"]][ptb]
                        exp_tok = p.op("scalar", "activation", PT[:],
                                       S2[:, buf * 1024:(buf + 1) * 1024],
                                       AF.Exp, scale=0.125,
                                       waits=[qk_tok[kt], pt_free[ptb]])
                        s2_free2[buf] = exp_tok
                        if kt + 1 < kmax:
                            if (kt + 1) not in qk_tok:
                                qk_tok[kt + 1] = emit_qk(ph, kt + 1)
                            if kt + 2 == kmax and ph + 1 < 8:
                                pend_qk = {0: emit_qk(ph + 1, 0)}
                        elif ph + 1 < 8:
                            if pend_qk is None:
                                pend_qk = {0: emit_qk(ph + 1, 0)}
                            pend_qk[1] = emit_qk(ph + 1, 1)
                        mask_toks = []
                        r = kt - 4 * qt
                        if 0 <= r <= 3:
                            for hh in range(2):
                                mask_toks.append(p.op(
                                    "vector", "tensor_tensor",
                                    PT[:, hh * QT:(hh + 1) * QT],
                                    PT[:, hh * QT:(hh + 1) * QT],
                                    t["masks"][:, r, :], ALU.mult,
                                    waits=[exp_tok, wl_all]))
                        first, last = (kt == 0), (kt == kmax - 1)
                        wv_ = [exp_tok] + mask_toks
                        if first:
                            wv_ += [oa_free, ob_free]
                        p.op("tensor", "matmul", OA, t["V"][:, kt, 2 * ph, :],
                             PT[:, 0:QT],
                             start=first, stop=last, waits=wv_)
                        av_tok = p.op("tensor", "matmul", OB,
                                      t["V"][:, kt, 2 * ph + 1, :],
                                      PT[:, QT:2 * QT],
                                      start=first, stop=last)
                        pt_free[ptb] = av_tok
                        last_av = av_tok
                    rbsb = t["rbs"]
                    rAB = p.op("vector", "reciprocal", t["rcpA"][64:65, :],
                               t["PS_B"][64:65, 0:1024], waits=[av_tok], lp=True)
                    bA = p.op("tensor", "matmul", RB, t["onesF"][64:65, 0:64],
                              t["rcpA"][64:65, 0:QT], start=True, stop=True,
                              tile_position=(64, 0), waits=[rAB, rbc_free])
                    bB = p.op("tensor", "matmul", RB2, t["onesF"][64:65, 0:64],
                              t["rcpA"][64:65, QT:2 * QT], start=True, stop=True,
                              tile_position=(64, 0))
                    cpA = p.op("vector", "tensor_copy", rbsb[:, 0:QT], RB,
                               waits=[bA, rbs_free[0]], lp=True)
                    cpB = p.op("vector", "tensor_copy", rbsb[:, QT:2 * QT], RB2,
                               waits=[bB], lp=True)
                    rbc_free = cpB
                    cA = p.op("vector", "scalar_tensor_tensor",
                              t["attnT"][0:64, ph, :], OA[0:64, :], 1.0,
                              rbsb[:, 0:QT], ALU.mult, ALU.mult,
                              waits=[cpA])
                    oa_free = cA
                    wB = p.op("vector", "scalar_tensor_tensor",
                              t["attnT"][64:128, ph, :],
                              OB[0:64, :], 1.0, rbsb[:, QT:2 * QT],
                              ALU.mult, ALU.mult, waits=[cpB])
                    ob_free = wB
                    rbs_free[0] = wB
                    attn_cons = [cA, wB]
                    emit_prep(1)
                do_tap("A%d_%d" % (s_i, qt))

                for tt_ in range(4):
                    gt = qt * 4 + tt_
                    tsl = slice(gt * TT, (gt + 1) * TT)
                    lsl = slice(tt_ * 128, (tt_ + 1) * 128)
                    ldr = p.dma("gpsimd", t["xr"][:], xs[tsl, :], "dre",
                                waits=[xr_free])
                    ev_toks = []
                    for eh in range(2):
                        esl = slice(eh * 512, (eh + 1) * 512)
                        mm = None
                        for c in range(8):
                            w0 = ([wo_free[eh], rbc_free] + attn_cons
                                  if c == 0 else [])
                            mm = p.op("tensor", "matmul", wop[eh],
                                      t["attnT"][:, c, lsl], t["wob"][:, c, esl],
                                      start=(c == 0), stop=False, waits=w0)
                        mm = p.op("tensor", "matmul", wop[eh],
                                  t["onesF"][0:1, 0:128], t["borb"][0:1, esl],
                                  start=False, stop=True)
                        ev = p.op("vector", "tensor_tensor", t["rres"][:, esl],
                                  wop[eh], t["xr"][:, esl], ALU.add,
                                  waits=[mm, p.total("dre"), ln_last])
                        wo_free[eh] = ev
                        ev_toks.append(ev)
                    xr_free = ev_toks[-1]
                    rbc_free = ev_toks[0]
                    sm = p.op("vector", "tensor_reduce", t["s_sum"][:], t["rres"][:],
                              axis=AX.X, op=ALU.add, waits=ev_toks)
                    sqa = p.op("scalar", "activation", t["qrw"][:], t["rres"][:, 0:512],
                               AF.Square, accum_out=t["s_sq"][:], waits=ev_toks)
                    sq = p.op("scalar", "activation", t["qrw"][:], t["rres"][:, 512:1024],
                              AF.Square, accum_out=t["s_sq2"][:], waits=[sqa])
                    mn = p.op("vector", "tensor_scalar", t["s_mean"][:], t["s_sum"][:],
                              1.0 / E, None, ALU.mult, waits=[sm])
                    s12 = p.op("vector", "tensor_tensor", t["s_sq"][:], t["s_sq"][:],
                               t["s_sq2"][:], ALU.add, waits=[sq])
                    v1 = p.op("vector", "tensor_scalar", t["s_var"][:], t["s_sq"][:],
                              1.0 / E, None, ALU.mult, waits=[s12])
                    m2 = p.op("vector", "tensor_tensor", t["s_sum"][:], t["s_mean"][:],
                              t["s_mean"][:], ALU.mult, waits=[mn])
                    v2 = p.op("vector", "tensor_tensor", t["s_var"][:], t["s_var"][:],
                              t["s_sum"][:], ALU.subtract, waits=[v1, m2])
                    lg = p.op("scalar", "activation", t["s_rstd"][:], t["s_var"][:],
                              AF.Ln, bias=t["epst"][:, 0:1], waits=[v2, tk_eps])
                    ex = p.op("scalar", "activation", t["s_rstd"][:], t["s_rstd"][:],
                              AF.Exp, scale=-0.5, waits=[lg])
                    ln1t = t["ln1t0"]
                    no = p.op("vector", "tensor_scalar", ln1t[:], t["rres"][:],
                              t["s_mean"][:, 0:1], t["s_rstd"][:, 0:1],
                              ALU.subtract, ALU.mult,
                              waits=[ex, ln1_free[0]])
                    ln_last = no
                    st_ = p.dma("gpsimd", dr["ln1s"][s_i][tsl, :], ln1t[:],
                                "dl10", waits=[no])
                    ln1_free[0] = st_
                slice_tok = [wo_free[0], wo_free[1], ln_last]
                do_tap("W%d_%d" % (s_i, qt))

        emit_prep(len(prep_items))
        p.barrier(dma_drain=["dwp0", "dwp1", "dwpt", "dwps", "dwps1", "dl10", "dl11",
                             "dre", "dxs", "dxt", "drt", "dsw", "dswg"])

        for nm, src in [("wgT", "wgs"), ("wuT", "wus"), ("wdT", "wds")]:
            p.dma("gpsimd", t[nm][:], dr[src].rearrange("c p m -> p c m"), "dwft")
        wft_all = p.total("dwft")
        T_H1 = {tl: t["h1ab"].rearrange("p (c t f) -> p c t f", t=2, f=128)
                [:, :, tl, :] for tl in range(NT)}
        xq_scr = {"abs": "qabs", "pa": "qpa", "pb": "qpb", "sgn": "qsgn",
                  "bm": "qbm", "sc": "qsc", "rc": "qrc"}
        tk_eps2 = p.op("gpsimd", "memset", t["epst"][:], 1e-5)

        gu_free = {"g0": ("sv", 0), "u0": ("sv", 0), "g1": ("sv", 0), "u1": ("sv", 0)}
        self_hsg = {0: ("sv", 0), 1: ("sv", 0)}
        prev_q_last = [("sg", 0)]
        dn_free = {0: ("ss", 0), 1: ("ss", 0)}
        out_free = {0: ("dou0", 0), 1: ("dou1", 0)}
        no_prev = {0: ("sv", 0), 1: ("sv", 0)}
        hq2_free = ("dhq", 0)
        xqT_free = ("pe", 0)
        hqT_free = ("pe", 0)
        R2 = {0: t["r2"], 1: t["r2b"]}
        T_LN = {0: t["ln1L0"], 1: t["ln1L1"]}
        xq_state = {"free": {0: ("sv", 0), 1: ("sv", 0)},
                    "xq2_free": ("dxq", 0), "ld": {}, "q_toks": []}

        def emit_xq(s_i2, g2):
            for tl in range(NT):
                gt = g2 * NT + tl
                tsl = slice(gt * TT, (gt + 1) * TT)
                lsem = "dlf0" if tl == 0 else "dlf1"
                p.dma("gpsimd", T_LN[tl][:], dr["ln1s"][s_i2][tsl, :], lsem,
                      waits=[xq_state["free"][tl]])
                xq_state["ld"][tl] = (lsem, p.cnt[lsem])
            q_toks = []
            for tl in range(NT):
                q_toks.append(quant(T_LN[tl][:], E, t["xq2"][:, :, tl, :], xq_scr,
                                    [xq_state["ld"][tl]], "sg",
                                    out_waits=[xq_state["xq2_free"]]))
            xq_state["q_toks"] = q_toks

        def emit_xqT(xqT_free_):
            for c in range(8):
                p.dma("scalar", t["xqT"][:, c].rearrange("p (j f) -> p j f", f=128),
                      t["xq2"][:, c], "dxq",
                      waits=xq_state["q_toks"] + [xqT_free_], transpose=True)
            xq_state["xq2_free"] = p.total("dxq")
            return p.total("dxq")

        glist = [(si2, gg) for si2 in range(NSEQ) for gg in range(NTT // NT)]
        emit_xq(*glist[0])
        xqT_rdy = emit_xqT(xqT_free)
        for gidx, (s_i, g_) in enumerate(glist):
            if True:
                cv_tok = {}
                for tl in range(NT):
                    cv_tok[tl] = p.op("gpsimd", "tensor_copy", R2[tl][:],
                                      T_LN[tl][:],
                                      waits=[out_free[tl], xq_state["ld"][tl]])
                    xq_state["free"][tl] = cv_tok[tl]
                nsteps = len(ITS)
                h2_tok = {}
                q_tok = {}
                tr_rdy = {}
                dn_last = {0: None, 1: None}

                def emit_gu(it):
                    i0, iw = ITS[it]
                    res = {}
                    for tl in range(NT):
                        pr = (it * NT + tl) % 2
                        gps = gps2[pr]
                        ups = ups2[pr]
                        mmg = None
                        for c in range(8):
                            mmg = p.op("tensor", "matmul", gps[:, 0:iw],
                                       t["xqT"][:, c, tl * 128:(tl + 1) * 128],
                                       t["wgT"][:, c, i0:i0 + iw],
                                       start=(c == 0), stop=(c == 7),
                                       waits=([xqT_rdy, gu_free["g%d" % pr],
                                               wft_all, rbc_free] if c == 0 else []))
                        mmu = None
                        for c in range(8):
                            mmu = p.op("tensor", "matmul", ups[:, 0:iw],
                                       t["xqT"][:, c, tl * 128:(tl + 1) * 128],
                                       t["wuT"][:, c, i0:i0 + iw],
                                       start=(c == 0), stop=(c == 7),
                                       waits=([gu_free["u%d" % pr]] if c == 0 else []))
                        res[tl] = (mmg, mmu, pr)
                    return res

                def emit_h(it, gu, xfree):
                    i0, iw = ITS[it]
                    toks = {}
                    for tl in range(NT):
                        mmg, mmu, pr = gu[tl]
                        gps = gps2[pr]
                        ups = ups2[pr]
                        hsg_ = t["hsg"] if tl == 0 else t["hsgB"]
                        hv = T_H1[tl][:, 0:iw // 128, :]
                        tg = p.op("scalar", "activation", hsg_[:, 0:iw],
                                  gps[:, 0:iw], AF.Tanh, scale=0.5,
                                  waits=[mmg, self_hsg[tl]])
                        h1 = p.op("vector", "scalar_tensor_tensor",
                                  hv,
                                  hsg_[:, 0:iw].rearrange(
                                      "p (c f) -> p c f", f=128),
                                  1.0,
                                  ups[:, 0:iw].rearrange(
                                      "p (c f) -> p c f", f=128),
                                  ALU.add, ALU.mult,
                                  waits=[tg, mmu] + (xfree if tl == 0 else []))
                        gu_free["u%d" % pr] = h1
                        h2 = p.op("vector", "scalar_tensor_tensor",
                                  hv, hv, 0.5,
                                  gps[:, 0:iw].rearrange(
                                      "p (c f) -> p c f", f=128),
                                  ALU.mult, ALU.mult, waits=[h1])
                        gu_free["g%d" % pr] = h2
                        self_hsg[tl] = h1
                        toks[tl] = h2
                    return toks

                def emit_quant(it, mid=None):
                    i0, iw = ITS[it]
                    c0 = i0 // 128
                    out_ap = t["hq2"][:, c0:c0 + iw // 128, :, :] \
                        .rearrange("p c t f -> p (c t) f")
                    tok = quant(t["h1ab"][:, 0:2 * iw], 2 * iw, out_ap,
                                xq_scr,
                                [h2_tok[it][0], h2_tok[it][1]], "sg",
                                out_waits=[hq2_free], mid=mid)
                    return [tok]

                def emit_tr(it):
                    i0, iw = ITS[it]
                    c0 = i0 // 128
                    dsem = ["dhq", "dhq1", "dhq2", "dhq3", "dhq4", "dhq5"][it]
                    dq = "sync" if it % 2 == 0 else "scalar"
                    for c in range(c0, c0 + iw // 128):
                        p.dma(dq,
                              t["hqT"][:, c].rearrange("p (j f) -> p j f", f=128),
                              t["hq2"][:, c], dsem,
                              waits=q_tok[it] + [hqT_free], transpose=True)
                    tr_rdy[it] = p.total(dsem)

                gu = emit_gu(0)
                gu_last = gu
                h2_tok[0] = emit_h(0, gu, [prev_q_last[0]])
                for it in range(nsteps):
                    gu_next = emit_gu(it + 1) if it + 1 < nsteps else None
                    if gu_next is not None:
                        gu_last = gu_next

                    def mid_fn(xfree, it=it, gu_next=gu_next):
                        if gu_next is not None:
                            h2_tok[it + 1] = emit_h(it + 1, gu_next, xfree)
                    q_tok[it] = emit_quant(it, mid=mid_fn)
                    emit_tr(it)
                prev_q_last[0] = q_tok[nsteps - 1][0]
                if gidx + 1 < len(glist):
                    emit_xq(*glist[gidx + 1])
                for it in range(nsteps):
                    i0, iw = ITS[it]
                    c0 = i0 // 128
                    for tl in range(NT):
                        for eh in range(2):
                            esl = slice(eh * 512, (eh + 1) * 512)
                            mm = None
                            for c in range(c0, c0 + iw // 128):
                                w0 = []
                                if c == c0:
                                    w0 = [tr_rdy[it]]
                                    if it == 0 and eh == 0:
                                        w0 += [dn_free[tl], s2_free]
                                mm = p.op("tensor", "matmul",
                                          t["PS_A"][:, tl * 1024 + esl.start:
                                                     tl * 1024 + esl.stop],
                                          t["hqT"][:, c, tl * 128:(tl + 1) * 128],
                                          t["wdT"][:, c, esl],
                                          start=(it == 0 and c == c0),
                                          stop=(it == nsteps - 1 and
                                                c == c0 + iw // 128 - 1),
                                          waits=w0)
                            if it == nsteps - 1:
                                dn_last[tl] = mm
                xqT_free = gu_last[NT - 1][1]
                if gidx + 1 < len(glist):
                    xqT_rdy = emit_xqT(xqT_free)
                hq2_free = dn_last[NT - 1]
                dn_toks = dn_last
                s2_free = ("ss", 0)
                hqT_free = dn_last[NT - 1]
                sq_toks = {}
                for tl in range(NT):
                    gt = g_ * NT + tl
                    tsl = slice(gt * TT, (gt + 1) * TT)
                    dnv = t["PS_A"][:, tl * 1024:(tl + 1) * 1024]
                    r2v = R2[tl]
                    r2t = []
                    for eh in range(2):
                        esl = slice(eh * 512, (eh + 1) * 512)
                        r2t.append(p.op("vector", "tensor_tensor", r2v[:, esl],
                                        dnv[:, esl], r2v[:, esl], ALU.add,
                                        waits=[dn_toks[tl], cv_tok[tl]]))
                    sm = p.op("vector", "tensor_reduce", t["f_sum"][:], r2v[:],
                              axis=AX.X, op=ALU.add, waits=r2t)
                    sqa = p.op("scalar", "activation", t["hsg"][:], r2v[:, 0:512],
                               AF.Square, accum_out=t["f_sq"][:], waits=r2t)
                    sq = p.op("scalar", "activation", t["hsg"][:], r2v[:, 512:1024],
                              AF.Square, accum_out=t["f_sq2"][:], waits=[sqa])
                    dn_free[tl] = sq
                    mn = p.op("vector", "tensor_scalar", t["f_mean"][:], t["f_sum"][:],
                              1.0 / E, None, ALU.mult, waits=[sm])
                    s12 = p.op("vector", "tensor_tensor", t["f_sq"][:], t["f_sq"][:],
                               t["f_sq2"][:], ALU.add, waits=[sq])
                    v1 = p.op("vector", "tensor_scalar", t["f_var"][:], t["f_sq"][:],
                              1.0 / E, None, ALU.mult, waits=[s12])
                    m2 = p.op("vector", "tensor_tensor", t["f_sum"][:], t["f_mean"][:],
                              t["f_mean"][:], ALU.mult, waits=[mn])
                    v2 = p.op("vector", "tensor_tensor", t["f_var"][:], t["f_var"][:],
                              t["f_sum"][:], ALU.subtract, waits=[v1, m2])
                    lg = p.op("scalar", "activation", t["f_rstd"][:], t["f_var"][:],
                              AF.Ln, bias=t["epst"][:, 0:1], waits=[v2, tk_eps2])
                    ex = p.op("scalar", "activation", t["f_rstd"][:], t["f_rstd"][:],
                              AF.Exp, scale=-0.5, waits=[lg])
                    no = p.op("vector", "tensor_scalar", r2v[:], r2v[:],
                              t["f_mean"][:, 0:1], t["f_rstd"][:, 0:1],
                              ALU.subtract, ALU.mult, waits=[ex])
                    osem = "dou0" if tl == 0 else "dou1"
                    p.dma("gpsimd", dr["out"][s_i][tsl, :], r2v[:], osem,
                          waits=[no])
                    out_free[tl] = p.total(osem)
                s2_free = dn_free[NT - 1]
                do_tap("F%d_%d" % (s_i, g_))
        p.wait("sync", [p.total("dou0"), p.total("dou1")])

    with nc.Block() as block:
        @block.sync
        def _(h):
            p.start("sync", h)
            emit_program()

        @block.gpsimd
        def _(h):
            p.start("gpsimd", h)
            emit_program()

        @block.tensor
        def _(h):
            p.start("tensor", h)
            emit_program()

        @block.scalar
        def _(h):
            p.start("scalar", h)
            emit_program()

        @block.vector
        def _(h):
            p.start("vector", h)
            emit_program()

    return nc, dr


def host_inputs(inputs, seq_pair):
    g = lambda k: np.ascontiguousarray(np.asarray(inputs[k]))
    im = {}
    im["x"] = g("L0_attn_x")[2 * seq_pair:2 * seq_pair + 2]
    im["wq"] = g("L0_attn_wq"); im["wqc"] = g("L0_attn_wqc"); im["wqr"] = g("L0_attn_wqr")
    im["wkv"] = g("L0_attn_wkv"); im["wkr"] = g("L0_attn_wkr"); im["wkc"] = g("L0_attn_wkc")
    im["wv"] = g("L0_attn_wv"); im["wo"] = g("L0_attn_wo")
    im["bqp"] = g("L0_attn_bq").reshape(2, 128).T.copy()
    im["bqcp"] = g("L0_attn_bqc").reshape(7, 128).T.copy()
    im["bqrp"] = g("L0_attn_bqr").reshape(1, 128).T.copy()
    im["bkvp"] = g("L0_attn_bkv").reshape(2, 128).T.copy()
    im["bkrp"] = g("L0_attn_bkr").reshape(1, 128).T.copy()
    im["bkcp"] = g("L0_attn_bkc").reshape(7, 128).T.copy()
    im["bvr"] = g("L0_attn_bv").reshape(1, E)
    im["bor"] = g("L0_attn_bo").reshape(1, E)
    im["wg"] = g("L0_ffn_w_gate"); im["wu"] = g("L0_ffn_w_up"); im["wdn"] = g("L0_ffn_w_down")
    import jax
    import jax.numpy as jnp
    with jax.default_device(jax.devices("cpu")[0]):
        inv_freq = 1.0 / 10000.0 ** (jnp.arange(0, DR, 2, dtype=jnp.float32) / DR)
        tpos = jnp.arange(S, dtype=jnp.float32)
        freqs = tpos[:, None] * inv_freq[None, :]
        cos32 = np.asarray(jnp.cos(freqs)).T
        sin32 = np.asarray(jnp.sin(freqs)).T
    im["chh"] = np.concatenate([cos32, cos32], 0).astype(np.float32)
    im["shh"] = np.concatenate([-sin32, sin32], 0).astype(np.float32)
    masks = np.zeros((128, 4, QT), dtype=ml_dtypes.bfloat16)
    qf = np.arange(QT)
    for r in range(4):
        for kp in range(128):
            masks[kp, r, :] = (qf >= kp + 128 * r).astype(ml_dtypes.bfloat16)
    im["masks"] = masks
    return {"i_" + k: v for k, v in im.items()}


_PROG_CACHE = {}


def kernel(**inputs):
    if "prog" not in _PROG_CACHE:
        _PROG_CACHE["prog"] = build_program()
    nc, dr = _PROG_CACHE["prog"]
    in_maps = [host_inputs(inputs, c) for c in range(NCORE)]
    res = run_bass_kernel_spmd(nc, in_maps, core_ids=list(range(NCORE)))
    return np.concatenate([res.results[c]["out"] for c in range(NCORE)], axis=0)
